# version 09
# baseline (speedup 1.0000x reference)
import contextlib
import numpy as np
import ml_dtypes
import concourse.bass as bass
import concourse.mybir as mybir
from concourse.bass_utils import run_bass_kernel_spmd

F32 = mybir.dt.float32
F32R = mybir.dt.float32r
BF16 = mybir.dt.bfloat16
AF = mybir.ActivationFunctionType
ALU = mybir.AluOpType
AX = mybir.AxisListType

NCORES = 8
T = 4096
D = 1024
bf16 = ml_dtypes.bfloat16
NEGBF = float(ml_dtypes.finfo(bf16).min)



def _l0_rot_tables():
    dh, ws, jlen = 64, 128, 384
    f32 = np.float32
    inv_freq = 1.0 / (f32(10000.0) ** (np.arange(0, dh, 2, dtype=f32) / f32(dh)))
    t = np.arange(jlen, dtype=f32)
    half = t[:, None] * inv_freq[None, :]
    freqs = np.concatenate([half, half], -1)
    power = (t - jlen // 2) / f32(ws // 2)
    sbase = ((np.arange(0, dh, 2, dtype=f32) + 0.4 * dh) / (1.4 * dh)).astype(f32)
    scale = sbase[None, :] ** power[:, None]
    scale = np.concatenate([scale, scale], -1)
    qf, qs = freqs[-ws:], scale[-ws:]
    inv_s = (1.0 / scale).astype(f32)
    sgn = np.where(np.arange(dh) < dh // 2, f32(-1.0), f32(1.0))[:, None]
    qsc = f32(dh ** -0.5)
    QC = (np.cos(qf) * qs * qsc).T
    QS = (np.sin(qf) * qs * qsc).T * sgn
    KC = (np.cos(freqs) * inv_s).T
    KS = (np.sin(freqs) * inv_s).T * sgn
    QCp = np.concatenate([QC, QC], 0)
    QSp = np.concatenate([QS, QS], 0)
    KCp = np.concatenate([KC, KC], 0)
    KSp = np.concatenate([KS, KS], 0)
    rotq = np.ascontiguousarray(np.concatenate([QCp, QSp], 1)).astype(bf16)
    rotk = np.ascontiguousarray(np.concatenate([KCp, KSp], 1)).astype(bf16)
    return rotq, rotk


def _l0_masks():
    ws = 128
    out = np.zeros((128, 3 * 384), dtype=np.float32)
    for mi, w in enumerate([0, 1, (T // ws) - 1]):
        i = np.arange(128)[:, None]
        j = np.arange(384)[None, :]
        bq_k = (w - 1) * ws + j
        valid = (bq_k >= 0) & (bq_k < T)
        b_t = w * ws + i
        masked = (~valid) | (bq_k - ws > b_t) | (b_t > bq_k + ws)
        out[:, mi * 384:(mi + 1) * 384] = np.where(masked, NEGBF, 0.0)
    return out.astype(bf16)


def _perm128():
    P = np.zeros((128, 128), dtype=bf16)
    for h in range(2):
        for e in range(64):
            P[h * 64 + (e + 32) % 64, h * 64 + e] = 1.0
    return P


def _pack_cols(v, p=128):
    v = np.asarray(v, dtype=np.float32)
    return np.ascontiguousarray(v.reshape(-1, p).T)


def host_consts():
    rotq, rotk = _l0_rot_tables()
    return {
        "c_rotq": rotq,
        "c_rotk": rotk,
        "c_masks": _l0_masks(),
        "c_perm": _perm128(),
        "c_eye": np.eye(128, dtype=bf16),
        "c_ones": np.ones((1, 512), dtype=np.float32),
    }



ENGINES = ("sync", "gpsimd", "vector", "scalar", "tensor")


class Prog:
    def __init__(self, nc, ctx):
        self.nc = nc
        self.q = {e: [] for e in ENGINES}
        self.cnt = {e: 0 for e in ENGINES}
        self.sem = {e: ctx.enter_context(nc.semaphore(f"sem_{e}"))
                    for e in ENGINES}
        self.dsem = {e: ctx.enter_context(nc.semaphore(f"dma_{e}"))
                     for e in ("sync", "gpsimd", "vector", "scalar")}
        self.dcnt = {e: 0 for e in self.dsem}

    def ins(self, eng, fn):
        sem = self.sem[eng]
        self.cnt[eng] += 1
        tgt = self.cnt[eng]

        def run(eobj, fn=fn, sem=sem):
            fn(eobj).then_inc(sem, 1)
        self.q[eng].append(run)
        return (eng, False, tgt)

    def dma(self, eng, out, in_, **kw):
        dsem = self.dsem[eng]
        self.dcnt[eng] += 16
        tgt = self.dcnt[eng]

        def run(eobj, out=out, in_=in_, dsem=dsem, kw=kw):
            eobj.dma_start(out=out, in_=in_, **kw).then_inc(dsem, 16)
        self.q[eng].append(run)
        return (eng, True, tgt)

    def wait(self, eng, *tokens):
        for (peng, isdma, val) in tokens:
            if val <= 0:
                continue
            sem = self.dsem[peng] if isdma else self.sem[peng]
            self.q[eng].append(
                lambda eobj, sem=sem, val=val: eobj.wait_ge(sem, val))

    def make_fences(self, nc):
        if hasattr(self, "_fb"):
            return
        self._fb = {
            "vector": sbuf(nc, "fence_v", [1, 8]),
            "gpsimd": sbuf(nc, "fence_g", [1, 8]),
            "scalar": sbuf(nc, "fence_s", [1, 8]),
        }

    def fence(self, eng):
        fb = self._fb[eng]
        if eng == "scalar":
            return self.ins(eng, lambda e, fb=fb: e.copy(fb[0:1, 0:4],
                                                         fb[0:1, 4:8]))
        return self.ins(eng, lambda e, fb=fb: e.tensor_copy(fb[0:1, 0:4],
                                                            fb[0:1, 4:8]))

    def mark(self, eng):
        return (eng, False, self.cnt[eng])

    def mark_dma(self, eng):
        return (eng, True, self.dcnt[eng])

    def barrier(self):
        toks = ([self.mark(e) for e in ENGINES]
                + [self.mark_dma(e) for e in self.dsem])
        for e in ENGINES:
            self.wait(e, *[t for t in toks if t[1] or t[0] != e])

    def finalize(self, final_store_tokens):
        self.wait("sync", *final_store_tokens)
        with self.nc.Block() as block:
            for eng in ENGINES:
                qq = self.q[eng]
                getattr(block, eng)(lambda eobj, qq=qq: [f(eobj) for f in qq])


_CUR_CTX = [None]


def sbuf(nc, name, shape, dtype=F32):
    if _CUR_CTX[0] is not None:
        return _CUR_CTX[0].enter_context(nc.sbuf_tensor(name, shape, dtype))[:]
    return nc.alloc_sbuf_tensor(name, shape, dtype).ap()


def psum(nc, name, shape, dtype=F32):
    if _CUR_CTX[0] is not None:
        return _CUR_CTX[0].enter_context(nc.psum_tensor(name, shape, dtype))[:]
    return nc.alloc_psum_tensor(name, shape, dtype).ap()



class LnKit:

    def __init__(self, nc, tag, W):
        self.W = W
        self.ra = sbuf(nc, f"{tag}_ra", [2, W])
        self.rb = sbuf(nc, f"{tag}_rb", [2, W])
        self.sq = sbuf(nc, f"{tag}_sq", [128, W], BF16)
        self.xstat = sbuf(nc, f"{tag}_xst", [128, W], BF16)
        self.onesr = sbuf(nc, f"{tag}_ones", [128, 2], BF16)
        self.epsc = sbuf(nc, f"{tag}_eps", [128, 1])
        self.init_tok = None


def emit_ln(P, kit, rr_chunks, psS, psS2, psB1, psB2, ones_row_r,
            out_writer, extra_pe_dep=()):
    nch = len(rr_chunks)
    last_mm = None
    P.wait("tensor", *extra_pe_dep)
    SSC = 1.0 / 128.0
    P.wait("scalar", P.mark("vector"))
    for c in range(nch):
        if last_mm is not None:
            P.wait("scalar", last_mm)
        P.ins("scalar", lambda s, c=c: s.activation(
            kit.xstat, rr_chunks[c], AF.Identity, bias=0.0, scale=SSC))
        tcp = P.fence("scalar")
        P.wait("vector", tcp)
        P.ins("vector", lambda v: v.tensor_tensor(kit.sq, kit.xstat,
                                                  kit.xstat, ALU.mult))
        tsq = P.fence("vector")
        P.wait("tensor", tcp, tsq)
        P.ins("tensor", lambda t, c=c: t.matmul(
            psS, kit.onesr, kit.xstat, start=(c == 0), stop=(c == nch - 1)))
        P.ins("tensor", lambda t, c=c: t.matmul(
            psS2, kit.onesr, kit.sq, start=(c == 0), stop=(c == nch - 1)))
        last_mm = P.mark("tensor")
    tok_mm = P.mark("tensor")
    inv_n = 1.0 / (128 * nch)
    inv_n_m = inv_n / SSC
    inv_n_v = inv_n / (SSC * SSC)
    ra0, rb0 = kit.ra[0:1, :], kit.rb[0:1, :]
    P.wait("vector", tok_mm)
    P.ins("vector", lambda v: v.tensor_scalar_mul(ra0, psS[0:1, :], inv_n_m))
    P.ins("vector", lambda v: v.tensor_scalar_mul(rb0, psS2[0:1, :], inv_n_v))
    P.ins("vector", lambda v: v.tensor_tensor(psS[0:1, :], ra0, ra0, ALU.mult))
    tv = P.ins("vector", lambda v: v.tensor_tensor(
        rb0, rb0, psS[0:1, :], ALU.subtract))
    P.wait("scalar", tv)
    tsd = P.ins("scalar", lambda s: s.activation(
        rb0, rb0, AF.Sqrt, bias=kit.epsc[0:1, 0:1], scale=1.0))
    P.wait("vector", tsd)
    P.ins("vector", lambda v: v.reciprocal(rb0, rb0))
    P.ins("vector", lambda v: v.tensor_tensor(ra0, ra0, rb0, ALU.mult))
    P.ins("vector", lambda v: v.tensor_scalar_mul(ra0, ra0, -1.0))
    tn = P.fence("vector")
    P.wait("tensor", tn)
    P.ins("tensor", lambda t: t.matmul(
        psB1, ones_row_r, kit.rb[:], start=True, stop=True))
    tb = P.ins("tensor", lambda t: t.matmul(
        psB2, ones_row_r, kit.ra[:], start=True, stop=True))
    return tb



DEBUG_P1 = False


def build_p1_program():
    nc = bass.Bass("TRN2", target_bir_lowering=False, debug=False)
    io = {}
    if DEBUG_P1:
        for nm, shp, dt in [("dbg_qk", [512, T], BF16), ("dbg_v", [T, 512], BF16),
                            ("dbg_ao", [512, T], BF16), ("dbg_xh", [D, 512], F32),
                            ("dbg_bw", [128, 8], F32), ("dbg_pb", [128, 512], F32),
                            ("dbg_rstd", [2, 512], F32), ("dbg_S", [2, 512], F32)]:
            io[nm] = nc.dram_tensor(nm, shp, dt, kind="ExternalOutput").ap()
    io["xT"] = nc.dram_tensor("xT", [D, T], F32, kind="ExternalInput").ap()
    io["wqkvT"] = nc.dram_tensor("wqkvT", [D, D], F32, kind="ExternalInput").ap()
    io["woutT"] = nc.dram_tensor("woutT", [512, D], F32, kind="ExternalInput").ap()
    io["lngb"] = nc.dram_tensor("lngb", [128, 16], F32, kind="ExternalInput").ap()
    for k, v in host_consts().items():
        dt = BF16 if v.dtype == bf16 else F32
        io[k] = nc.dram_tensor(k, list(v.shape), dt, kind="ExternalInput").ap()
    io["x1T"] = nc.dram_tensor("x1T", [D, T], F32, kind="ExternalOutput").ap()

    with contextlib.ExitStack() as ctx:
        P = Prog(nc, ctx)
        emit_p1(nc, P, io)
        P.finalize([P.mark_dma("sync")])
    return nc


def emit_p1(nc, P, io, tag="p1"):
    W = 512
    NT = T // W
    NW = T // 128

    wq = [sbuf(nc, f"{tag}_wq{k}", [128, 1024], F32R) for k in range(8)]
    arena = sbuf(nc, f"{tag}_arena", [128, 12032])
    _ab = arena.bitcast(BF16)
    wo = [_ab[:, 16384 + k * 1024:16384 + (k + 1) * 1024] for k in range(4)]
    wraw = arena[:, 4096:5120]
    lngb = sbuf(nc, f"{tag}_lngb", [128, 16])
    lnbr = sbuf(nc, f"{tag}_lnbr", [128, 18], F32R)
    rotq = sbuf(nc, f"{tag}_rotq", [128, 256], BF16)
    rotk = sbuf(nc, f"{tag}_rotk", [128, 768], BF16)
    masks = sbuf(nc, f"{tag}_masks", [128, 1152], BF16)
    perm = sbuf(nc, f"{tag}_perm", [128, 128], BF16)
    eye = sbuf(nc, f"{tag}_eye", [128, 128], BF16)
    ones1r = sbuf(nc, f"{tag}_ones1r", [2, 128])
    bw_cols = sbuf(nc, f"{tag}_bwcols", [128, 8])

    xtile = [arena[:, c * W:(c + 1) * W] for c in range(8)]
    xh8 = sbuf(nc, f"{tag}_xh8", [128, 4096], F32R)
    xhat = [xh8[:, c * W:(c + 1) * W] for c in range(8)]
    qkT = [sbuf(nc, f"{tag}_qk{c}", [128, T], BF16) for c in range(4)]
    pnT = [arena.bitcast(BF16)[:, c * T:(c + 1) * T] for c in range(4)]
    vtok = [sbuf(nc, f"{tag}_vt{j}", [128, 512], BF16) for j in range(NW)]
    aoT = [sbuf(nc, f"{tag}_ao{c}", [128, T], BF16) for c in range(4)]

    _sc = 20480
    bk = [_ab[:, _sc:_sc + 384], _ab[:, _sc + 384:_sc + 768]]
    bkb = [_ab[:, _sc + 768:_sc + 1152], _ab[:, _sc + 1152:_sc + 1536]]
    bq0 = [_ab[:, _sc + 1536:_sc + 1664], _ab[:, _sc + 1664:_sc + 1792]]
    bq1 = [_ab[:, _sc + 1792:_sc + 1920], _ab[:, _sc + 1920:_sc + 2048]]
    ex = [_ab[:, _sc + 2048:_sc + 2816], _ab[:, _sc + 2816:_sc + 3584]]
    aTs = ex
    nmax = [sbuf(nc, f"{tag}_nm{i}", [128, 2]) for i in range(2)]
    rsum = [sbuf(nc, f"{tag}_rs{i}", [128, 2]) for i in range(2)]
    rcp = [sbuf(nc, f"{tag}_rc{i}", [128, 2]) for i in range(2)]

    kit = LnKit(nc, f"{tag}_ln", W)
    P.make_fences(nc)
    P.ins("vector", lambda v: v.memset(P._fb["vector"], 0.0))
    P.ins("gpsimd", lambda g: g.memset(P._fb["gpsimd"], 0.0))
    P.ins("vector", lambda v: v.memset(P._fb["scalar"], 0.0))

    pb = [psum(nc, f"{tag}_pb{i}", [128, 512]) for i in range(8)]
    psS = pb[0][0:2, :]
    psS2 = pb[1][0:2, :]
    psB1 = pb[2]
    psB2 = pb[3]
    ps_qk = pb[4]
    ps_v = pb[5]
    ps_s = [pb[0][:, 0:384], pb[1][:, 0:384]]
    ps_s2 = [pb[2][:, 0:384], pb[3][:, 0:384]]
    ps_t = [pb[4].bitcast(BF16)[:, 0:768], pb[5].bitcast(BF16)[:, 0:768]]
    ps_o = [pb[6][:, 0:128], pb[7][:, 0:128]]
    ps_c = pb[7]

    tk_lngb = P.dma("sync", lngb[:], io["lngb"])
    for nm, dst in [("c_rotq", rotq), ("c_rotk", rotk), ("c_masks", masks),
                    ("c_perm", perm), ("c_eye", eye)]:
        P.dma("sync", dst[:], io[nm])
    P.wait("vector", tk_lngb)
    P.ins("vector", lambda v: v.memset(lnbr[:, 16:18].bitcast(F32), 0.0))
    P.ins("vector", lambda v: v.memset(kit.epsc, 1e-5))
    P.ins("vector", lambda v: v.memset(kit.onesr, 1.0))
    P.ins("vector", lambda v: v.memset(ones1r[:], 1.0))
    P.ins("vector", lambda v: v.memset(kit.ra, 0.0))
    P.ins("vector", lambda v: v.memset(kit.rb, 0.0))
    tk_lnbr = P.ins("vector", lambda v: v.tensor_copy(lnbr[:, 0:16],
                                                      lngb[:]))

    rhs_pad = arena[:, 5120:5248]
    P.ins("vector", lambda v: v.memset(rhs_pad, 0.0))
    t_prev_use = None
    for k in range(8):
        tl = P.dma("sync", wraw[:], io["wqkvT"][k * 128:(k + 1) * 128, :])
        P.wait("vector", tl, t_prev_use or tl)
        tgw = P.ins("vector", lambda v, k=k: v.tensor_scalar_mul(
            wq[k][:], wraw[:], lngb[:, k:k + 1]))
        P.ins("vector", lambda v, k=k: v.tensor_copy(
            rhs_pad[:, 0:1], lnbr[:, 8 + k:9 + k]))
        trr = P.fence("vector")
        P.wait("tensor", trr, tk_lnbr)
        for oc in range(8):
            P.ins("tensor", lambda t, k=k, oc=oc: t.matmul(
                pb[oc][:, 0:128],
                wraw[:, oc * 128:(oc + 1) * 128], rhs_pad,
                start=(k == 0), stop=(k == 7)))
        t_prev_use = P.mark("tensor")
        if k < 7:
            P.wait("sync", trr, t_prev_use)
    P.wait("vector", t_prev_use)
    for oc in range(8):
        P.ins("vector", lambda v, oc=oc: v.tensor_copy(
            bw_cols[:, oc:oc + 1], pb[oc][:, 0:1]))
    t_bw = P.fence("vector")
    P.barrier()

    for tt in range(NT):
        t0 = tt * W
        for c in range(8):
            P.dma("sync", xtile[c][:], io["xT"][c * 128:(c + 1) * 128,
                                                t0:t0 + W])
        ld_all = P.mark_dma("sync")
        P.wait("vector", ld_all)
        tc = []
        for c in range(8):
            tc.append(P.ins("vector", lambda v, c=c: v.tensor_copy(
                xhat[c][:], xtile[c][:])))
        last_mm = None
        for c in range(8):
            if last_mm is not None:
                P.wait("vector", last_mm)
            P.ins("vector", lambda v, c=c: v.tensor_copy(
                kit.xstat, xtile[c][:]))
            P.ins("vector", lambda v: v.tensor_tensor(
                kit.sq, kit.xstat, kit.xstat, ALU.mult))
            tf = P.fence("vector")
            P.wait("tensor", tf)
            P.ins("tensor", lambda t, c=c: t.matmul(
                psS, kit.onesr, kit.xstat, start=(c == 0), stop=(c == 7)))
            P.ins("tensor", lambda t, c=c: t.matmul(
                psS2, kit.onesr, kit.sq, start=(c == 0), stop=(c == 7)))
            last_mm = P.mark("tensor")
        inv_n = 1.0 / D
        ra0, rb0 = kit.ra[0:1, :], kit.rb[0:1, :]
        P.wait("vector", last_mm)
        if io.get("dbg_S") is not None and tt == 0:
            t1_ = P.ins("vector", lambda v: v.tensor_copy(
                arena[0:1, 6144:6656], psS[0:1, :]))
            t2_ = P.ins("vector", lambda v: v.tensor_copy(
                arena[0:1, 6656:7168], psS2[0:1, :]))
            P.wait("sync", t2_)
            P.dma("sync", io["dbg_S"][0:1, :], arena[0:1, 6144:6656])
            P.dma("sync", io["dbg_S"][1:2, :], arena[0:1, 6656:7168])
        P.ins("vector", lambda v: v.tensor_scalar_mul(ra0, psS[0:1, :], inv_n))
        P.ins("vector", lambda v: v.tensor_scalar_mul(rb0, psS2[0:1, :], inv_n))
        P.ins("vector", lambda v: v.tensor_tensor(psS[0:1, :], ra0, ra0,
                                                  ALU.mult))
        tv = P.ins("vector", lambda v: v.tensor_tensor(
            rb0, rb0, psS[0:1, :], ALU.subtract))
        P.wait("scalar", tv)
        tsd = P.ins("scalar", lambda s: s.activation(
            rb0, rb0, AF.Sqrt, bias=kit.epsc[0:1, 0:1], scale=1.0))
        P.wait("vector", tsd)
        P.ins("vector", lambda v: v.reciprocal(rb0, rb0))
        P.ins("vector", lambda v: v.tensor_tensor(ra0, ra0, rb0, ALU.mult))
        P.ins("vector", lambda v: v.tensor_scalar_mul(ra0, ra0, -1.0))
        tn = P.fence("vector")
        P.wait("tensor", tn)
        P.ins("tensor", lambda t: t.matmul(
            psB1, ones1r[:], kit.rb[:], start=True, stop=True))
        tb = P.ins("tensor", lambda t: t.matmul(
            psB2, ones1r[:], kit.ra[:], start=True, stop=True))
        P.wait("vector", tb)
        th = []
        for c in range(8):
            P.ins("vector", lambda v, c=c: v.tensor_tensor(
                xhat[c][:], xtile[c][:], psB1, ALU.mult))
            th.append(P.ins("vector", lambda v, c=c: v.tensor_tensor(
                xhat[c][:], xhat[c][:], psB2, ALU.add)))
        th.append(P.fence("vector"))
        P.wait("tensor", th[-1], t_bw)
        if io.get("dbg_xh") is not None and tt == 0:
            P.wait("sync", th[-1], t_bw)
            for c in range(8):
                P.dma("sync", io["dbg_xh"][c * 128:(c + 1) * 128, :],
                      xhat[c].bitcast(F32))
            P.dma("sync", io["dbg_bw"], bw_cols[:])
            tpb = P.ins("vector", lambda v: v.tensor_copy(
                arena[:, 5632:6144], psB1))
            P.wait("sync", tpb)
            P.dma("sync", io["dbg_pb"], arena[:, 5632:6144])
            P.dma("sync", io["dbg_rstd"], kit.rb[:])
        for oc in range(4):
            for k in range(8):
                P.ins("tensor", lambda t, oc=oc, k=k: t.matmul(
                    ps_qk, wq[k][:, oc * 128:(oc + 1) * 128], xhat[k][:],
                    start=(k == 0), stop=(k == 7)))
            tm = P.mark("tensor")
            P.wait("scalar", tm, t_bw)
            te = P.ins("scalar", lambda s, oc=oc, t0=t0: s.activation(
                qkT[oc][:, t0:t0 + W], ps_qk, AF.Identity,
                bias=bw_cols[:, oc:oc + 1], scale=1.0))
            P.wait("tensor", te)
        for sj in range(4):
            wnd = tt * 4 + sj
            for k in range(8):
                P.ins("tensor", lambda t, sj=sj, k=k: t.matmul(
                    ps_v, xhat[k][:, sj * 128:(sj + 1) * 128],
                    wq[k][:, 512:1024], start=(k == 0), stop=(k == 7)))
            tm = P.mark("tensor")
            P.wait("scalar", tm)
            te = P.ins("scalar", lambda s, wnd=wnd: s.copy(vtok[wnd][:], ps_v))
            P.wait("tensor", te)
        P.barrier()

    wstage = arena[:, 5120:5632]
    for k in range(4):
        for hh in range(2):
            tl = P.dma("sync", wstage,
                       io["woutT"][k * 128:(k + 1) * 128,
                                   hh * 512:(hh + 1) * 512])
            P.wait("vector", tl)
            tw = P.ins("vector", lambda v, k=k, hh=hh: v.tensor_copy(
                wo[k][:, hh * 512:(hh + 1) * 512], wstage))
            P.wait("sync", tw)
    P.barrier()

    for c in range(4):
        for j8 in range(8):
            j0 = j8 * 512
            tm = P.ins("tensor", lambda t, c=c, j0=j0: t.matmul(
                ps_qk, perm[:], qkT[c][:, j0:j0 + 512], start=True, stop=True))
            P.wait("scalar", tm)
            te = P.ins("scalar", lambda s, c=c, j0=j0: s.copy(
                pnT[c][:, j0:j0 + 512], ps_qk))
            P.wait("tensor", te)
    P.barrier()

    P.ins("vector", lambda v: v.memset(bq0[0][:], 0.0))
    P.ins("vector", lambda v: v.memset(bq0[1][:], 0.0))
    P.ins("vector", lambda v: v.memset(bq1[0][:], 0.0))
    tz = P.ins("vector", lambda v: v.memset(bq1[1][:], 0.0))
    P.barrier()

    iters = [(hp, w) for hp in range(4) for w in range(NW)]
    last_pe_scores = [None, None]
    last_act_exp = [None, None]
    last_dve_scale = [None, None]
    last_pe_tr = [None, None]
    last_act_evat = [None, None]
    last_pe_av = [None, None]
    last_act_evo = [None, None]
    pss = [ps_s, ps_s2]

    for it, (hp, w) in enumerate(iters):
        p = it % 2
        j0 = (w - 1) * 128
        cs = 128 if w == 0 else 0
        ce = 256 if w == NW - 1 else 384
        mi = 0 if w == 0 else (2 if w == NW - 1 else 1)
        edge = (w == 0 or w == NW - 1)
        a0 = j0 + cs
        ln = ce - cs

        if last_pe_scores[p] is not None:
            P.wait("vector", last_pe_scores[p])
        if edge:
            P.ins("vector", lambda v, p=p: v.memset(bk[p][:], 0.0))
        P.ins("vector", lambda v, p=p, hp=hp, a0=a0, cs=cs, ln=ln:
              v.tensor_tensor(bk[p][:, cs:cs + ln], qkT[hp][:, a0:a0 + ln],
                              rotk[:, cs:cs + ln], ALU.mult))
        P.ins("vector", lambda v, p=p, hp=hp, a0=a0, cs=cs, ln=ln:
              v.tensor_tensor(bkb[p][:, cs:cs + ln], pnT[hp][:, a0:a0 + ln],
                              rotk[:, 384 + cs:384 + cs + ln], ALU.mult))
        tbk = P.ins("vector", lambda v, p=p, cs=cs, ln=ln: v.tensor_tensor(
            bk[p][:, cs:cs + ln], bk[p][:, cs:cs + ln],
            bkb[p][:, cs:cs + ln], ALU.add))
        c0 = w * 128
        P.ins("vector", lambda v, p=p, hp=hp, c0=c0: v.tensor_tensor(
            bq0[p][0:64, :], qkT[hp][0:64, c0:c0 + 128], rotq[0:64, 0:128],
            ALU.mult))
        P.ins("vector", lambda v, p=p, hp=hp, c0=c0: v.tensor_tensor(
            bkb[p][0:64, 0:128], pnT[hp][0:64, c0:c0 + 128],
            rotq[0:64, 128:256], ALU.mult))
        tq0 = P.ins("vector", lambda v, p=p: v.tensor_tensor(
            bq0[p][0:64, :], bq0[p][0:64, :], bkb[p][0:64, 0:128], ALU.add))
        P.ins("vector", lambda v, p=p, hp=hp, c0=c0: v.tensor_tensor(
            bq1[p][64:128, :], qkT[hp][64:128, c0:c0 + 128],
            rotq[64:128, 0:128], ALU.mult))
        P.ins("vector", lambda v, p=p, hp=hp, c0=c0: v.tensor_tensor(
            bkb[p][64:128, 0:128], pnT[hp][64:128, c0:c0 + 128],
            rotq[64:128, 128:256], ALU.mult))
        tq1 = P.ins("vector", lambda v, p=p: v.tensor_tensor(
            bq1[p][64:128, :], bq1[p][64:128, :], bkb[p][64:128, 0:128],
            ALU.add))
        tq1 = P.fence("vector")

        P.wait("tensor", tbk, tq0, tq1)
        if last_act_exp[p] is not None:
            P.wait("tensor", last_act_exp[p])
        for h, bqt in ((0, bq0), (1, bq1)):
            P.ins("tensor", lambda t, p=p, h=h, bqt=bqt: t.matmul(
                pss[p][h], bqt[p][:], bk[p][:], start=True, stop=False))
            P.ins("tensor", lambda t, p=p, h=h, mi=mi: t.matmul(
                pss[p][h], eye[:], masks[:, mi * 384:(mi + 1) * 384],
                start=False, stop=True))
        last_pe_scores[p] = P.mark("tensor")

        P.wait("vector", last_pe_scores[p])
        tm0 = P.ins("vector", lambda v, p=p: v.reduce_max(
            nmax[p][:, 0:1], pss[p][0], AX.X, negate=True))
        tm1 = P.ins("vector", lambda v, p=p: v.reduce_max(
            nmax[p][:, 1:2], pss[p][1], AX.X, negate=True))
        tm1 = P.fence("vector")
        P.wait("scalar", tm0, tm1)
        if last_pe_av[p] is not None:
            P.wait("scalar", last_pe_av[p])
        te0 = P.ins("scalar", lambda s, p=p: s.activation(
            ex[p][:, 0:384], pss[p][0], AF.Exp, bias=nmax[p][:, 0:1],
            scale=1.0, accum_out=rsum[p][:, 0:1]))
        te1 = P.ins("scalar", lambda s, p=p: s.activation(
            ex[p][:, 384:768], pss[p][1], AF.Exp, bias=nmax[p][:, 1:2],
            scale=1.0, accum_out=rsum[p][:, 1:2]))
        tfx = P.fence("scalar")
        last_act_exp[p] = P.mark("scalar")
        P.wait("vector", tfx)
        trc = P.ins("vector", lambda v, p=p: v.reciprocal(rcp[p][:], rsum[p][:]))
        P.fence("vector")
        if last_pe_tr[p] is not None:
            P.wait("vector", last_pe_tr[p])
        P.ins("vector", lambda v, p=p: v.tensor_scalar_mul(
            ex[p][:, 0:384], ex[p][:, 0:384], rcp[p][:, 0:1]))
        tat = P.ins("vector", lambda v, p=p: v.tensor_scalar_mul(
            ex[p][:, 384:768], ex[p][:, 384:768], rcp[p][:, 1:2]))
        tat = P.fence("vector")
        last_dve_scale[p] = tat

        P.wait("tensor", tat)
        if last_act_evat[p] is not None:
            P.wait("tensor", last_act_evat[p])
        jcs = [jc for jc in range(3) if cs <= jc * 128 < ce]
        for h in (0, 1):
            for jc in jcs:
                P.ins("tensor", lambda t, p=p, h=h, jc=jc: t.transpose(
                    ps_t[p][:, h * 384 + jc * 128: h * 384 + jc * 128 + 128],
                    ex[p][:, h * 384 + jc * 128: h * 384 + jc * 128 + 128],
                    eye[:]))
        last_pe_tr[p] = P.mark("tensor")
        P.wait("scalar", last_pe_tr[p])
        if last_pe_av[p] is not None:
            P.wait("scalar", last_pe_av[p])
        tev = P.ins("scalar", lambda s, p=p: s.copy(aTs[p][:], ps_t[p][:]))
        tev = P.fence("scalar")
        last_act_evat[p] = tev

        P.wait("tensor", tev)
        if last_act_evo[p] is not None:
            P.wait("tensor", last_act_evo[p])
        for h in (0, 1):
            for ji, jc in enumerate(jcs):
                P.ins("tensor",
                      lambda t, p=p, h=h, jc=jc, w=w, hp=hp, ji=ji,
                      nj=len(jcs): t.matmul(
                          ps_o[p][h * 64:(h + 1) * 64, :],
                          vtok[w - 1 + jc][:, hp * 128 + h * 64:
                                           hp * 128 + h * 64 + 64],
                          aTs[p][:, h * 384 + jc * 128:h * 384 + jc * 128 + 128],
                          start=(ji == 0), stop=(ji == nj - 1)))
        last_pe_av[p] = P.mark("tensor")
        P.wait("scalar", last_pe_av[p])
        teo = P.ins("scalar", lambda s, hp=hp, w=w, p=p: s.activation(
            aoT[hp][:, w * 128:(w + 1) * 128], ps_o[p], AF.Identity,
            bias=bw_cols[:, 4 + hp:5 + hp], scale=1.0))
        last_act_evo[p] = teo

    P.barrier()

    if io.get("dbg_qk") is not None:
        for c in range(4):
            P.dma("sync", io["dbg_qk"][c * 128:(c + 1) * 128, :], qkT[c])
            P.dma("sync", io["dbg_ao"][c * 128:(c + 1) * 128, :], aoT[c])
        for j in range(NW):
            P.dma("sync", io["dbg_v"][j * 128:(j + 1) * 128, :], vtok[j][:])
        P.barrier()

    for tt in range(NT):
        t0 = tt * W
        for c in range(8):
            P.dma("sync", xtile[c][:], io["xT"][c * 128:(c + 1) * 128,
                                                t0:t0 + W])
        ld_all = P.mark_dma("sync")
        P.wait("vector", ld_all)
        for oc in range(8):
            for k in range(4):
                P.ins("tensor", lambda t, oc=oc, k=k, t0=t0: t.matmul(
                    ps_c, wo[k][:, oc * 128:(oc + 1) * 128],
                    aoT[k][:, t0:t0 + W], start=(k == 0), stop=(k == 3)))
            tm = P.mark("tensor")
            P.wait("vector", tm)
            ta = P.ins("vector", lambda v, oc=oc: v.tensor_tensor(
                xtile[oc][:], ps_c, xtile[oc][:], ALU.add))
            tc = P.fence("vector")
            P.ins("vector", lambda v, oc=oc: v.tensor_copy(
                xhat[oc][:], xtile[oc][:]))
            tc = P.fence("vector")
            P.wait("tensor", tc)
        tb = emit_ln(P, kit, [xh[:] for xh in xhat], psS, psS2, psB1, psB2,
                     ones1r[:], None)
        P.wait("vector", tb)
        st_toks = []
        for c in range(8):
            P.ins("vector", lambda v, c=c: v.tensor_tensor(
                xtile[c][:], xhat[c][:], psB1, ALU.mult))
            tx = P.ins("vector", lambda v, c=c: v.tensor_tensor(
                xtile[c][:], xtile[c][:], psB2, ALU.add))
            tx = P.fence("vector")
            P.wait("sync", tx)
            st_toks.append(P.dma("sync", io["x1T"][c * 128:(c + 1) * 128,
                                                   t0:t0 + W], xtile[c][:]))
        P.wait("vector", *st_toks)
        P.barrier()






DEBUG_P2 = False
DEBUG_P2W = False


def build_p2_program():
    nc = bass.Bass("TRN2", target_bir_lowering=False, debug=False)
    io = {}
    if DEBUG_P2W:
        io["dbg_w1q"] = nc.dram_tensor("dbg_w1q", [128, 4096], BF16,
                                       kind="ExternalOutput").ap()
        io["dbg_w2q"] = nc.dram_tensor("dbg_w2q", [128, 1024], BF16,
                                       kind="ExternalOutput").ap()
        io["dbg_w1qb"] = nc.dram_tensor("dbg_w1qb", [128, 4096], BF16,
                                        kind="ExternalOutput").ap()
        io["dbg_thr"] = nc.dram_tensor("dbg_thr", [128, 4], F32,
                                       kind="ExternalOutput").ap()
        io["dbg_hall"] = nc.dram_tensor("dbg_hall", [4096, 512], BF16,
                                        kind="ExternalOutput").ap()
        io["dbg_x1b"] = nc.dram_tensor("dbg_x1b", [1024, 512], BF16,
                                       kind="ExternalOutput").ap()
        io["dbg_gs"] = nc.dram_tensor("dbg_gs", [128, 4], F32,
                                      kind="ExternalOutput").ap()
    if DEBUG_P2:
        io["dbg_thr"] = nc.dram_tensor("dbg_thr", [128, 4], F32,
                                       kind="ExternalOutput").ap()
        io["dbg_w1q"] = nc.dram_tensor("dbg_w1q", [128, 4096], BF16,
                                       kind="ExternalOutput").ap()
        io["dbg_h"] = nc.dram_tensor("dbg_h", [128, 512], BF16,
                                     kind="ExternalOutput").ap()
        io["dbg_ps"] = nc.dram_tensor("dbg_ps", [128, 512], F32,
                                      kind="ExternalOutput").ap()
        io["dbg_g"] = nc.dram_tensor("dbg_g", [128, 2], F32,
                                     kind="ExternalOutput").ap()
        io["dbg_hall"] = nc.dram_tensor("dbg_hall", [4096, 512], BF16,
                                        kind="ExternalOutput").ap()
        io["dbg_r"] = nc.dram_tensor("dbg_r", [1024, 512], F32,
                                     kind="ExternalOutput").ap()
        io["dbg_w2q"] = nc.dram_tensor("dbg_w2q", [128, 1024], BF16,
                                       kind="ExternalOutput").ap()
        io["dbg_pso"] = nc.dram_tensor("dbg_pso", [128, 512], F32,
                                       kind="ExternalOutput").ap()
        io["dbg_x1b"] = nc.dram_tensor("dbg_x1b", [128, 512], BF16,
                                       kind="ExternalOutput").ap()
    io["x1T"] = nc.dram_tensor("x1T", [D, T], F32, kind="ExternalInput").ap()
    io["w1T"] = nc.dram_tensor("w1T", [D, 4096], F32, kind="ExternalInput").ap()
    io["w2T"] = nc.dram_tensor("w2T", [4096, D], F32, kind="ExternalInput").ap()
    io["b1c"] = nc.dram_tensor("b1c", [128, 32], F32, kind="ExternalInput").ap()
    io["b2c"] = nc.dram_tensor("b2c", [128, 8], F32, kind="ExternalInput").ap()
    io["x2T"] = nc.dram_tensor("x2T", [D, T], F32, kind="ExternalOutput").ap()
    with contextlib.ExitStack() as ctx:
        P = Prog(nc, ctx)
        emit_p2(nc, P, io)
        P.finalize([P.mark_dma("sync")])
    return nc


def emit_p2(nc, P, io, tag="p2"):
    W = 512
    NT = T // W
    DF = 4096
    g_dram = nc.dram_tensor(f"{tag}_gd", [2, 128], F32, kind="Internal").ap()

    w1q = [sbuf(nc, f"{tag}_w1q{k}", [128, DF], BF16) for k in range(8)]
    w2q = [sbuf(nc, f"{tag}_w2q{f}", [128, D], BF16) for f in range(32)]
    arena = sbuf(nc, f"{tag}_arena", [128, 4096])
    wstage = arena[:, 0:2048]
    nscr = arena[:, 2048:4096]
    hb_ = sbuf(nc, f"{tag}_hb", [128, 32 * W], BF16)
    hbf = [hb_[:, f * W:(f + 1) * W] for f in range(32)]
    x1t = [sbuf(nc, f"{tag}_x1_{c}", [128, W]) for c in range(8)]
    x1b = [sbuf(nc, f"{tag}_x1b{c}", [128, W], BF16) for c in range(8)]
    b1c = sbuf(nc, f"{tag}_b1c", [128, 32])
    b2c = sbuf(nc, f"{tag}_b2c", [128, 8])
    gcols = sbuf(nc, f"{tag}_gcols", [128, 2])
    gtmp = sbuf(nc, f"{tag}_gtmp", [128, 2])
    gacc = sbuf(nc, f"{tag}_gacc", [128, 48])
    gpad = hb_.bitcast(F32)[:, 0:512]
    thr4 = sbuf(nc, f"{tag}_thr4", [128, 4])
    trow = hb_.bitcast(F32)[0:2, 512:1024]
    ones1r = sbuf(nc, f"{tag}_ones1r", [2, 128])
    onescol = sbuf(nc, f"{tag}_onesc", [128, 2])
    kit = LnKit(nc, f"{tag}_ln", W)
    dbgbuf = nscr[:, 0:512]
    P.make_fences(nc)

    pb = [psum(nc, f"{tag}_pb{i}", [128, 512]) for i in range(8)]
    psS = pb[0][0:2, :]
    psS2 = pb[1][0:2, :]
    psB1 = pb[2]
    psB2 = pb[3]
    ps_h = [pb[6], pb[7]]
    ps_o = [pb[4], pb[4]]

    tb1 = P.dma("sync", b1c[:], io["b1c"])
    tb2 = P.dma("sync", b2c[:], io["b2c"])
    P.ins("vector", lambda v: v.memset(P._fb["vector"], 0.0))
    P.ins("vector", lambda v: v.memset(P._fb["scalar"], 0.0))
    P.ins("vector", lambda v: v.memset(kit.epsc, 1e-5))
    P.ins("vector", lambda v: v.memset(kit.onesr, 1.0))
    P.ins("vector", lambda v: v.memset(ones1r[:], 1.0))
    P.ins("vector", lambda v: v.memset(onescol[:], 1.0))
    P.ins("vector", lambda v: v.memset(kit.ra, 0.0))
    P.ins("vector", lambda v: v.memset(kit.rb, 0.0))
    P.ins("vector", lambda v: v.memset(gcols[:], 0.0))
    P.ins("vector", lambda v: v.memset(gpad[:], 0.0))
    P.ins("vector", lambda v: v.memset(trow[:], 0.0))

    gpi = [0]
    wping = [wstage, nscr]

    def gamma_stream(jobs):
        cons = {0: None, 1: None}
        for i, (src, wdt) in enumerate(jobs):
            pp = i % 2
            if cons[pp] is not None:
                P.wait("sync", cons[pp])
            P.dma("sync", wping[pp][:, 0:wdt], src)
            tl = P.mark_dma("sync")
            P.wait("vector", tl)
            gi = gpi[0]
            cons[pp] = P.ins("vector", lambda v, wdt=wdt, gi=gi, pp=pp:
                             v.reduce_sum(gacc[:, gi:gi + 1],
                                          wping[pp][:, 0:wdt], AX.X,
                                          apply_absolute_value=True))
            gpi[0] += 1

    def wjobs(dram, nrows, ncols):
        out = []
        for k in range(nrows // 128):
            for h0 in range(0, ncols, 2048):
                wdt = min(2048, ncols - h0)
                out.append((dram[k * 128:(k + 1) * 128, h0:h0 + wdt], wdt))
        return out
    gamma_stream(wjobs(io["w1T"], D, DF) + wjobs(io["w2T"], DF, D))
    P.ins("vector", lambda v: v.reduce_sum(gpad[:, 0:1], gacc[:, 0:16],
                                           AX.X))
    P.ins("vector", lambda v: v.reduce_sum(gpad[:, 1:2], gacc[:, 16:48],
                                           AX.X))
    tf = P.fence("vector")
    P.wait("tensor", tf)
    tg = P.ins("tensor", lambda t: t.matmul(pb[5][0:2, 0:2], onescol[:],
                                            gpad[:, 0:2], start=True,
                                            stop=True))
    P.wait("vector", tg)
    nw1 = float(D * DF)
    nw2 = float(DF * D)
    P.ins("vector", lambda v: v.tensor_scalar(
        trow[0:1, 0:1], pb[5][0:1, 0:1], 0.5 / nw1, 0.5e-5, ALU.mult,
        ALU.add))
    P.ins("vector", lambda v: v.tensor_scalar(
        trow[0:1, 2:3], pb[5][0:1, 1:2], 0.5 / nw2, 0.5e-5, ALU.mult,
        ALU.add))
    tr1 = P.ins("vector", lambda v: v.tensor_scalar_mul(
        trow[0:1, 1:2], trow[0:1, 0:1], -1.0))
    P.wait("vector", tr1)
    tr2 = P.ins("vector", lambda v: v.tensor_scalar_mul(
        trow[0:1, 3:4], trow[0:1, 2:3], -1.0))
    P.wait("vector", tr2)
    tf = P.fence("vector")
    P.wait("tensor", tf)
    tb = P.ins("tensor", lambda t: t.matmul(psB1, ones1r[:], trow[:],
                                            start=True, stop=True))
    P.wait("vector", tb)
    if io.get("dbg_gs") is not None:
        tdg = P.ins("vector", lambda v: v.tensor_copy(gcols[:], gpad[:, 0:2]))
        tps = P.ins("vector", lambda v: v.tensor_copy(gtmp[0:1, 0:2],
                                                      pb[5][0:1, 0:2]))
        P.wait("sync", tdg, tps)
        P.dma("sync", io["dbg_gs"][:, 0:2], gcols[:])
        P.dma("sync", io["dbg_gs"][0:1, 2:4], gtmp[0:1, 0:2])
    tthr = P.ins("vector", lambda v: v.tensor_copy(thr4[:], psB1[:, 0:4]))

    def quant_stream(jobs):
        cons = {0: None, 1: None}
        for i, (src, wdt, tcol, dst) in enumerate(jobs):
            pp = i % 2
            if cons[pp] is not None:
                P.wait("sync", cons[pp])
            P.dma("sync", wping[pp][:, 0:wdt], src)
            tl = P.mark_dma("sync")
            P.wait("vector", tl, tthr)
            P.ins("vector", lambda v, wdt=wdt, tcol=tcol, pp=pp, dst=dst:
                  v.tensor_single_scalar(
                      dst, wping[pp][:, 0:wdt],
                      thr4[:, tcol + 1:tcol + 2], ALU.is_le))
            cons[pp] = P.ins("vector", lambda v, wdt=wdt, tcol=tcol,
                             dst=dst, pp=pp: v.scalar_tensor_tensor(
                                 dst, wping[pp][:, 0:wdt],
                                 thr4[:, tcol:tcol + 1], dst,
                                 ALU.is_ge, ALU.subtract))

    def qjobs(dram, nrows, ncols, tcol, dsts):
        out = []
        for k in range(nrows // 128):
            for h0 in range(0, ncols, 2048):
                wdt = min(2048, ncols - h0)
                out.append((dram[k * 128:(k + 1) * 128, h0:h0 + wdt], wdt,
                            tcol, dsts(k, h0, wdt)))
        return out
    quant_stream(qjobs(io["w1T"], D, DF, 0,
                       lambda k, h0, wdt: w1q[k][:, h0:h0 + wdt])
                 + qjobs(io["w2T"], DF, D, 2,
                         lambda k, h0, wdt: w2q[k][:, h0:h0 + wdt]))
    P.barrier()
    if io.get("dbg_thr") is not None:
        P.dma("sync", io["dbg_thr"], thr4[:])
        P.barrier()

    P.wait("scalar", tb1, tb2)
    P.wait("vector", tb1, tb2)
    for tt in range(NT):
        t0 = tt * W
        for c in range(8):
            P.dma("sync", x1t[c][:], io["x1T"][c * 128:(c + 1) * 128,
                                               t0:t0 + W])
        ld_all = P.mark_dma("sync")
        P.wait("scalar", ld_all)
        for c in range(8):
            P.ins("scalar", lambda s, c=c: s.copy(x1b[c][:], x1t[c][:]))
        tf = P.fence("scalar")
        P.wait("tensor", tf)
        ev_prev = None
        for fc in range(32):
            pp = ps_h[fc % 2]
            if fc >= 1:
                P.wait("tensor", ev_prev[fc - 1])
            P.ins("tensor", lambda t: t.nop())
            P.ins("tensor", lambda t: t.nop())
            for k in range(8):
                P.ins("tensor", lambda t, fc=fc, k=k, pp=pp: t.matmul(
                    pp, w1q[k][:, fc * 128:(fc + 1) * 128], x1b[k][:],
                    start=(k == 0), stop=(k == 7)))
            tm = P.mark("tensor")
            if io.get("dbg_ps") is not None and tt == 5 and fc == 0:
                P.wait("vector", tm)
                tdd = P.ins("vector", lambda v, pp=pp: v.tensor_copy(
                    dbgbuf, pp))
                P.wait("sync", tdd)
                P.dma("sync", io["dbg_ps"], dbgbuf)
                P.dma("sync", io["dbg_x1b"], x1b[0][:])
                P.wait("scalar", tdd)
            P.wait("scalar", tm)
            te = P.ins("scalar", lambda s, fc=fc, pp=pp: s.activation(
                hbf[fc], pp, AF.Relu, bias=b1c[:, fc:fc + 1], scale=1.0))
            if ev_prev is None:
                ev_prev = {}
            ev_prev[fc] = te
        tfh = P.fence("scalar")
        if io.get("dbg_h") is not None and tt == 5:
            P.wait("sync", tfh)
            P.dma("sync", io["dbg_h"], hbf[0])
        P.wait("tensor", tfh)
        if io.get("dbg_hall") is not None and tt == 5:
            tfh2 = P.fence("scalar")
            P.wait("sync", tfh2)
            for f in range(32):
                P.dma("sync", io["dbg_hall"][f * 128:(f + 1) * 128, :],
                      hbf[f])
        evo = {}
        for oc in range(8):
            pp = ps_o[oc % 2]
            if oc >= 1:
                P.wait("tensor", evo[oc - 1])
            for f in range(32):
                if f % 8 == 0:
                    P.ins("tensor", lambda t: t.nop())
                    P.ins("tensor", lambda t: t.nop())
                P.ins("tensor", lambda t, oc=oc, f=f, pp=pp: t.matmul(
                    pp, w2q[f][:, oc * 128:(oc + 1) * 128], hbf[f],
                    start=(f == 0), stop=(f == 31)))
            tm = P.mark("tensor")
            if io.get("dbg_pso") is not None and tt == 5 and oc == 0:
                P.barrier()
                td2 = P.ins("vector", lambda v, pp=pp: v.tensor_copy(
                    dbgbuf, pp))
                P.wait("sync", td2)
                P.dma("sync", io["dbg_pso"], dbgbuf)
                P.dma("sync", io["dbg_w2q"], w2q[0][:])
                P.barrier()
            P.wait("vector", tm)
            P.ins("vector", lambda v, oc=oc, pp=pp: v.tensor_tensor(
                x1t[oc][:], pp, x1t[oc][:], ALU.add))
            P.ins("vector", lambda v, oc=oc: v.tensor_scalar_add(
                x1t[oc][:], x1t[oc][:], b2c[:, oc:oc + 1]))
            evo[oc] = P.fence("vector")
        if io.get("dbg_r") is not None and tt == 5:
            tfr = P.fence("vector")
            P.wait("sync", tfr)
            for c in range(8):
                P.dma("sync", io["dbg_r"][c * 128:(c + 1) * 128, :],
                      x1t[c][:])
        tb = emit_ln(P, kit, [x[:] for x in x1t], psS, psS2, psB1, psB2,
                     ones1r[:], None)
        P.wait("vector", tb)
        st_toks = []
        for c in range(8):
            P.ins("vector", lambda v, c=c: v.tensor_tensor(
                x1t[c][:], x1t[c][:], psB1, ALU.mult))
            P.ins("vector", lambda v, c=c: v.tensor_tensor(
                x1t[c][:], x1t[c][:], psB2, ALU.add))
            tx = P.fence("vector")
            P.wait("sync", tx)
            st_toks.append(P.dma("sync", io["x2T"][c * 128:(c + 1) * 128,
                                                   t0:t0 + W], x1t[c][:]))
        P.barrier()
    if io.get("dbg_w1q") is not None:
        P.dma("sync", io["dbg_w1q"], w1q[0][:])
        P.dma("sync", io["dbg_w2q"], w2q[0][:])
        P.dma("sync", io["dbg_w1qb"], w1q[7][:])
        if DEBUG_P2W:
            P.dma("sync", io["dbg_thr"], thr4[:])
            for f in range(32):
                P.dma("sync", io["dbg_hall"][f * 128:(f + 1) * 128, :],
                      hbf[f])
            for c in range(8):
                P.dma("sync", io["dbg_x1b"][c * 128:(c + 1) * 128, :],
                      x1b[c][:])
            P.dma("gpsimd", io["dbg_hall"][4000:4002, 0:512], kit.rb[:])
        P.barrier()


def prep_p2_inputs(inputs, core, x1T=None):
    d = inputs
    m = dict(
        w1T=np.ascontiguousarray(d["L0_ffn_W1"].T),
        w2T=np.ascontiguousarray(d["L0_ffn_W2"].T),
        b1c=_pack_cols(d["L0_ffn_b1"]),
        b2c=_pack_cols(d["L0_ffn_b2"]),
    )
    if x1T is not None:
        m["x1T"] = x1T
    return m





DEBUG_P3 = False


def build_p3_program():
    nc = bass.Bass("TRN2", target_bir_lowering=False, debug=False)
    io = {}
    if DEBUG_P3:
        io["dbg_xab"] = nc.dram_tensor("dbg_xab", [2048, 512], BF16,
                                       kind="ExternalOutput").ap()
        io["dbg_zb"] = nc.dram_tensor("dbg_zb", [2048, 512], BF16,
                                      kind="ExternalOutput").ap()
        io["dbg_xdb"] = nc.dram_tensor("dbg_xdb", [128, 512], F32,
                                       kind="ExternalOutput").ap()
        io["dbg_dtf"] = nc.dram_tensor("dbg_dtf", [2048, 512], F32,
                                       kind="ExternalOutput").ap()
        io["dbg_ydc"] = nc.dram_tensor("dbg_ydc", [2048, 512], F32,
                                       kind="ExternalOutput").ap()
        io["dbg_yb"] = nc.dram_tensor("dbg_yb", [2048, 512], BF16,
                                      kind="ExternalOutput").ap()
        io["dbg_xst0"] = nc.dram_tensor("dbg_xst0", [128, 516], BF16,
                                        kind="ExternalOutput").ap()
        io["dbg_st"] = nc.dram_tensor("dbg_st", [128, 256], F32,
                                      kind="ExternalOutput").ap()
        io["dbg_halo0"] = nc.dram_tensor("dbg_halo0", [128, 64], BF16,
                                         kind="ExternalOutput").ap()
        io["dbg_halo1"] = nc.dram_tensor("dbg_halo1", [128, 64], BF16,
                                         kind="ExternalOutput").ap()
        io["dbg_bc"] = nc.dram_tensor("dbg_bc", [32, 512], F32,
                                      kind="ExternalOutput").ap()
        io["dbg_conv0"] = nc.dram_tensor("dbg_conv0", [128, 512], F32,
                                         kind="ExternalOutput").ap()
    io["x2T"] = nc.dram_tensor("x2T", [D, T], F32, kind="ExternalInput").ap()
    io["inpT"] = nc.dram_tensor("inpT", [D, 4096], F32,
                                kind="ExternalInput").ap()
    io["convw"] = nc.dram_tensor("convw", [128, 64], F32,
                                 kind="ExternalInput").ap()
    io["convb"] = nc.dram_tensor("convb", [128, 16], F32,
                                 kind="ExternalInput").ap()
    io["xpT"] = nc.dram_tensor("xpT", [2048, 128], F32,
                               kind="ExternalInput").ap()
    io["dtT"] = nc.dram_tensor("dtT", [64, 2048], F32,
                               kind="ExternalInput").ap()
    io["dtb"] = nc.dram_tensor("dtb", [128, 16], F32,
                               kind="ExternalInput").ap()
    io["alog"] = nc.dram_tensor("alog", [128, 256], F32,
                                kind="ExternalInput").ap()
    io["dcol"] = nc.dram_tensor("dcol", [128, 16], F32,
                                kind="ExternalInput").ap()
    io["outT"] = nc.dram_tensor("outT", [2048, D], F32,
                                kind="ExternalInput").ap()
    io["x3T"] = nc.dram_tensor("x3T", [D, T], F32, kind="ExternalOutput").ap()
    io["c_eye"] = nc.dram_tensor("c_eye", [128, 128], BF16,
                                 kind="ExternalInput").ap()
    with contextlib.ExitStack() as ctx:
        P = Prog(nc, ctx)
        emit_p3(nc, P, io)
        P.finalize([P.mark_dma("sync")])
    return nc


def emit_p3(nc, P, io, tag="p3"):
    W = 512
    NC = T // W

    bc_dram = nc.dram_tensor(f"{tag}_bc", [32, W], BF16, kind="Internal").ap()

    inph = [sbuf(nc, f"{tag}_inph{k}", [128, 2048], BF16) for k in range(8)]
    outw = [sbuf(nc, f"{tag}_outw{k}", [128, D], BF16) for k in range(16)]
    xpw = sbuf(nc, f"{tag}_xpw", [128, 16 * 128], BF16)
    dtw = sbuf(nc, f"{tag}_dtw", [64, 2048], BF16)
    convw = sbuf(nc, f"{tag}_convw", [128, 64])
    convb = sbuf(nc, f"{tag}_convb", [128, 16])
    dtb = sbuf(nc, f"{tag}_dtb", [128, 16])
    acol = sbuf(nc, f"{tag}_acol", [128, 256])
    dcol = sbuf(nc, f"{tag}_dcol", [128, 16])
    onesc = sbuf(nc, f"{tag}_onesc", [128, 1])
    ones1r = sbuf(nc, f"{tag}_ones1r", [2, 128])

    ar3 = sbuf(nc, f"{tag}_ar3", [128, 8192])
    x2t = [ar3[:, c * W:(c + 1) * W] for c in range(8)]
    ydc = [ar3[:, d * W:(d + 1) * W] for d in range(16)]
    yb = [inph[d // 4][:, (d % 4) * W:(d % 4 + 1) * W] for d in range(16)]
    ar5 = sbuf(nc, f"{tag}_ar5", [128, 2048])
    x2b = [ar5.bitcast(BF16)[:, c * W:(c + 1) * W] for c in range(8)]
    sp1 = sbuf(nc, f"{tag}_sp1", [128, W])
    sp2 = sbuf(nc, f"{tag}_sp2", [128, W])
    xdb = ar5[:, 1024:1536]
    xdbb = ar5.bitcast(BF16)[0:64, 3072:3584]
    xst = [sbuf(nc, f"{tag}_xst{i}", [128, W + 4], BF16) for i in range(2)]
    halo = sbuf(nc, f"{tag}_halo", [128, 64], BF16)

    zb = [sbuf(nc, f"{tag}_zb{d}", [128, W], BF16) for d in range(16)]
    xab = [sbuf(nc, f"{tag}_xab{d}", [128, W], BF16) for d in range(16)]
    dtf = [sbuf(nc, f"{tag}_dt{d}", [128, W]) for d in range(16)]
    ub = [sbuf(nc, f"{tag}_ub{d}", [128, W], BF16) for d in range(16)]
    state = [sbuf(nc, f"{tag}_st{d}", [128, 16]) for d in range(16)]
    def _f32t(k):
        v = inph[k].bitcast(F32)
        return [v[:, 0:W], v[:, W:2 * W]]
    def _bf16t(k):
        return [inph[k][:, 0:W], inph[k][:, W:2 * W]]
    bcb = _bf16t(4)
    ccb = _bf16t(5)
    dat = _f32t(6)
    gin = _bf16t(7)
    gout = _bf16t(0)
    ymul = _bf16t(1)
    yacc = [ar3[:, 4096 + i * W:4096 + (i + 1) * W] for i in range(8)]
    eyeb3 = sbuf(nc, f"{tag}_eye", [128, 128], BF16)
    kit = LnKit(nc, f"{tag}_ln", W)
    P.make_fences(nc)

    pb = [psum(nc, f"{tag}_pb{i}", [128, 512]) for i in range(8)]
    psS = pb[0][0:2, :]
    psS2 = pb[1][0:2, :]
    psB1 = pb[2]
    psB2 = pb[3]
    ps_a = [pb[4], pb[5]]
    ps_b2 = [pb[6], pb[7]]

    P.ins("vector", lambda v: v.memset(P._fb["vector"], 0.0))
    P.ins("vector", lambda v: v.memset(P._fb["scalar"], 0.0))
    P.ins("vector", lambda v: v.memset(kit.epsc, 1e-5))
    P.ins("vector", lambda v: v.memset(kit.onesr, 1.0))
    P.ins("vector", lambda v: v.memset(onesc, 1.0))
    P.ins("vector", lambda v: v.memset(ones1r[:], 1.0))
    P.ins("vector", lambda v: v.memset(kit.ra, 0.0))
    P.ins("vector", lambda v: v.memset(kit.rb, 0.0))
    P.ins("vector", lambda v: v.memset(halo[:], 0.0))
    for d in range(16):
        P.ins("vector", lambda v, d=d: v.memset(state[d][:], 0.0))
    for nm, dst in [("convw", convw), ("convb", convb), ("dtb", dtb),
                    ("dcol", dcol)]:
        P.dma("sync", dst[:], io[nm])
    P.dma("sync", eyeb3[:], io["c_eye"])
    ta = P.dma("sync", acol[:], io["alog"])
    tall = P.mark_dma("sync")
    P.wait("scalar", tall)
    tae = P.ins("scalar", lambda s: s.activation(acol[:], acol[:], AF.Exp))
    P.wait("vector", tae)
    P.ins("vector", lambda v: v.tensor_scalar_mul(acol[:], acol[:], -1.0))
    P.dma("gpsimd", dtw[:], io["dtT"])
    for k in range(16):
        P.dma("gpsimd", xpw[:, k * 128:(k + 1) * 128],
              io["xpT"][k * 128:(k + 1) * 128, :])
        P.dma("gpsimd", outw[k][:], io["outT"][k * 128:(k + 1) * 128, :])
    P.barrier()

    for cc in range(NC):
        t0 = cc * W
        for c in range(8):
            P.dma("sync", x2t[c], io["x2T"][c * 128:(c + 1) * 128,
                                            t0:t0 + W])
        for k in range(8):
            P.dma("gpsimd", inph[k][:],
                  io["inpT"][k * 128:(k + 1) * 128, 0:2048])
        ld_all = P.mark_dma("sync")
        ld_g = P.mark_dma("gpsimd")
        P.wait("scalar", ld_all)
        for c in range(8):
            P.ins("scalar", lambda s, c=c: s.copy(x2b[c], x2t[c]))
        tf = P.fence("scalar")
        P.wait("tensor", tf, ld_g)

        evp = {}
        for oc in range(16):
            pp = ps_a[oc % 2]
            xp = xst[oc % 2]
            if oc >= 2:
                P.wait("tensor", evp[oc - 2])
            for k in range(8):
                P.ins("tensor", lambda t, oc=oc, k=k, pp=pp: t.matmul(
                    pp, inph[k][:, oc * 128:(oc + 1) * 128], x2b[k][:],
                    start=(k == 0), stop=(k == 7)))
            tm = P.mark("tensor")
            P.wait("vector", tm)
            P.ins("vector", lambda v, oc=oc, xp=xp: v.tensor_copy(
                xp[:, 0:4], halo[:, oc * 4:(oc + 1) * 4]))
            tev = P.ins("vector", lambda v, oc=oc, xp=xp, pp=pp:
                        v.tensor_copy(xp[:, 4:W + 4], pp))
            P.ins("vector", lambda v, oc=oc, pp=pp: v.tensor_copy(
                halo[:, oc * 4:(oc + 1) * 4], pp[:, W - 4:W]))
            evp[oc] = P.fence("vector")
            P.ins("vector", lambda v, oc=oc, xp=xp: v.tensor_scalar_mul(
                sp1, xp[:, 1:W + 1], convw[:, oc:oc + 1]))
            P.ins("vector", lambda v, oc=oc, xp=xp: v.scalar_tensor_tensor(
                sp1, xp[:, 2:W + 2], convw[:, 16 + oc:17 + oc], sp1,
                ALU.mult, ALU.add))
            P.ins("vector", lambda v, oc=oc, xp=xp: v.scalar_tensor_tensor(
                sp1, xp[:, 3:W + 3], convw[:, 32 + oc:33 + oc], sp1,
                ALU.mult, ALU.add))
            P.ins("vector", lambda v, oc=oc, xp=xp: v.scalar_tensor_tensor(
                sp1, xp[:, 4:W + 4], convw[:, 48 + oc:49 + oc], sp1,
                ALU.mult, ALU.add))
            tcv = P.fence("vector")
            if DEBUG_P3 and cc == 1 and oc == 0:
                P.wait("sync", tcv)
                P.dma("sync", io["dbg_xst0"], xp[:])
                P.dma("sync", io["dbg_conv0"], sp1)
                tdmp = P.mark_dma("sync")
                P.wait("scalar", tdmp)
            P.wait("scalar", tcv)
            tsl = P.ins("scalar", lambda s, oc=oc: s.activation(
                xab[oc][:], sp1, AF.Silu, bias=convb[:, oc:oc + 1],
                scale=1.0))
            tslf = P.fence("scalar")
            P.wait("vector", tslf)
        P.wait("gpsimd", P.mark("tensor"))
        for k in range(8):
            P.dma("gpsimd", inph[k][:],
                  io["inpT"][k * 128:(k + 1) * 128, 2048:4096])
        ld_g = P.mark_dma("gpsimd")
        P.wait("tensor", ld_g, evp[14], evp[15])
        for oc in range(16):
            pp = ps_a[oc % 2]
            if oc >= 2:
                P.wait("tensor", evp[oc - 2])
            for k in range(8):
                P.ins("tensor", lambda t, oc=oc, k=k, pp=pp: t.matmul(
                    pp, inph[k][:, oc * 128:(oc + 1) * 128], x2b[k][:],
                    start=(k == 0), stop=(k == 7)))
            tm = P.mark("tensor")
            P.wait("scalar", tm)
            P.ins("scalar", lambda s, oc=oc, pp=pp: s.activation(
                zb[oc][:], pp, AF.Silu))
            evp[oc] = P.fence("scalar")
        P.wait("tensor", evp[14], evp[15])

        if DEBUG_P3 and cc in (0, 1):
            tdz = P.fence("scalar")
            P.wait("sync", tdz, P.fence("vector"))
            P.dma("sync", io["dbg_halo0" if cc == 0 else "dbg_halo1"],
                  halo[:])
        if DEBUG_P3 and cc == 0:
            tdz = P.fence("scalar")
            P.wait("sync", tdz, P.fence("vector"))
            for dd in range(16):
                P.dma("sync", io["dbg_xab"][dd * 128:(dd + 1) * 128, :],
                      xab[dd][:])
                P.dma("sync", io["dbg_zb"][dd * 128:(dd + 1) * 128, :],
                      zb[dd][:])
        tfx = P.fence("scalar")
        P.wait("tensor", tfx)
        for k in range(16):
            P.ins("tensor", lambda t, k=k: t.matmul(
                ps_b2[0], xpw[:, k * 128:(k + 1) * 128], xab[k][:],
                start=(k == 0), stop=(k == 15)))
        tm = P.mark("tensor")
        P.wait("vector", tm)
        tx = P.ins("vector", lambda v: v.tensor_copy(xdb, ps_b2[0]))
        txf = P.fence("vector")
        P.wait("scalar", txf)
        txb = P.ins("scalar", lambda s: s.copy(xdbb, xdb[0:64, :]))
        P.wait("gpsimd", txf)
        P.dma("gpsimd", bc_dram, xdb[64:96, :])
        tbc = P.mark_dma("gpsimd")
        tfs = P.fence("scalar")
        P.wait("tensor", tfs)

        for oc in range(16):
            pp = ps_b2[1]
            P.ins("tensor", lambda t, oc=oc, pp=pp: t.matmul(
                pp, dtw[:, oc * 128:(oc + 1) * 128], xdbb,
                start=True, stop=True))
            tm = P.mark("tensor")
            P.wait("scalar", tm)
            P.ins("scalar", lambda s, oc=oc, pp=pp: s.activation(
                sp1, pp, AF.Abs, bias=dtb[:, oc:oc + 1], scale=1.0))
            P.wait("scalar", P.fence("scalar"))
            P.ins("scalar", lambda s: s.activation(
                sp1, sp1, AF.Exp, bias=0.0, scale=-1.0))
            P.wait("scalar", P.fence("scalar"))
            P.ins("scalar", lambda s: s.activation(
                sp1, sp1, AF.Ln, bias=onesc[:, 0:1], scale=1.0))
            P.wait("scalar", P.fence("scalar"))
            P.ins("scalar", lambda s, oc=oc, pp=pp: s.activation(
                sp2, pp, AF.Relu, bias=dtb[:, oc:oc + 1], scale=1.0))
            tsf = P.fence("scalar")
            P.wait("vector", tsf)
            P.ins("vector", lambda v, oc=oc: v.tensor_tensor(
                dtf[oc][:], sp1, sp2, ALU.add))
            P.ins("vector", lambda v, oc=oc: v.tensor_tensor(
                ub[oc][:], dtf[oc][:], xab[oc][:], ALU.mult))
            tfu = P.fence("vector")
            P.wait("tensor", tfu)

        if DEBUG_P3 and cc == 0:
            P.wait("sync", P.fence("vector"), P.fence("scalar"))
            P.dma("sync", io["dbg_xdb"], xdb)
            for dd in range(16):
                P.dma("sync", io["dbg_dtf"][dd * 128:(dd + 1) * 128, :],
                      dtf[dd][:])
        tf0 = P.fence("vector")
        stepv = {}
        stepg = {}
        bankf = {}
        for dh in range(2):
            scanf = {0: None, 1: None}
            ymf = {0: None, 1: None}
            pemk = {0: None, 1: None}
            for n in range(16):
                nb = n % 2
                P.wait("sync", tbc)
                if (dh, n - 2) in stepv:
                    P.wait("sync", stepv[(dh, n - 2)], stepg[(dh, n - 2)])
                elif dh == 1 and n < 2:
                    P.wait("sync", stepv[(0, 14 + n)], stepg[(0, 14 + n)])
                P.dma("sync", bcb[nb][:],
                      bc_dram[n:n + 1, :].partition_broadcast(128))
                P.dma("sync", ccb[nb][:],
                      bc_dram[16 + n:17 + n, :].partition_broadcast(128))
                tB = P.mark_dma("sync")
                P.wait("vector", tB)
                for di in range(8):
                    d = dh * 8 + di
                    pp = di % 2
                    if scanf[pp] is not None:
                        P.wait("scalar", scanf[pp])
                    else:
                        P.wait("scalar", P.mark("vector"))
                    P.ins("scalar", lambda sc, d=d, n=n, pp=pp:
                          sc.activation(dat[pp][:], dtf[d][:], AF.Exp,
                                        scale=acol[:, d * 16 + n:
                                                   d * 16 + n + 1]))
                    tdaf = P.fence("scalar")
                    if ymf[pp] is not None:
                        P.wait("vector", ymf[pp])
                    P.ins("vector", lambda v, d=d, nb=nb, pp=pp:
                          v.tensor_tensor(gin[pp][:], ub[d][:], bcb[nb][:],
                                          ALU.mult))
                    P.wait("vector", tdaf)
                    tsc = P.ins("vector", lambda v, d=d, n=n, pp=pp:
                                v.tensor_tensor_scan(
                                    gout[pp][:], dat[pp][:], gin[pp][:],
                                    state[d][:, n:n + 1], ALU.mult, ALU.add))
                    P.wait("vector", tsc)
                    P.ins("vector", lambda v, d=d, n=n, pp=pp: v.tensor_copy(
                        state[d][:, n:n + 1], gout[pp][:, W - 1:W]))
                    if pemk[pp] is not None:
                        P.wait("vector", pemk[pp])
                    P.ins("vector", lambda v, nb=nb, pp=pp: v.tensor_tensor(
                        ymul[pp][:], gout[pp][:], ccb[nb][:], ALU.mult))
                    scanf[pp] = P.fence("vector")
                    ymf[pp] = scanf[pp]
                    P.wait("tensor", ymf[pp])
                    if n == 0 and (dh, di) in bankf:
                        P.wait("tensor", bankf[(dh, di)])
                    P.ins("tensor", lambda t, di=di, n=n, pp=pp: t.matmul(
                        pb[di], eyeb3[:], ymul[pp][:],
                        start=(n == 0), stop=(n == 15)))
                    pemk[pp] = P.mark("tensor")
                stepv[(dh, n)] = scanf[pp]
                stepg[(dh, n)] = ymf[pp]
            tmall = P.mark("tensor")
            P.wait("vector", tmall)
            if dh == 0:
                for di in range(8):
                    P.ins("vector", lambda v, di=di: v.tensor_copy(
                        yacc[di], pb[di]))
                    bankf[(1, di)] = P.fence("vector")

        tmall = P.mark("tensor")
        P.wait("vector", tmall)
        for d in range(16):
            ysrc = yacc[d] if d < 8 else pb[d - 8]
            P.ins("vector", lambda v, d=d: v.tensor_scalar_mul(
                sp2, xab[d][:], dcol[:, d:d + 1]))
            P.ins("vector", lambda v, d=d, ysrc=ysrc: v.tensor_tensor(
                sp2, ysrc, sp2, ALU.add))
            P.ins("vector", lambda v, d=d: v.tensor_tensor(
                yb[d], sp2, zb[d][:], ALU.mult))
        tf = P.fence("vector")

        P.wait("sync", tf)
        for c in range(8):
            P.dma("sync", x2t[c], io["x2T"][c * 128:(c + 1) * 128,
                                            t0:t0 + W])
        ld_all = P.mark_dma("sync")
        P.wait("vector", ld_all)
        P.wait("tensor", tf)

        evo = {}
        for oc in range(8):
            pp = ps_a[oc % 2]
            if oc >= 2:
                P.wait("tensor", evo[oc - 2])
            for k in range(16):
                P.ins("tensor", lambda t, oc=oc, k=k, pp=pp: t.matmul(
                    pp, outw[k][:, oc * 128:(oc + 1) * 128], yb[k],
                    start=(k == 0), stop=(k == 15)))
            tm = P.mark("tensor")
            P.wait("vector", tm)
            P.ins("vector", lambda v, oc=oc, pp=pp: v.tensor_tensor(
                x2t[oc], pp, x2t[oc], ALU.add))
            evo[oc] = P.fence("vector")
        tb = emit_ln(P, kit, [x for x in x2t], psS, psS2, psB1, psB2,
                     ones1r[:], None)
        P.wait("vector", tb)
        for c in range(8):
            P.ins("vector", lambda v, c=c: v.tensor_tensor(
                x2t[c], x2t[c], psB1, ALU.mult))
            P.ins("vector", lambda v, c=c: v.tensor_tensor(
                x2t[c], x2t[c], psB2, ALU.add))
            tx = P.fence("vector")
            P.wait("sync", tx)
            P.dma("sync", io["x3T"][c * 128:(c + 1) * 128, t0:t0 + W],
                  x2t[c])
        P.barrier()


DEBUG_P4 = False


def build_p4_program():
    nc = bass.Bass("TRN2", target_bir_lowering=False, debug=False)
    io = {}
    io["x3T"] = nc.dram_tensor("x3T", [D, T], F32, kind="ExternalInput").ap()
    io["wrPT"] = nc.dram_tensor("wrPT", [D, 128], F32,
                                kind="ExternalInput").ap()
    io["wmhT"] = nc.dram_tensor("wmhT", [D, D], F32,
                                kind="ExternalInput").ap()
    io["kT"] = nc.dram_tensor("kT", [1024, 4096], F32,
                              kind="ExternalInput").ap()
    io["vT"] = nc.dram_tensor("vT", [32768, 128], F32,
                              kind="ExternalInput").ap()
    io["wmgT"] = nc.dram_tensor("wmgT", [D, D], F32,
                                kind="ExternalInput").ap()
    io["eyeB"] = nc.dram_tensor("eyeB", [128, 128], BF16,
                                kind="ExternalInput").ap()
    io["x4T"] = nc.dram_tensor("x4T", [D, T], F32, kind="ExternalOutput").ap()
    if DEBUG_P4:
        io["dbg_gw"] = nc.dram_tensor("dbg_gw", [16, T], BF16,
                                      kind="ExternalOutput").ap()
        io["dbg_q"] = nc.dram_tensor("dbg_q", [1024, T], BF16,
                                     kind="ExternalOutput").ap()
        io["dbg_yc"] = nc.dram_tensor("dbg_yc", [1024, T], BF16,
                                      kind="ExternalOutput").ap()
        io["dbg_m"] = nc.dram_tensor("dbg_m", [128, 64], F32,
                                     kind="ExternalOutput").ap()
        io["dbg_msk"] = nc.dram_tensor("dbg_msk", [128, 512], F32,
                                       kind="ExternalOutput").ap()
        io["dbg_sig"] = nc.dram_tensor("dbg_sig", [128, 512], F32,
                                       kind="ExternalOutput").ap()
        io["dbg_gwt"] = nc.dram_tensor("dbg_gwt", [128, 512], BF16,
                                       kind="ExternalOutput").ap()
    with contextlib.ExitStack() as ctx:
        P = Prog(nc, ctx)
        st = emit_p4(nc, P, io)
        P.finalize(st)
    return nc


def emit_p4(nc, P, io, tag="p4"):
    W = 512
    H = 8
    gw_dram = nc.dram_tensor(f"{tag}_gw", [16, T], BF16, kind="Internal").ap()

    x3b = [sbuf(nc, f"{tag}_x3b{k}", [128, T], BF16) for k in range(8)]
    qall = x3b
    ycat = [sbuf(nc, f"{tag}_yc{h}", [128, T], BF16) for h in range(8)]
    kb = [sbuf(nc, f"{tag}_kb{i}", [128, 4096], BF16) for i in range(2)]
    vb = [sbuf(nc, f"{tag}_vb{i}", [128, 4096], BF16) for i in range(2)]
    wmg = ([kb[0][:, k * 1024:(k + 1) * 1024] for k in range(4)]
           + [vb[0][:, k * 1024:(k + 1) * 1024] for k in range(4)])
    x3r = ([kb[1].bitcast(F32)[:, c * W:(c + 1) * W] for c in range(4)]
           + [vb[1].bitcast(F32)[:, c * W:(c + 1) * W] for c in range(4)])
    wmh_ar = sbuf(nc, f"{tag}_wmh", [128, 8192], BF16)
    wmh = [wmh_ar[:, k * 1024:(k + 1) * 1024] for k in range(8)]
    gwb1_ar = sbuf(nc, f"{tag}_gwb1", [128, 8192], BF16)
    gwb = [[wmh_ar[:, e * W:(e + 1) * W] for e in range(16)],
           [gwb1_ar[:, e * W:(e + 1) * W] for e in range(16)]]
    aT = [sbuf(nc, f"{tag}_aT{i}", [128, W], BF16) for i in range(4)]
    wrb = sbuf(nc, f"{tag}_wrb", [128, 1024], BF16)
    eyeb = sbuf(nc, f"{tag}_eye", [128, 128], BF16)
    gwt = sbuf(nc, f"{tag}_gwt", [128, 128], BF16)
    stg16 = sbuf(nc, f"{tag}_st16", [16, 128], BF16)
    m1 = sbuf(nc, f"{tag}_m1", [128, 1])
    m2 = sbuf(nc, f"{tag}_m2", [128, 1])
    msk = sbuf(nc, f"{tag}_msk", [128, 16])
    l2s = sbuf(nc, f"{tag}_l2", [128, 16])
    sig = sbuf(nc, f"{tag}_sig", [128, 16])
    ones1r = sbuf(nc, f"{tag}_ones1r", [2, 128])
    kit = LnKit(nc, f"{tag}_ln", W)
    P.make_fences(nc)

    pb = [psum(nc, f"{tag}_pb{i}", [128, 512]) for i in range(8)]
    psS = pb[0][0:2, :]
    psS2 = pb[1][0:2, :]
    psB1 = pb[2]
    psB2 = pb[3]

    P.ins("vector", lambda v: v.memset(P._fb["vector"], 0.0))
    P.ins("vector", lambda v: v.memset(P._fb["scalar"], 0.0))
    P.ins("vector", lambda v: v.memset(kit.epsc, 1e-5))
    P.ins("vector", lambda v: v.memset(kit.onesr, 1.0))
    P.ins("vector", lambda v: v.memset(ones1r[:], 1.0))
    P.ins("vector", lambda v: v.memset(kit.ra, 0.0))
    P.ins("vector", lambda v: v.memset(kit.rb, 0.0))
    P.ins("vector", lambda v: v.memset(gwt[:], 0.0))
    P.dma("sync", eyeb[:], io["eyeB"])
    teye = P.mark_dma("sync")
    P.dma("gpsimd", wrb[:],
          io["wrPT"].rearrange("(c p) e -> p c e", p=128))
    for k in range(8):
        P.dma("gpsimd", wmh[k], io["wmhT"][k * 128:(k + 1) * 128, :])
        P.dma("gpsimd", x3b[k][:], io["x3T"][k * 128:(k + 1) * 128, :])
    P.dma("gpsimd", kb[0][:], io["kT"][0:128, :])
    P.dma("gpsimd", vb[0][:],
          io["vT"][0:4096, :].rearrange("(c p) d -> p c d", p=128))
    tinit = P.mark_dma("gpsimd")
    kvmark = {0: tinit}

    P.wait("tensor", tinit, teye)
    rfin = {}
    prev_tevf = None
    prev_spill = None
    prev_tmt = None
    for tc32 in range(32):
        pr = pb[tc32 % 2]
        if tc32 >= 2:
            P.wait("tensor", rfin[tc32 - 2])
        for k in range(8):
            P.ins("tensor", lambda t, k=k, tc32=tc32, pr=pr: t.matmul(
                pr[:, 0:128], x3b[k][:, tc32 * 128:(tc32 + 1) * 128],
                wrb[:, k * 128:(k + 1) * 128],
                start=(k == 0), stop=(k == 7)))
        tm = P.mark("tensor")
        P.wait("vector", tm)
        if prev_tmt is not None:
            P.wait("vector", prev_tmt)
        lg = pr[:, 0:16]
        def FW():
            P.wait("vector", P.fence("vector"))
        P.ins("vector", lambda v, lg=lg: v.reduce_max(m1[:], lg, AX.X))
        FW()
        P.ins("vector", lambda v, lg=lg: v.tensor_single_scalar(
            msk[:], lg, m1[:, 0:1], ALU.is_ge))
        FW()
        P.ins("vector", lambda v: v.tensor_scalar_mul(msk[:], msk[:],
                                                      -1000.0))
        FW()
        P.ins("vector", lambda v, lg=lg: v.tensor_tensor(
            l2s[:], msk[:], lg, ALU.add))
        FW()
        P.ins("vector", lambda v: v.reduce_max(m2[:], l2s[:], AX.X))
        FW()
        P.ins("vector", lambda v, lg=lg: v.tensor_single_scalar(
            msk[:], lg, m2[:, 0:1], ALU.is_ge))
        FW()
        tv = P.fence("vector")
        P.wait("scalar", tv)
        P.ins("scalar", lambda s, lg=lg: s.activation(sig[:], lg, AF.Sigmoid))
        tsf = P.fence("scalar")
        P.wait("vector", tsf)
        P.ins("vector", lambda v: v.tensor_tensor(gwt[:, 0:16], sig[:],
                                                  msk[:], ALU.mult))
        rfin[tc32] = P.fence("vector")
        if DEBUG_P4 and tc32 < 32:
            P.wait("sync", rfin[tc32])
            P.dma("sync", io["dbg_m"][:, tc32:tc32 + 1], m1[:],
                  allow_slow_non_contiguous=True)
            P.dma("sync", io["dbg_m"][:, 32 + tc32:33 + tc32], m2[:],
                  allow_slow_non_contiguous=True)
            P.dma("sync", io["dbg_msk"][:, tc32 * 16:(tc32 + 1) * 16], msk[:])
            P.dma("sync", io["dbg_sig"][:, tc32 * 16:(tc32 + 1) * 16], sig[:])
            P.dma("sync", io["dbg_gwt"][:, tc32 * 16:(tc32 + 1) * 16],
                  gwt[:, 0:16])
            tdmp = P.mark_dma("sync")
            P.wait("vector", tdmp)
        P.wait("tensor", rfin[tc32])
        if prev_tevf is not None:
            P.wait("tensor", prev_tevf)
        P.ins("tensor", lambda t: t.matmul(pb[2][:, 0:128], gwt[:], eyeb[:],
                                           start=True, stop=True))
        prev_tmt = P.mark("tensor")
        P.wait("vector", prev_tmt)
        if prev_spill is not None:
            P.wait("vector", prev_spill)
        P.ins("vector", lambda v: v.tensor_copy(stg16[:],
                                                pb[2][0:16, 0:128]))
        prev_tevf = P.fence("vector")
        P.wait("sync", prev_tevf)
        P.dma("sync", gw_dram[:, tc32 * 128:(tc32 + 1) * 128], stg16[:])
        prev_spill = P.mark_dma("sync")
    P.barrier()

    for tcq in range(8):
        for h in range(8):
            for k in range(8):
                P.ins("tensor", lambda t, h=h, k=k, tcq=tcq: t.matmul(
                    pb[h], wmh[k][:, h * 128:(h + 1) * 128],
                    x3b[k][:, tcq * W:(tcq + 1) * W],
                    start=(k == 0), stop=(k == 7)))
        tmq = P.mark("tensor")
        P.wait("vector", tmq)
        for h in range(8):
            P.ins("vector", lambda v, h=h, tcq=tcq: v.tensor_copy(
                qall[h][:, tcq * W:(tcq + 1) * W], pb[h]))
        tqf = P.fence("vector")
        P.wait("tensor", tqf)
    P.barrier()
    if DEBUG_P4:
        for h in range(8):
            P.dma("sync", io["dbg_q"][h * 128:(h + 1) * 128, :], qall[h][:])
        P.dma("sync", io["dbg_gw"], gw_dram)
        P.wait("sync", P.mark_dma("sync"))

    for e in range(16):
        P.dma("sync", gwb[0][e],
              gw_dram[e:e + 1, 0:W].partition_broadcast(128))
    bmark = {0: (P.mark_dma("sync"), P.mark_dma("gpsimd"))}
    bguard = {}
    wmg_mark = None
    for h in range(H):
        P.wait("gpsimd", P.mark("tensor"))
        if h + 1 < H:
            P.dma("gpsimd", kb[(h + 1) % 2][:],
                  io["kT"][(h + 1) * 128:(h + 2) * 128, :])
            P.dma("gpsimd", vb[(h + 1) % 2][:],
                  io["vT"][(h + 1) * 4096:(h + 2) * 4096, :]
                  .rearrange("(c p) d -> p c d", p=128))
            kvmark[h + 1] = P.mark_dma("gpsimd")
        else:
            for k in range(8):
                P.dma("gpsimd", wmg[k], io["wmgT"][k * 128:(k + 1) * 128, :])
            wmg_mark = P.mark_dma("gpsimd")
        P.wait("tensor", kvmark[h])
        gelf_by_bank = {}
        yevf = {}
        multf = {}
        for tc in range(8):
            it = h * 8 + tc
            if it + 1 < 64:
                if it >= 1:
                    P.wait("sync", bguard[it - 1])
                    P.wait("gpsimd", bguard[it - 1])
                tcn = (tc + 1) % 8
                for e in range(16):
                    q = "sync" if e % 2 == 0 else "gpsimd"
                    P.dma(q, gwb[(it + 1) % 2][e],
                          gw_dram[e:e + 1, tcn * W:(tcn + 1) * W]
                          .partition_broadcast(128))
                bmark[it + 1] = (P.mark_dma("sync"), P.mark_dma("gpsimd"))
            sgw = gwb[it % 2]
            P.wait("vector", *bmark[it])
            psy = pb[6 + (tc % 2)]

            def issueV(j, tc=tc, h=h, psy=psy):
                P.wait("tensor", multf[j])
                if j == 0 and tc >= 2:
                    P.wait("tensor", yevf[tc - 2])
                P.ins("tensor", lambda t, h=h, j=j, psy=psy:
                      t.matmul(psy, vb[h % 2][:, j * 128:(j + 1) * 128],
                               aT[j % 4][:], start=(j == 0),
                               stop=(j == 31)))
            for ec in range(32):
                bank = pb[ec % 4]
                if (ec % 4) in gelf_by_bank:
                    P.wait("tensor", gelf_by_bank[ec % 4])
                P.ins("tensor", lambda t, h=h, ec=ec, tc=tc, bank=bank:
                      t.matmul(bank, kb[h % 2][:, ec * 128:(ec + 1) * 128],
                               qall[h][:, tc * W:(tc + 1) * W],
                               start=True, stop=True))
                tK = P.mark("tensor")
                if ec >= 3:
                    issueV(ec - 3)
                P.wait("scalar", tK)
                P.ins("scalar", lambda s, ec=ec, bank=bank: s.activation(
                    aT[ec % 4][:], bank, AF.Gelu))
                gelf_by_bank[ec % 4] = P.fence("scalar")
                P.wait("vector", gelf_by_bank[ec % 4])
                P.ins("vector", lambda v, ec=ec, sgw=sgw: v.tensor_tensor(
                    aT[ec % 4][:], aT[ec % 4][:], sgw[ec // 2], ALU.mult))
                multf[ec] = P.fence("vector")
            for j in (29, 30, 31):
                issueV(j)
            tmv = P.mark("tensor")
            P.wait("vector", tmv)
            P.ins("vector", lambda v, h=h, tc=tc, psy=psy: v.tensor_copy(
                ycat[h][:, tc * W:(tc + 1) * W], psy))
            yevf[tc] = P.fence("vector")
            bguard[it] = yevf[tc]
    P.barrier()
    if DEBUG_P4:
        for h in range(8):
            P.dma("sync", io["dbg_yc"][h * 128:(h + 1) * 128, :], ycat[h][:])
        P.wait("sync", P.mark_dma("sync"))

    stores = []
    P.wait("tensor", wmg_mark)
    for tc in range(8):
        for c in range(8):
            P.dma("sync", x3r[c], io["x3T"][c * 128:(c + 1) * 128,
                                            tc * W:(tc + 1) * W])
        ldr = P.mark_dma("sync")
        P.wait("vector", ldr)
        oev = {}
        for oc in range(8):
            po = pb[4 + (oc % 2)]
            if oc >= 2:
                P.wait("tensor", oev[oc - 2])
            for k in range(8):
                P.ins("tensor", lambda t, k=k, oc=oc, tc=tc, po=po: t.matmul(
                    po, wmg[k][:, oc * 128:(oc + 1) * 128],
                    ycat[k][:, tc * W:(tc + 1) * W],
                    start=(k == 0), stop=(k == 7)))
            tm = P.mark("tensor")
            P.wait("vector", tm)
            P.ins("vector", lambda v, oc=oc, po=po: v.tensor_tensor(
                x3r[oc], po, x3r[oc], ALU.add))
            oev[oc] = P.fence("vector")
        tb = emit_ln(P, kit, x3r, psS, psS2, psB1, psB2, ones1r[:], None)
        P.wait("vector", tb)
        for c in range(8):
            P.ins("vector", lambda v, c=c: v.tensor_tensor(
                x3r[c], x3r[c], psB1, ALU.mult))
            P.ins("vector", lambda v, c=c: v.tensor_tensor(
                x3r[c], x3r[c], psB2, ALU.add))
            tx = P.fence("vector")
            P.wait("sync", tx)
            stores.append(P.dma("sync",
                                io["x4T"][c * 128:(c + 1) * 128,
                                          tc * W:(tc + 1) * W], x3r[c]))
        P.wait("sync", P.mark_dma("sync"))
    return [P.mark_dma("sync")]


def prep_p4_inputs(inputs, core, x3T=None):
    d = inputs
    wr = d["L1_ffn_w_router"].astype(np.float32)
    wrPT = np.zeros((1024, 128), np.float32)
    wrPT[:, 0:16] = wr.T
    kT = np.ascontiguousarray(
        d["L1_ffn_k_ffwd"].astype(np.float32)
        .transpose(0, 3, 1, 2).reshape(1024, 4096))
    vT = np.ascontiguousarray(
        d["L1_ffn_v_ffwd"].astype(np.float32).reshape(32768, 128))
    m = dict(
        wrPT=wrPT,
        wmhT=np.ascontiguousarray(d["L1_ffn_w_mh"].astype(np.float32).T),
        kT=kT,
        vT=vT,
        wmgT=np.ascontiguousarray(d["L1_ffn_w_mg"].astype(np.float32).T),
        eyeB=np.eye(128, dtype=ml_dtypes.bfloat16),
    )
    if x3T is not None:
        m["x3T"] = x3T
    return m


def prep_p3_inputs(inputs, core, x2T=None):
    d = inputs
    cw = d["L1_attn_conv_w"]
    convw = np.ascontiguousarray(
        cw.reshape(16, 128, 4).transpose(1, 2, 0).reshape(128, 64))
    alog = np.ascontiguousarray(
        d["L1_attn_A_log"].reshape(16, 128, 16).transpose(1, 0, 2)
        .reshape(128, 256))
    xpT = np.zeros((2048, 128), np.float32)
    xpT[:, 0:96] = d["L1_attn_x_proj_w"].T
    m = dict(
        inpT=np.ascontiguousarray(d["L1_attn_in_proj_w"].T),
        convw=convw,
        convb=_pack_cols(d["L1_attn_conv_b"]),
        xpT=np.ascontiguousarray(xpT),
        dtT=np.ascontiguousarray(d["L1_attn_dt_proj_w"].T),
        dtb=_pack_cols(d["L1_attn_dt_proj_b"]),
        alog=alog,
        dcol=_pack_cols(d["L1_attn_D"]),
        outT=np.ascontiguousarray(d["L1_attn_out_proj_w"].T),
        c_eye=np.eye(128, dtype=bf16),
    )
    if x2T is not None:
        m["x2T"] = x2T
    return m


def prep_p1_inputs(inputs, core):
    d = inputs
    xT = np.ascontiguousarray(d["L0_attn_x"][core].T.astype(np.float32))
    wqkvT = np.ascontiguousarray(d["L0_attn_w_qkv"].T)
    woutT = np.ascontiguousarray(d["L0_attn_w_out"].T)
    lngb = np.concatenate([_pack_cols(d["L0_attn_ln_g"]),
                           _pack_cols(d["L0_attn_ln_b"])], axis=1)
    m = dict(xT=xT, wqkvT=wqkvT, woutT=woutT,
             lngb=np.ascontiguousarray(lngb))
    m.update(host_consts())
    return m


def build_fused_program():
    nc = bass.Bass("TRN2", target_bir_lowering=False, debug=False)
    io = {}
    io["xT"] = nc.dram_tensor("xT", [D, T], F32, kind="ExternalInput").ap()
    io["wqkvT"] = nc.dram_tensor("wqkvT", [D, D], F32,
                                 kind="ExternalInput").ap()
    io["woutT"] = nc.dram_tensor("woutT", [512, D], F32,
                                 kind="ExternalInput").ap()
    io["lngb"] = nc.dram_tensor("lngb", [128, 16], F32,
                                kind="ExternalInput").ap()
    for k, v in host_consts().items():
        dt = BF16 if v.dtype == bf16 else F32
        io[k] = nc.dram_tensor(k, list(v.shape), dt,
                               kind="ExternalInput").ap()
    io["w1T"] = nc.dram_tensor("w1T", [D, 4096], F32,
                               kind="ExternalInput").ap()
    io["w2T"] = nc.dram_tensor("w2T", [4096, D], F32,
                               kind="ExternalInput").ap()
    io["b1c"] = nc.dram_tensor("b1c", [128, 32], F32,
                               kind="ExternalInput").ap()
    io["b2c"] = nc.dram_tensor("b2c", [128, 8], F32,
                               kind="ExternalInput").ap()
    io["inpT"] = nc.dram_tensor("inpT", [D, 4096], F32,
                                kind="ExternalInput").ap()
    io["convw"] = nc.dram_tensor("convw", [128, 64], F32,
                                 kind="ExternalInput").ap()
    io["convb"] = nc.dram_tensor("convb", [128, 16], F32,
                                 kind="ExternalInput").ap()
    io["xpT"] = nc.dram_tensor("xpT", [2048, 128], F32,
                               kind="ExternalInput").ap()
    io["dtT"] = nc.dram_tensor("dtT", [64, 2048], F32,
                               kind="ExternalInput").ap()
    io["dtb"] = nc.dram_tensor("dtb", [128, 16], F32,
                               kind="ExternalInput").ap()
    io["alog"] = nc.dram_tensor("alog", [128, 256], F32,
                                kind="ExternalInput").ap()
    io["dcol"] = nc.dram_tensor("dcol", [128, 16], F32,
                                kind="ExternalInput").ap()
    io["outT"] = nc.dram_tensor("outT", [2048, D], F32,
                                kind="ExternalInput").ap()
    io["wrPT"] = nc.dram_tensor("wrPT", [D, 128], F32,
                                kind="ExternalInput").ap()
    io["wmhT"] = nc.dram_tensor("wmhT", [D, D], F32,
                                kind="ExternalInput").ap()
    io["kT"] = nc.dram_tensor("kT", [1024, 4096], F32,
                              kind="ExternalInput").ap()
    io["vT"] = nc.dram_tensor("vT", [32768, 128], F32,
                              kind="ExternalInput").ap()
    io["wmgT"] = nc.dram_tensor("wmgT", [D, D], F32,
                                kind="ExternalInput").ap()
    io["eyeB"] = nc.dram_tensor("eyeB", [128, 128], BF16,
                                kind="ExternalInput").ap()
    io["x1T"] = nc.dram_tensor("x1T", [D, T], F32, kind="Internal").ap()
    io["x2T"] = nc.dram_tensor("x2T", [D, T], F32, kind="Internal").ap()
    io["x3T"] = nc.dram_tensor("x3T", [D, T], F32, kind="Internal").ap()
    io["x4T"] = nc.dram_tensor("x4T", [D, T], F32, kind="ExternalOutput").ap()
    with contextlib.ExitStack() as ctx:
        P = Prog(nc, ctx)
        P.make_fences(nc)
        for emitter in (emit_p1, emit_p2, emit_p3):
            with contextlib.ExitStack() as cph:
                _CUR_CTX[0] = cph
                emitter(nc, P, io)
                _CUR_CTX[0] = None
            P.barrier()
        with contextlib.ExitStack() as cph:
            _CUR_CTX[0] = cph
            st = emit_p4(nc, P, io)
            _CUR_CTX[0] = None
            P.finalize(st)
    return nc


def prep_fused_inputs(inputs, core):
    m = prep_p1_inputs(inputs, core)
    m.update(prep_p2_inputs(inputs, core))
    m.update(prep_p3_inputs(inputs, core))
    m.update(prep_p4_inputs(inputs, core))
    return m


def kernel_fused(**inputs):
    cores = list(range(8))
    nc = build_fused_program()
    maps = [prep_fused_inputs(inputs, c) for c in cores]
    r = run_bass_kernel_spmd(nc, maps, core_ids=cores)
    out = np.stack([np.asarray(r.results[c]["x4T"]).T for c in cores])
    return np.ascontiguousarray(out.astype(np.float32))


def kernel(**inputs):
    return kernel_fused(**inputs)


# version 11
# speedup vs baseline: 1.0069x; 1.0069x over previous
import contextlib
import numpy as np
import ml_dtypes
import concourse.bass as bass
import concourse.mybir as mybir
from concourse.bass_utils import run_bass_kernel_spmd

F32 = mybir.dt.float32
F32R = mybir.dt.float32r
BF16 = mybir.dt.bfloat16
AF = mybir.ActivationFunctionType
ALU = mybir.AluOpType
AX = mybir.AxisListType

NCORES = 8
T = 4096
D = 1024
bf16 = ml_dtypes.bfloat16
NEGBF = float(ml_dtypes.finfo(bf16).min)



def _l0_rot_tables():
    dh, ws, jlen = 64, 128, 384
    f32 = np.float32
    inv_freq = 1.0 / (f32(10000.0) ** (np.arange(0, dh, 2, dtype=f32) / f32(dh)))
    t = np.arange(jlen, dtype=f32)
    half = t[:, None] * inv_freq[None, :]
    freqs = np.concatenate([half, half], -1)
    power = (t - jlen // 2) / f32(ws // 2)
    sbase = ((np.arange(0, dh, 2, dtype=f32) + 0.4 * dh) / (1.4 * dh)).astype(f32)
    scale = sbase[None, :] ** power[:, None]
    scale = np.concatenate([scale, scale], -1)
    qf, qs = freqs[-ws:], scale[-ws:]
    inv_s = (1.0 / scale).astype(f32)
    sgn = np.where(np.arange(dh) < dh // 2, f32(-1.0), f32(1.0))[:, None]
    qsc = f32(dh ** -0.5)
    QC = (np.cos(qf) * qs * qsc).T
    QS = (np.sin(qf) * qs * qsc).T * sgn
    KC = (np.cos(freqs) * inv_s).T
    KS = (np.sin(freqs) * inv_s).T * sgn
    QCp = np.concatenate([QC, QC], 0)
    QSp = np.concatenate([QS, QS], 0)
    KCp = np.concatenate([KC, KC], 0)
    KSp = np.concatenate([KS, KS], 0)
    rotq = np.ascontiguousarray(np.concatenate([QCp, QSp], 1)).astype(bf16)
    rotk = np.ascontiguousarray(np.concatenate([KCp, KSp], 1)).astype(bf16)
    return rotq, rotk


def _l0_masks():
    ws = 128
    out = np.zeros((128, 3 * 384), dtype=np.float32)
    for mi, w in enumerate([0, 1, (T // ws) - 1]):
        i = np.arange(128)[:, None]
        j = np.arange(384)[None, :]
        bq_k = (w - 1) * ws + j
        valid = (bq_k >= 0) & (bq_k < T)
        b_t = w * ws + i
        masked = (~valid) | (bq_k - ws > b_t) | (b_t > bq_k + ws)
        out[:, mi * 384:(mi + 1) * 384] = np.where(masked, NEGBF, 0.0)
    return out.astype(bf16)


def _perm128():
    P = np.zeros((128, 128), dtype=bf16)
    for h in range(2):
        for e in range(64):
            P[h * 64 + (e + 32) % 64, h * 64 + e] = 1.0
    return P


def _pack_cols(v, p=128):
    v = np.asarray(v, dtype=np.float32)
    return np.ascontiguousarray(v.reshape(-1, p).T)


def host_consts():
    rotq, rotk = _l0_rot_tables()
    return {
        "c_rotq": rotq,
        "c_rotk": rotk,
        "c_masks": _l0_masks(),
        "c_perm": _perm128(),
        "c_eye": np.eye(128, dtype=bf16),
        "c_ones": np.ones((1, 512), dtype=np.float32),
    }



ENGINES = ("sync", "gpsimd", "vector", "scalar", "tensor")


class Prog:
    def __init__(self, nc, ctx):
        self.nc = nc
        self.q = {e: [] for e in ENGINES}
        self.cnt = {e: 0 for e in ENGINES}
        self.sem = {e: ctx.enter_context(nc.semaphore(f"sem_{e}"))
                    for e in ENGINES}
        self.dsem = {e: ctx.enter_context(nc.semaphore(f"dma_{e}"))
                     for e in ("sync", "gpsimd", "vector", "scalar")}
        self.dcnt = {e: 0 for e in self.dsem}

    def ins(self, eng, fn):
        sem = self.sem[eng]
        self.cnt[eng] += 1
        tgt = self.cnt[eng]

        def run(eobj, fn=fn, sem=sem):
            fn(eobj).then_inc(sem, 1)
        self.q[eng].append(run)
        return (eng, False, tgt)

    def dma(self, eng, out, in_, **kw):
        dsem = self.dsem[eng]
        self.dcnt[eng] += 16
        tgt = self.dcnt[eng]

        def run(eobj, out=out, in_=in_, dsem=dsem, kw=kw):
            eobj.dma_start(out=out, in_=in_, **kw).then_inc(dsem, 16)
        self.q[eng].append(run)
        return (eng, True, tgt)

    def wait(self, eng, *tokens):
        for (peng, isdma, val) in tokens:
            if val <= 0:
                continue
            sem = self.dsem[peng] if isdma else self.sem[peng]
            self.q[eng].append(
                lambda eobj, sem=sem, val=val: eobj.wait_ge(sem, val))

    def make_fences(self, nc):
        if hasattr(self, "_fb"):
            return
        self._fb = {
            "vector": sbuf(nc, "fence_v", [1, 8]),
            "gpsimd": sbuf(nc, "fence_g", [1, 8]),
            "scalar": sbuf(nc, "fence_s", [1, 8]),
        }

    def fence(self, eng):
        fb = self._fb[eng]
        if eng == "scalar":
            return self.ins(eng, lambda e, fb=fb: e.copy(fb[0:1, 0:4],
                                                         fb[0:1, 4:8]))
        return self.ins(eng, lambda e, fb=fb: e.tensor_copy(fb[0:1, 0:4],
                                                            fb[0:1, 4:8]))

    def mark(self, eng):
        return (eng, False, self.cnt[eng])

    def mark_dma(self, eng):
        return (eng, True, self.dcnt[eng])

    def barrier(self):
        if hasattr(self, "_fb"):
            for e in ("vector", "scalar", "gpsimd"):
                self.fence(e)
        toks = ([self.mark(e) for e in ENGINES]
                + [self.mark_dma(e) for e in self.dsem])
        for e in ENGINES:
            self.wait(e, *[t for t in toks if t[1] or t[0] != e])

    def finalize(self, final_store_tokens):
        self.wait("sync", *final_store_tokens)
        with self.nc.Block() as block:
            for eng in ENGINES:
                qq = self.q[eng]
                getattr(block, eng)(lambda eobj, qq=qq: [f(eobj) for f in qq])


_CUR_CTX = [None]


def sbuf(nc, name, shape, dtype=F32):
    if _CUR_CTX[0] is not None:
        return _CUR_CTX[0].enter_context(nc.sbuf_tensor(name, shape, dtype))[:]
    return nc.alloc_sbuf_tensor(name, shape, dtype).ap()


def psum(nc, name, shape, dtype=F32):
    if _CUR_CTX[0] is not None:
        return _CUR_CTX[0].enter_context(nc.psum_tensor(name, shape, dtype))[:]
    return nc.alloc_psum_tensor(name, shape, dtype).ap()



class LnKit:

    def __init__(self, nc, tag, W):
        self.W = W
        self.ra = sbuf(nc, f"{tag}_ra", [2, W])
        self.rb = sbuf(nc, f"{tag}_rb", [2, W])
        self.sq = sbuf(nc, f"{tag}_sq", [128, W], BF16)
        self.xstat = sbuf(nc, f"{tag}_xst", [128, W], BF16)
        self.onesr = sbuf(nc, f"{tag}_ones", [128, 2], BF16)
        self.epsc = sbuf(nc, f"{tag}_eps", [128, 1])
        self.init_tok = None


def emit_ln(P, kit, rr_chunks, psS, psS2, psB1, psB2, ones_row_r,
            out_writer, extra_pe_dep=()):
    nch = len(rr_chunks)
    last_mm = None
    P.wait("tensor", *extra_pe_dep)
    SSC = 1.0 / 128.0
    P.wait("scalar", P.mark("vector"))
    for c in range(nch):
        if last_mm is not None:
            P.wait("scalar", last_mm)
        P.ins("scalar", lambda s, c=c: s.activation(
            kit.xstat, rr_chunks[c], AF.Identity, bias=0.0, scale=SSC))
        tcp = P.fence("scalar")
        P.wait("vector", tcp)
        P.ins("vector", lambda v: v.tensor_tensor(kit.sq, kit.xstat,
                                                  kit.xstat, ALU.mult))
        tsq = P.fence("vector")
        P.wait("tensor", tcp, tsq)
        P.ins("tensor", lambda t, c=c: t.matmul(
            psS, kit.onesr, kit.xstat, start=(c == 0), stop=(c == nch - 1)))
        P.ins("tensor", lambda t, c=c: t.matmul(
            psS2, kit.onesr, kit.sq, start=(c == 0), stop=(c == nch - 1)))
        last_mm = P.mark("tensor")
    tok_mm = P.mark("tensor")
    inv_n = 1.0 / (128 * nch)
    inv_n_m = inv_n / SSC
    inv_n_v = inv_n / (SSC * SSC)
    ra0, rb0 = kit.ra[0:1, :], kit.rb[0:1, :]
    P.wait("vector", tok_mm)
    P.ins("vector", lambda v: v.tensor_scalar_mul(ra0, psS[0:1, :], inv_n_m))
    P.ins("vector", lambda v: v.tensor_scalar_mul(rb0, psS2[0:1, :], inv_n_v))
    P.ins("vector", lambda v: v.tensor_tensor(psS[0:1, :], ra0, ra0, ALU.mult))
    tv = P.ins("vector", lambda v: v.tensor_tensor(
        rb0, rb0, psS[0:1, :], ALU.subtract))
    P.wait("scalar", tv)
    tsd = P.ins("scalar", lambda s: s.activation(
        rb0, rb0, AF.Sqrt, bias=kit.epsc[0:1, 0:1], scale=1.0))
    P.wait("vector", tsd)
    P.ins("vector", lambda v: v.reciprocal(rb0, rb0))
    P.ins("vector", lambda v: v.tensor_tensor(ra0, ra0, rb0, ALU.mult))
    P.ins("vector", lambda v: v.tensor_scalar_mul(ra0, ra0, -1.0))
    tn = P.fence("vector")
    P.wait("tensor", tn)
    P.ins("tensor", lambda t: t.matmul(
        psB1, ones_row_r, kit.rb[:], start=True, stop=True))
    tb = P.ins("tensor", lambda t: t.matmul(
        psB2, ones_row_r, kit.ra[:], start=True, stop=True))
    return tb



DEBUG_P1 = False


def build_p1_program():
    nc = bass.Bass("TRN2", target_bir_lowering=False, debug=False)
    io = {}
    if DEBUG_P1:
        for nm, shp, dt in [("dbg_qk", [512, T], BF16), ("dbg_v", [T, 512], BF16),
                            ("dbg_ao", [512, T], BF16), ("dbg_xh", [D, 512], F32),
                            ("dbg_bw", [128, 8], F32), ("dbg_pb", [128, 512], F32),
                            ("dbg_rstd", [2, 512], F32), ("dbg_S", [2, 512], F32)]:
            io[nm] = nc.dram_tensor(nm, shp, dt, kind="ExternalOutput").ap()
    io["xT"] = nc.dram_tensor("xT", [D, T], F32, kind="ExternalInput").ap()
    io["wqkvT"] = nc.dram_tensor("wqkvT", [D, D], F32, kind="ExternalInput").ap()
    io["woutT"] = nc.dram_tensor("woutT", [512, D], F32, kind="ExternalInput").ap()
    io["lngb"] = nc.dram_tensor("lngb", [128, 16], F32, kind="ExternalInput").ap()
    for k, v in host_consts().items():
        dt = BF16 if v.dtype == bf16 else F32
        io[k] = nc.dram_tensor(k, list(v.shape), dt, kind="ExternalInput").ap()
    io["x1T"] = nc.dram_tensor("x1T", [D, T], F32, kind="ExternalOutput").ap()

    with contextlib.ExitStack() as ctx:
        P = Prog(nc, ctx)
        emit_p1(nc, P, io)
        P.finalize([P.mark_dma("sync")])
    return nc


def emit_p1(nc, P, io, tag="p1"):
    W = 512
    NT = T // W
    NW = T // 128

    wq = [sbuf(nc, f"{tag}_wq{k}", [128, 1024], F32R) for k in range(8)]
    arena = sbuf(nc, f"{tag}_arena", [128, 12032])
    _ab = arena.bitcast(BF16)
    wo = [_ab[:, 16384 + k * 1024:16384 + (k + 1) * 1024] for k in range(4)]
    wraw = arena[:, 4096:5120]
    lngb = sbuf(nc, f"{tag}_lngb", [128, 16])
    lnbr = sbuf(nc, f"{tag}_lnbr", [128, 18], F32R)
    rotq = sbuf(nc, f"{tag}_rotq", [128, 256], BF16)
    rotk = sbuf(nc, f"{tag}_rotk", [128, 768], BF16)
    masks = sbuf(nc, f"{tag}_masks", [128, 1152], BF16)
    perm = sbuf(nc, f"{tag}_perm", [128, 128], BF16)
    eye = sbuf(nc, f"{tag}_eye", [128, 128], BF16)
    ones1r = sbuf(nc, f"{tag}_ones1r", [2, 128])
    bw_cols = sbuf(nc, f"{tag}_bwcols", [128, 8])

    xtile = [arena[:, c * W:(c + 1) * W] for c in range(8)]
    xh8 = sbuf(nc, f"{tag}_xh8", [128, 4096], F32R)
    xhat = [xh8[:, c * W:(c + 1) * W] for c in range(8)]
    qkT = [sbuf(nc, f"{tag}_qk{c}", [128, T], BF16) for c in range(4)]
    pnT = [arena.bitcast(BF16)[:, c * T:(c + 1) * T] for c in range(4)]
    vtok = [sbuf(nc, f"{tag}_vt{j}", [128, 512], BF16) for j in range(NW)]
    aoT = [sbuf(nc, f"{tag}_ao{c}", [128, T], BF16) for c in range(4)]

    _sc = 20480
    bk = [_ab[:, _sc:_sc + 384], _ab[:, _sc + 384:_sc + 768]]
    bkb = [_ab[:, _sc + 768:_sc + 1152], _ab[:, _sc + 1152:_sc + 1536]]
    bq0 = [_ab[:, _sc + 1536:_sc + 1664], _ab[:, _sc + 1664:_sc + 1792]]
    bq1 = [_ab[:, _sc + 1792:_sc + 1920], _ab[:, _sc + 1920:_sc + 2048]]
    ex = [_ab[:, _sc + 2048:_sc + 2816], _ab[:, _sc + 2816:_sc + 3584]]
    aTs = ex
    nmax = [sbuf(nc, f"{tag}_nm{i}", [128, 2]) for i in range(2)]
    rsum = [sbuf(nc, f"{tag}_rs{i}", [128, 2]) for i in range(2)]
    rcp = [sbuf(nc, f"{tag}_rc{i}", [128, 2]) for i in range(2)]

    kit = LnKit(nc, f"{tag}_ln", W)
    P.make_fences(nc)
    P.ins("vector", lambda v: v.memset(P._fb["vector"], 0.0))
    P.ins("gpsimd", lambda g: g.memset(P._fb["gpsimd"], 0.0))
    P.ins("vector", lambda v: v.memset(P._fb["scalar"], 0.0))

    pb = [psum(nc, f"{tag}_pb{i}", [128, 512]) for i in range(8)]
    psS = pb[0][0:2, :]
    psS2 = pb[1][0:2, :]
    psB1 = pb[2]
    psB2 = pb[3]
    ps_qk = pb[4]
    ps_v = pb[5]
    ps_s = [pb[0][:, 0:384], pb[1][:, 0:384]]
    ps_s2 = [pb[2][:, 0:384], pb[3][:, 0:384]]
    ps_t = [pb[4].bitcast(BF16)[:, 0:768], pb[5].bitcast(BF16)[:, 0:768]]
    ps_o = [pb[6][:, 0:128], pb[7][:, 0:128]]
    ps_c = pb[7]

    P.dma("sync", lngb[:], io["lngb"])
    for nm, dst in [("c_rotq", rotq), ("c_rotk", rotk), ("c_masks", masks),
                    ("c_perm", perm), ("c_eye", eye)]:
        P.dma("sync", dst[:], io[nm])
    tk_lngb = P.mark_dma("sync")
    P.wait("vector", tk_lngb)
    P.ins("vector", lambda v: v.memset(lnbr[:, 16:18].bitcast(F32), 0.0))
    P.ins("vector", lambda v: v.memset(kit.epsc, 1e-5))
    P.ins("vector", lambda v: v.memset(kit.onesr, 1.0))
    P.ins("vector", lambda v: v.memset(ones1r[:], 1.0))
    P.ins("vector", lambda v: v.memset(kit.ra, 0.0))
    P.ins("vector", lambda v: v.memset(kit.rb, 0.0))
    tk_lnbr = P.ins("vector", lambda v: v.tensor_copy(lnbr[:, 0:16],
                                                      lngb[:]))

    rhs_pad = arena[:, 5120:5248]
    P.ins("vector", lambda v: v.memset(rhs_pad, 0.0))
    t_prev_use = None
    for k in range(8):
        tl = P.dma("sync", wraw[:], io["wqkvT"][k * 128:(k + 1) * 128, :])
        P.wait("vector", tl, t_prev_use or tl)
        tgw = P.ins("vector", lambda v, k=k: v.tensor_scalar_mul(
            wq[k][:], wraw[:], lngb[:, k:k + 1]))
        P.ins("vector", lambda v, k=k: v.tensor_copy(
            rhs_pad[:, 0:1], lnbr[:, 8 + k:9 + k]))
        trr = P.fence("vector")
        P.wait("tensor", trr, tk_lnbr)
        for oc in range(8):
            P.ins("tensor", lambda t, k=k, oc=oc: t.matmul(
                pb[oc][:, 0:128],
                wraw[:, oc * 128:(oc + 1) * 128], rhs_pad,
                start=(k == 0), stop=(k == 7)))
        t_prev_use = P.mark("tensor")
        if k < 7:
            P.wait("sync", trr, t_prev_use)
    P.wait("vector", t_prev_use)
    for oc in range(8):
        P.ins("vector", lambda v, oc=oc: v.tensor_copy(
            bw_cols[:, oc:oc + 1], pb[oc][:, 0:1]))
    t_bw = P.fence("vector")
    P.barrier()

    for tt in range(NT):
        t0 = tt * W
        for c in range(8):
            P.dma("sync", xtile[c][:], io["xT"][c * 128:(c + 1) * 128,
                                                t0:t0 + W])
        ld_all = P.mark_dma("sync")
        P.wait("vector", ld_all)
        tc = []
        for c in range(8):
            tc.append(P.ins("vector", lambda v, c=c: v.tensor_copy(
                xhat[c][:], xtile[c][:])))
        last_mm = None
        for c in range(8):
            if last_mm is not None:
                P.wait("vector", last_mm)
            P.ins("vector", lambda v, c=c: v.tensor_copy(
                kit.xstat, xtile[c][:]))
            P.ins("vector", lambda v: v.tensor_tensor(
                kit.sq, kit.xstat, kit.xstat, ALU.mult))
            tf = P.fence("vector")
            P.wait("tensor", tf)
            P.ins("tensor", lambda t, c=c: t.matmul(
                psS, kit.onesr, kit.xstat, start=(c == 0), stop=(c == 7)))
            P.ins("tensor", lambda t, c=c: t.matmul(
                psS2, kit.onesr, kit.sq, start=(c == 0), stop=(c == 7)))
            last_mm = P.mark("tensor")
        inv_n = 1.0 / D
        ra0, rb0 = kit.ra[0:1, :], kit.rb[0:1, :]
        P.wait("vector", last_mm)
        if io.get("dbg_S") is not None and tt == 0:
            t1_ = P.ins("vector", lambda v: v.tensor_copy(
                arena[0:1, 6144:6656], psS[0:1, :]))
            t2_ = P.ins("vector", lambda v: v.tensor_copy(
                arena[0:1, 6656:7168], psS2[0:1, :]))
            P.wait("sync", t2_)
            P.dma("sync", io["dbg_S"][0:1, :], arena[0:1, 6144:6656])
            P.dma("sync", io["dbg_S"][1:2, :], arena[0:1, 6656:7168])
        P.ins("vector", lambda v: v.tensor_scalar_mul(ra0, psS[0:1, :], inv_n))
        P.ins("vector", lambda v: v.tensor_scalar_mul(rb0, psS2[0:1, :], inv_n))
        P.ins("vector", lambda v: v.tensor_tensor(psS[0:1, :], ra0, ra0,
                                                  ALU.mult))
        tv = P.ins("vector", lambda v: v.tensor_tensor(
            rb0, rb0, psS[0:1, :], ALU.subtract))
        P.wait("scalar", tv)
        tsd = P.ins("scalar", lambda s: s.activation(
            rb0, rb0, AF.Sqrt, bias=kit.epsc[0:1, 0:1], scale=1.0))
        P.wait("vector", tsd)
        P.ins("vector", lambda v: v.reciprocal(rb0, rb0))
        P.ins("vector", lambda v: v.tensor_tensor(ra0, ra0, rb0, ALU.mult))
        P.ins("vector", lambda v: v.tensor_scalar_mul(ra0, ra0, -1.0))
        tn = P.fence("vector")
        P.wait("tensor", tn)
        P.ins("tensor", lambda t: t.matmul(
            psB1, ones1r[:], kit.rb[:], start=True, stop=True))
        tb = P.ins("tensor", lambda t: t.matmul(
            psB2, ones1r[:], kit.ra[:], start=True, stop=True))
        P.wait("vector", tb)
        th = []
        for c in range(8):
            P.ins("vector", lambda v, c=c: v.tensor_tensor(
                xhat[c][:], xtile[c][:], psB1, ALU.mult))
            th.append(P.ins("vector", lambda v, c=c: v.tensor_tensor(
                xhat[c][:], xhat[c][:], psB2, ALU.add)))
        th.append(P.fence("vector"))
        P.wait("tensor", th[-1], t_bw)
        if io.get("dbg_xh") is not None and tt == 0:
            P.wait("sync", th[-1], t_bw)
            for c in range(8):
                P.dma("sync", io["dbg_xh"][c * 128:(c + 1) * 128, :],
                      xhat[c].bitcast(F32))
            P.dma("sync", io["dbg_bw"], bw_cols[:])
            tpb = P.ins("vector", lambda v: v.tensor_copy(
                arena[:, 5632:6144], psB1))
            P.wait("sync", tpb)
            P.dma("sync", io["dbg_pb"], arena[:, 5632:6144])
            P.dma("sync", io["dbg_rstd"], kit.rb[:])
        for oc in range(4):
            for k in range(8):
                P.ins("tensor", lambda t, oc=oc, k=k: t.matmul(
                    ps_qk, wq[k][:, oc * 128:(oc + 1) * 128], xhat[k][:],
                    start=(k == 0), stop=(k == 7)))
            tm = P.mark("tensor")
            P.wait("scalar", tm, t_bw)
            te = P.ins("scalar", lambda s, oc=oc, t0=t0: s.activation(
                qkT[oc][:, t0:t0 + W], ps_qk, AF.Identity,
                bias=bw_cols[:, oc:oc + 1], scale=1.0))
            P.wait("tensor", te)
        for sj in range(4):
            wnd = tt * 4 + sj
            for k in range(8):
                P.ins("tensor", lambda t, sj=sj, k=k: t.matmul(
                    ps_v, xhat[k][:, sj * 128:(sj + 1) * 128],
                    wq[k][:, 512:1024], start=(k == 0), stop=(k == 7)))
            tm = P.mark("tensor")
            P.wait("scalar", tm)
            te = P.ins("scalar", lambda s, wnd=wnd: s.copy(vtok[wnd][:], ps_v))
            P.wait("tensor", te)
        P.barrier()

    wstage = arena[:, 5120:5632]
    for k in range(4):
        for hh in range(2):
            tl = P.dma("sync", wstage,
                       io["woutT"][k * 128:(k + 1) * 128,
                                   hh * 512:(hh + 1) * 512])
            P.wait("vector", tl)
            tw = P.ins("vector", lambda v, k=k, hh=hh: v.tensor_copy(
                wo[k][:, hh * 512:(hh + 1) * 512], wstage))
            P.wait("sync", tw)
    P.barrier()

    for c in range(4):
        for j8 in range(8):
            j0 = j8 * 512
            tm = P.ins("tensor", lambda t, c=c, j0=j0: t.matmul(
                ps_qk, perm[:], qkT[c][:, j0:j0 + 512], start=True, stop=True))
            P.wait("scalar", tm)
            te = P.ins("scalar", lambda s, c=c, j0=j0: s.copy(
                pnT[c][:, j0:j0 + 512], ps_qk))
            P.wait("tensor", te)
    P.barrier()

    P.ins("vector", lambda v: v.memset(bq0[0][:], 0.0))
    P.ins("vector", lambda v: v.memset(bq0[1][:], 0.0))
    P.ins("vector", lambda v: v.memset(bq1[0][:], 0.0))
    tz = P.ins("vector", lambda v: v.memset(bq1[1][:], 0.0))
    P.barrier()

    iters = [(hp, w) for hp in range(4) for w in range(NW)]
    last_pe_scores = [None, None]
    last_act_exp = [None, None]
    last_dve_scale = [None, None]
    last_pe_tr = [None, None]
    last_act_evat = [None, None]
    last_pe_av = [None, None]
    last_act_evo = [None, None]
    pss = [ps_s, ps_s2]

    for it, (hp, w) in enumerate(iters):
        p = it % 2
        j0 = (w - 1) * 128
        cs = 128 if w == 0 else 0
        ce = 256 if w == NW - 1 else 384
        mi = 0 if w == 0 else (2 if w == NW - 1 else 1)
        edge = (w == 0 or w == NW - 1)
        a0 = j0 + cs
        ln = ce - cs

        if last_pe_scores[p] is not None:
            P.wait("vector", last_pe_scores[p])
        if edge:
            P.ins("vector", lambda v, p=p: v.memset(bk[p][:], 0.0))
        P.ins("vector", lambda v, p=p, hp=hp, a0=a0, cs=cs, ln=ln:
              v.tensor_tensor(bk[p][:, cs:cs + ln], qkT[hp][:, a0:a0 + ln],
                              rotk[:, cs:cs + ln], ALU.mult))
        P.ins("vector", lambda v, p=p, hp=hp, a0=a0, cs=cs, ln=ln:
              v.tensor_tensor(bkb[p][:, cs:cs + ln], pnT[hp][:, a0:a0 + ln],
                              rotk[:, 384 + cs:384 + cs + ln], ALU.mult))
        tbk = P.ins("vector", lambda v, p=p, cs=cs, ln=ln: v.tensor_tensor(
            bk[p][:, cs:cs + ln], bk[p][:, cs:cs + ln],
            bkb[p][:, cs:cs + ln], ALU.add))
        c0 = w * 128
        P.ins("vector", lambda v, p=p, hp=hp, c0=c0: v.tensor_tensor(
            bq0[p][0:64, :], qkT[hp][0:64, c0:c0 + 128], rotq[0:64, 0:128],
            ALU.mult))
        P.ins("vector", lambda v, p=p, hp=hp, c0=c0: v.tensor_tensor(
            bkb[p][0:64, 0:128], pnT[hp][0:64, c0:c0 + 128],
            rotq[0:64, 128:256], ALU.mult))
        tq0 = P.ins("vector", lambda v, p=p: v.tensor_tensor(
            bq0[p][0:64, :], bq0[p][0:64, :], bkb[p][0:64, 0:128], ALU.add))
        P.ins("vector", lambda v, p=p, hp=hp, c0=c0: v.tensor_tensor(
            bq1[p][64:128, :], qkT[hp][64:128, c0:c0 + 128],
            rotq[64:128, 0:128], ALU.mult))
        P.ins("vector", lambda v, p=p, hp=hp, c0=c0: v.tensor_tensor(
            bkb[p][64:128, 0:128], pnT[hp][64:128, c0:c0 + 128],
            rotq[64:128, 128:256], ALU.mult))
        tq1 = P.ins("vector", lambda v, p=p: v.tensor_tensor(
            bq1[p][64:128, :], bq1[p][64:128, :], bkb[p][64:128, 0:128],
            ALU.add))
        tq1 = P.fence("vector")

        P.wait("tensor", tbk, tq0, tq1)
        if last_act_exp[p] is not None:
            P.wait("tensor", last_act_exp[p])
        for h, bqt in ((0, bq0), (1, bq1)):
            P.ins("tensor", lambda t, p=p, h=h, bqt=bqt: t.matmul(
                pss[p][h], bqt[p][:], bk[p][:], start=True, stop=False))
            P.ins("tensor", lambda t, p=p, h=h, mi=mi: t.matmul(
                pss[p][h], eye[:], masks[:, mi * 384:(mi + 1) * 384],
                start=False, stop=True))
        last_pe_scores[p] = P.mark("tensor")

        P.wait("vector", last_pe_scores[p])
        tm0 = P.ins("vector", lambda v, p=p: v.reduce_max(
            nmax[p][:, 0:1], pss[p][0], AX.X, negate=True))
        tm1 = P.ins("vector", lambda v, p=p: v.reduce_max(
            nmax[p][:, 1:2], pss[p][1], AX.X, negate=True))
        tm1 = P.fence("vector")
        P.wait("scalar", tm0, tm1)
        if last_pe_av[p] is not None:
            P.wait("scalar", last_pe_av[p])
        te0 = P.ins("scalar", lambda s, p=p: s.activation(
            ex[p][:, 0:384], pss[p][0], AF.Exp, bias=nmax[p][:, 0:1],
            scale=1.0, accum_out=rsum[p][:, 0:1]))
        te1 = P.ins("scalar", lambda s, p=p: s.activation(
            ex[p][:, 384:768], pss[p][1], AF.Exp, bias=nmax[p][:, 1:2],
            scale=1.0, accum_out=rsum[p][:, 1:2]))
        tfx = P.fence("scalar")
        last_act_exp[p] = P.mark("scalar")
        P.wait("vector", tfx)
        trc = P.ins("vector", lambda v, p=p: v.reciprocal(rcp[p][:], rsum[p][:]))
        P.wait("vector", trc)
        if last_pe_tr[p] is not None:
            P.wait("vector", last_pe_tr[p])
        P.ins("vector", lambda v, p=p: v.tensor_scalar_mul(
            ex[p][:, 0:384], ex[p][:, 0:384], rcp[p][:, 0:1]))
        tat = P.ins("vector", lambda v, p=p: v.tensor_scalar_mul(
            ex[p][:, 384:768], ex[p][:, 384:768], rcp[p][:, 1:2]))
        tat = P.fence("vector")
        last_dve_scale[p] = tat

        P.wait("tensor", tat)
        if last_act_evat[p] is not None:
            P.wait("tensor", last_act_evat[p])
        jcs = [jc for jc in range(3) if cs <= jc * 128 < ce]
        for h in (0, 1):
            for jc in jcs:
                P.ins("tensor", lambda t, p=p, h=h, jc=jc: t.transpose(
                    ps_t[p][:, h * 384 + jc * 128: h * 384 + jc * 128 + 128],
                    ex[p][:, h * 384 + jc * 128: h * 384 + jc * 128 + 128],
                    eye[:]))
        last_pe_tr[p] = P.mark("tensor")
        P.wait("scalar", last_pe_tr[p])
        if last_pe_av[p] is not None:
            P.wait("scalar", last_pe_av[p])
        tev = P.ins("scalar", lambda s, p=p: s.copy(aTs[p][:], ps_t[p][:]))
        tev = P.fence("scalar")
        last_act_evat[p] = tev

        P.wait("tensor", tev)
        if last_act_evo[p] is not None:
            P.wait("tensor", last_act_evo[p])
        for h in (0, 1):
            for ji, jc in enumerate(jcs):
                P.ins("tensor",
                      lambda t, p=p, h=h, jc=jc, w=w, hp=hp, ji=ji,
                      nj=len(jcs): t.matmul(
                          ps_o[p][h * 64:(h + 1) * 64, :],
                          vtok[w - 1 + jc][:, hp * 128 + h * 64:
                                           hp * 128 + h * 64 + 64],
                          aTs[p][:, h * 384 + jc * 128:h * 384 + jc * 128 + 128],
                          start=(ji == 0), stop=(ji == nj - 1)))
        last_pe_av[p] = P.mark("tensor")
        P.wait("scalar", last_pe_av[p])
        teo = P.ins("scalar", lambda s, hp=hp, w=w, p=p: s.activation(
            aoT[hp][:, w * 128:(w + 1) * 128], ps_o[p], AF.Identity,
            bias=bw_cols[:, 4 + hp:5 + hp], scale=1.0))
        last_act_evo[p] = teo

    P.barrier()

    if io.get("dbg_qk") is not None:
        for c in range(4):
            P.dma("sync", io["dbg_qk"][c * 128:(c + 1) * 128, :], qkT[c])
            P.dma("sync", io["dbg_ao"][c * 128:(c + 1) * 128, :], aoT[c])
        for j in range(NW):
            P.dma("sync", io["dbg_v"][j * 128:(j + 1) * 128, :], vtok[j][:])
        P.barrier()

    for tt in range(NT):
        t0 = tt * W
        for c in range(8):
            P.dma("sync", xtile[c][:], io["xT"][c * 128:(c + 1) * 128,
                                                t0:t0 + W])
        ld_all = P.mark_dma("sync")
        P.wait("vector", ld_all)
        for oc in range(8):
            for k in range(4):
                P.ins("tensor", lambda t, oc=oc, k=k, t0=t0: t.matmul(
                    ps_c, wo[k][:, oc * 128:(oc + 1) * 128],
                    aoT[k][:, t0:t0 + W], start=(k == 0), stop=(k == 3)))
            tm = P.mark("tensor")
            P.wait("vector", tm)
            ta = P.ins("vector", lambda v, oc=oc: v.tensor_tensor(
                xtile[oc][:], ps_c, xtile[oc][:], ALU.add))
            tc = P.fence("vector")
            P.ins("vector", lambda v, oc=oc: v.tensor_copy(
                xhat[oc][:], xtile[oc][:]))
            tc = P.fence("vector")
            P.wait("tensor", tc)
        tb = emit_ln(P, kit, [xh[:] for xh in xhat], psS, psS2, psB1, psB2,
                     ones1r[:], None)
        P.wait("vector", tb)
        st_toks = []
        for c in range(8):
            P.ins("vector", lambda v, c=c: v.tensor_tensor(
                xtile[c][:], xhat[c][:], psB1, ALU.mult))
            tx = P.ins("vector", lambda v, c=c: v.tensor_tensor(
                xtile[c][:], xtile[c][:], psB2, ALU.add))
            tx = P.fence("vector")
            P.wait("sync", tx)
            st_toks.append(P.dma("sync", io["x1T"][c * 128:(c + 1) * 128,
                                                   t0:t0 + W], xtile[c][:]))
        P.wait("vector", *st_toks)
        P.barrier()






DEBUG_P2 = False
DEBUG_P2W = False


def build_p2_program():
    nc = bass.Bass("TRN2", target_bir_lowering=False, debug=False)
    io = {}
    if DEBUG_P2W:
        io["dbg_w1q"] = nc.dram_tensor("dbg_w1q", [128, 4096], BF16,
                                       kind="ExternalOutput").ap()
        io["dbg_w2q"] = nc.dram_tensor("dbg_w2q", [128, 1024], BF16,
                                       kind="ExternalOutput").ap()
        io["dbg_w1qb"] = nc.dram_tensor("dbg_w1qb", [128, 4096], BF16,
                                        kind="ExternalOutput").ap()
        io["dbg_thr"] = nc.dram_tensor("dbg_thr", [128, 4], F32,
                                       kind="ExternalOutput").ap()
        io["dbg_hall"] = nc.dram_tensor("dbg_hall", [4096, 512], BF16,
                                        kind="ExternalOutput").ap()
        io["dbg_x1b"] = nc.dram_tensor("dbg_x1b", [1024, 512], BF16,
                                       kind="ExternalOutput").ap()
        io["dbg_gs"] = nc.dram_tensor("dbg_gs", [128, 4], F32,
                                      kind="ExternalOutput").ap()
    if DEBUG_P2:
        io["dbg_thr"] = nc.dram_tensor("dbg_thr", [128, 4], F32,
                                       kind="ExternalOutput").ap()
        io["dbg_w1q"] = nc.dram_tensor("dbg_w1q", [128, 4096], BF16,
                                       kind="ExternalOutput").ap()
        io["dbg_h"] = nc.dram_tensor("dbg_h", [128, 512], BF16,
                                     kind="ExternalOutput").ap()
        io["dbg_ps"] = nc.dram_tensor("dbg_ps", [128, 512], F32,
                                      kind="ExternalOutput").ap()
        io["dbg_g"] = nc.dram_tensor("dbg_g", [128, 2], F32,
                                     kind="ExternalOutput").ap()
        io["dbg_hall"] = nc.dram_tensor("dbg_hall", [4096, 512], BF16,
                                        kind="ExternalOutput").ap()
        io["dbg_r"] = nc.dram_tensor("dbg_r", [1024, 512], F32,
                                     kind="ExternalOutput").ap()
        io["dbg_w2q"] = nc.dram_tensor("dbg_w2q", [128, 1024], BF16,
                                       kind="ExternalOutput").ap()
        io["dbg_pso"] = nc.dram_tensor("dbg_pso", [128, 512], F32,
                                       kind="ExternalOutput").ap()
        io["dbg_x1b"] = nc.dram_tensor("dbg_x1b", [128, 512], BF16,
                                       kind="ExternalOutput").ap()
    io["x1T"] = nc.dram_tensor("x1T", [D, T], F32, kind="ExternalInput").ap()
    io["w1T"] = nc.dram_tensor("w1T", [D, 4096], F32, kind="ExternalInput").ap()
    io["w2T"] = nc.dram_tensor("w2T", [4096, D], F32, kind="ExternalInput").ap()
    io["b1c"] = nc.dram_tensor("b1c", [128, 32], F32, kind="ExternalInput").ap()
    io["b2c"] = nc.dram_tensor("b2c", [128, 8], F32, kind="ExternalInput").ap()
    io["x2T"] = nc.dram_tensor("x2T", [D, T], F32, kind="ExternalOutput").ap()
    with contextlib.ExitStack() as ctx:
        P = Prog(nc, ctx)
        emit_p2(nc, P, io)
        P.finalize([P.mark_dma("sync")])
    return nc


def emit_p2(nc, P, io, tag="p2"):
    W = 512
    NT = T // W
    DF = 4096
    g_dram = nc.dram_tensor(f"{tag}_gd", [2, 128], F32, kind="Internal").ap()

    w1q = [sbuf(nc, f"{tag}_w1q{k}", [128, DF], BF16) for k in range(8)]
    w2q = [sbuf(nc, f"{tag}_w2q{f}", [128, D], BF16) for f in range(32)]
    arena = sbuf(nc, f"{tag}_arena", [128, 4096])
    wstage = arena[:, 0:2048]
    nscr = arena[:, 2048:4096]
    hb_ = sbuf(nc, f"{tag}_hb", [128, 32 * W], BF16)
    hbf = [hb_[:, f * W:(f + 1) * W] for f in range(32)]
    x1t = [sbuf(nc, f"{tag}_x1_{c}", [128, W]) for c in range(8)]
    x1b = [sbuf(nc, f"{tag}_x1b{c}", [128, W], BF16) for c in range(8)]
    b1c = sbuf(nc, f"{tag}_b1c", [128, 32])
    b2c = sbuf(nc, f"{tag}_b2c", [128, 8])
    gcols = sbuf(nc, f"{tag}_gcols", [128, 2])
    gtmp = sbuf(nc, f"{tag}_gtmp", [128, 2])
    gacc = sbuf(nc, f"{tag}_gacc", [128, 48])
    gpad = hb_.bitcast(F32)[:, 0:512]
    thr4 = sbuf(nc, f"{tag}_thr4", [128, 4])
    trow = hb_.bitcast(F32)[0:2, 512:1024]
    ones1r = sbuf(nc, f"{tag}_ones1r", [2, 128])
    onescol = sbuf(nc, f"{tag}_onesc", [128, 2])
    kit = LnKit(nc, f"{tag}_ln", W)
    dbgbuf = nscr[:, 0:512]
    P.make_fences(nc)

    pb = [psum(nc, f"{tag}_pb{i}", [128, 512]) for i in range(8)]
    psS = pb[0][0:2, :]
    psS2 = pb[1][0:2, :]
    psB1 = pb[2]
    psB2 = pb[3]
    ps_h = [pb[6], pb[7]]
    ps_o = [pb[4], pb[4]]

    tb1 = P.dma("sync", b1c[:], io["b1c"])
    tb2 = P.dma("sync", b2c[:], io["b2c"])
    P.ins("vector", lambda v: v.memset(P._fb["vector"], 0.0))
    P.ins("vector", lambda v: v.memset(P._fb["scalar"], 0.0))
    P.ins("vector", lambda v: v.memset(kit.epsc, 1e-5))
    P.ins("vector", lambda v: v.memset(kit.onesr, 1.0))
    P.ins("vector", lambda v: v.memset(ones1r[:], 1.0))
    P.ins("vector", lambda v: v.memset(onescol[:], 1.0))
    P.ins("vector", lambda v: v.memset(kit.ra, 0.0))
    P.ins("vector", lambda v: v.memset(kit.rb, 0.0))
    P.ins("vector", lambda v: v.memset(gcols[:], 0.0))
    P.ins("vector", lambda v: v.memset(gpad[:], 0.0))
    P.ins("vector", lambda v: v.memset(trow[:], 0.0))

    gpi = [0]
    wping = [wstage, nscr]

    def gamma_stream(jobs):
        cons = {0: None, 1: None}
        for i, (src, wdt) in enumerate(jobs):
            pp = i % 2
            if cons[pp] is not None:
                P.wait("sync", cons[pp])
            P.dma("sync", wping[pp][:, 0:wdt], src)
            tl = P.mark_dma("sync")
            P.wait("vector", tl)
            gi = gpi[0]
            cons[pp] = P.ins("vector", lambda v, wdt=wdt, gi=gi, pp=pp:
                             v.reduce_sum(gacc[:, gi:gi + 1],
                                          wping[pp][:, 0:wdt], AX.X,
                                          apply_absolute_value=True))
            gpi[0] += 1

    def wjobs(dram, nrows, ncols):
        out = []
        for k in range(nrows // 128):
            for h0 in range(0, ncols, 2048):
                wdt = min(2048, ncols - h0)
                out.append((dram[k * 128:(k + 1) * 128, h0:h0 + wdt], wdt))
        return out
    gamma_stream(wjobs(io["w1T"], D, DF) + wjobs(io["w2T"], DF, D))
    P.wait("vector", P.mark("vector"))
    tg1 = P.ins("vector", lambda v: v.reduce_sum(gpad[:, 0:1], gacc[:, 0:16],
                                                 AX.X))
    P.wait("vector", tg1)
    tg2 = P.ins("vector", lambda v: v.reduce_sum(gpad[:, 1:2], gacc[:, 16:48],
                                                 AX.X))
    P.wait("vector", tg2)
    tf = P.fence("vector")
    P.wait("tensor", tf)
    tg = P.ins("tensor", lambda t: t.matmul(pb[5][0:2, 0:2], onescol[:],
                                            gpad[:, 0:2], start=True,
                                            stop=True))
    P.wait("vector", tg)
    nw1 = float(D * DF)
    nw2 = float(DF * D)
    t0_ = P.ins("vector", lambda v: v.tensor_scalar(
        trow[0:1, 0:1], pb[5][0:1, 0:1], 0.5 / nw1, 0.5e-5, ALU.mult,
        ALU.add))
    P.wait("vector", t0_)
    t1_ = P.ins("vector", lambda v: v.tensor_scalar(
        trow[0:1, 2:3], pb[5][0:1, 1:2], 0.5 / nw2, 0.5e-5, ALU.mult,
        ALU.add))
    P.wait("vector", t1_)
    tr1 = P.ins("vector", lambda v: v.tensor_scalar_mul(
        trow[0:1, 1:2], trow[0:1, 0:1], -1.0))
    P.wait("vector", tr1)
    tr2 = P.ins("vector", lambda v: v.tensor_scalar_mul(
        trow[0:1, 3:4], trow[0:1, 2:3], -1.0))
    P.wait("vector", tr2)
    tf = P.fence("vector")
    P.wait("tensor", tf)
    tb = P.ins("tensor", lambda t: t.matmul(psB1, ones1r[:], trow[:],
                                            start=True, stop=True))
    P.wait("vector", tb)
    if io.get("dbg_gs") is not None:
        tdg = P.ins("vector", lambda v: v.tensor_copy(gcols[:], gpad[:, 0:2]))
        tps = P.ins("vector", lambda v: v.tensor_copy(gtmp[0:1, 0:2],
                                                      pb[5][0:1, 0:2]))
        P.wait("sync", tdg, tps)
        P.dma("sync", io["dbg_gs"][:, 0:2], gcols[:])
        P.dma("sync", io["dbg_gs"][0:1, 2:4], gtmp[0:1, 0:2])
    tthr = P.ins("vector", lambda v: v.tensor_copy(thr4[:], psB1[:, 0:4]))

    def quant_stream(jobs):
        cons = {0: None, 1: None}
        for i, (src, wdt, tcol, dst) in enumerate(jobs):
            pp = i % 2
            if cons[pp] is not None:
                P.wait("sync", cons[pp])
            P.dma("sync", wping[pp][:, 0:wdt], src)
            tl = P.mark_dma("sync")
            P.wait("vector", tl, tthr)
            P.ins("vector", lambda v, wdt=wdt, tcol=tcol, pp=pp, dst=dst:
                  v.tensor_single_scalar(
                      dst, wping[pp][:, 0:wdt],
                      thr4[:, tcol + 1:tcol + 2], ALU.is_le))
            cons[pp] = P.ins("vector", lambda v, wdt=wdt, tcol=tcol,
                             dst=dst, pp=pp: v.scalar_tensor_tensor(
                                 dst, wping[pp][:, 0:wdt],
                                 thr4[:, tcol:tcol + 1], dst,
                                 ALU.is_ge, ALU.subtract))

    def qjobs(dram, nrows, ncols, tcol, dsts):
        out = []
        for k in range(nrows // 128):
            for h0 in range(0, ncols, 2048):
                wdt = min(2048, ncols - h0)
                out.append((dram[k * 128:(k + 1) * 128, h0:h0 + wdt], wdt,
                            tcol, dsts(k, h0, wdt)))
        return out
    quant_stream(qjobs(io["w1T"], D, DF, 0,
                       lambda k, h0, wdt: w1q[k][:, h0:h0 + wdt])
                 + qjobs(io["w2T"], DF, D, 2,
                         lambda k, h0, wdt: w2q[k][:, h0:h0 + wdt]))
    P.wait("tensor", P.fence("vector"))
    P.barrier()
    if io.get("dbg_thr") is not None:
        P.dma("sync", io["dbg_thr"], thr4[:])
        P.barrier()

    tbm = P.mark_dma("sync")
    P.wait("scalar", tbm)
    P.wait("vector", tbm)
    for tt in range(NT):
        t0 = tt * W
        for c in range(8):
            P.dma("sync", x1t[c][:], io["x1T"][c * 128:(c + 1) * 128,
                                               t0:t0 + W])
        ld_all = P.mark_dma("sync")
        P.wait("scalar", ld_all)
        for c in range(8):
            P.ins("scalar", lambda s, c=c: s.copy(x1b[c][:], x1t[c][:]))
        tf = P.fence("scalar")
        P.wait("tensor", tf)
        ev_prev = None
        for fc in range(32):
            pp = ps_h[fc % 2]
            if fc >= 1:
                P.wait("tensor", ev_prev[fc - 1])
            P.ins("tensor", lambda t: t.nop())
            P.ins("tensor", lambda t: t.nop())
            for k in range(8):
                P.ins("tensor", lambda t, fc=fc, k=k, pp=pp: t.matmul(
                    pp, w1q[k][:, fc * 128:(fc + 1) * 128], x1b[k][:],
                    start=(k == 0), stop=(k == 7)))
            tm = P.mark("tensor")
            if io.get("dbg_ps") is not None and tt == 5 and fc == 0:
                P.wait("vector", tm)
                tdd = P.ins("vector", lambda v, pp=pp: v.tensor_copy(
                    dbgbuf, pp))
                P.wait("sync", tdd)
                P.dma("sync", io["dbg_ps"], dbgbuf)
                P.dma("sync", io["dbg_x1b"], x1b[0][:])
                P.wait("scalar", tdd)
            P.wait("scalar", tm)
            te = P.ins("scalar", lambda s, fc=fc, pp=pp: s.activation(
                hbf[fc], pp, AF.Relu, bias=b1c[:, fc:fc + 1], scale=1.0))
            if ev_prev is None:
                ev_prev = {}
            ev_prev[fc] = te
        tfh = P.fence("scalar")
        if io.get("dbg_h") is not None and tt == 5:
            P.wait("sync", tfh)
            P.dma("sync", io["dbg_h"], hbf[0])
        P.wait("tensor", tfh)
        if io.get("dbg_hall") is not None and tt == 5:
            tfh2 = P.fence("scalar")
            P.wait("sync", tfh2)
            for f in range(32):
                P.dma("sync", io["dbg_hall"][f * 128:(f + 1) * 128, :],
                      hbf[f])
        evo = {}
        for oc in range(8):
            pp = ps_o[oc % 2]
            if oc >= 1:
                P.wait("tensor", evo[oc - 1])
            for f in range(32):
                if f % 8 == 0:
                    P.ins("tensor", lambda t: t.nop())
                    P.ins("tensor", lambda t: t.nop())
                P.ins("tensor", lambda t, oc=oc, f=f, pp=pp: t.matmul(
                    pp, w2q[f][:, oc * 128:(oc + 1) * 128], hbf[f],
                    start=(f == 0), stop=(f == 31)))
            tm = P.mark("tensor")
            if io.get("dbg_pso") is not None and tt == 5 and oc == 0:
                P.barrier()
                td2 = P.ins("vector", lambda v, pp=pp: v.tensor_copy(
                    dbgbuf, pp))
                P.wait("sync", td2)
                P.dma("sync", io["dbg_pso"], dbgbuf)
                P.dma("sync", io["dbg_w2q"], w2q[0][:])
                P.barrier()
            P.wait("vector", tm)
            P.ins("vector", lambda v, oc=oc, pp=pp: v.tensor_tensor(
                x1t[oc][:], pp, x1t[oc][:], ALU.add))
            P.ins("vector", lambda v, oc=oc: v.tensor_scalar_add(
                x1t[oc][:], x1t[oc][:], b2c[:, oc:oc + 1]))
            evo[oc] = P.fence("vector")
        if io.get("dbg_r") is not None and tt == 5:
            tfr = P.fence("vector")
            P.wait("sync", tfr)
            for c in range(8):
                P.dma("sync", io["dbg_r"][c * 128:(c + 1) * 128, :],
                      x1t[c][:])
        tb = emit_ln(P, kit, [x[:] for x in x1t], psS, psS2, psB1, psB2,
                     ones1r[:], None)
        P.wait("vector", tb)
        st_toks = []
        for c in range(8):
            P.ins("vector", lambda v, c=c: v.tensor_tensor(
                x1t[c][:], x1t[c][:], psB1, ALU.mult))
            P.ins("vector", lambda v, c=c: v.tensor_tensor(
                x1t[c][:], x1t[c][:], psB2, ALU.add))
            tx = P.fence("vector")
            P.wait("sync", tx)
            st_toks.append(P.dma("sync", io["x2T"][c * 128:(c + 1) * 128,
                                                   t0:t0 + W], x1t[c][:]))
        P.barrier()
    if io.get("dbg_w1q") is not None:
        P.dma("sync", io["dbg_w1q"], w1q[0][:])
        P.dma("sync", io["dbg_w2q"], w2q[0][:])
        P.dma("sync", io["dbg_w1qb"], w1q[7][:])
        if DEBUG_P2W:
            P.dma("sync", io["dbg_thr"], thr4[:])
            for f in range(32):
                P.dma("sync", io["dbg_hall"][f * 128:(f + 1) * 128, :],
                      hbf[f])
            for c in range(8):
                P.dma("sync", io["dbg_x1b"][c * 128:(c + 1) * 128, :],
                      x1b[c][:])
            P.dma("gpsimd", io["dbg_hall"][4000:4002, 0:512], kit.rb[:])
        P.barrier()


def prep_p2_inputs(inputs, core, x1T=None):
    d = inputs
    m = dict(
        w1T=np.ascontiguousarray(d["L0_ffn_W1"].T),
        w2T=np.ascontiguousarray(d["L0_ffn_W2"].T),
        b1c=_pack_cols(d["L0_ffn_b1"]),
        b2c=_pack_cols(d["L0_ffn_b2"]),
    )
    if x1T is not None:
        m["x1T"] = x1T
    return m





DEBUG_P3 = False


def build_p3_program():
    nc = bass.Bass("TRN2", target_bir_lowering=False, debug=False)
    io = {}
    if DEBUG_P3:
        io["dbg_xab"] = nc.dram_tensor("dbg_xab", [2048, 512], BF16,
                                       kind="ExternalOutput").ap()
        io["dbg_zb"] = nc.dram_tensor("dbg_zb", [2048, 512], BF16,
                                      kind="ExternalOutput").ap()
        io["dbg_xdb"] = nc.dram_tensor("dbg_xdb", [128, 512], F32,
                                       kind="ExternalOutput").ap()
        io["dbg_dtf"] = nc.dram_tensor("dbg_dtf", [2048, 512], F32,
                                       kind="ExternalOutput").ap()
        io["dbg_ydc"] = nc.dram_tensor("dbg_ydc", [2048, 512], F32,
                                       kind="ExternalOutput").ap()
        io["dbg_yb"] = nc.dram_tensor("dbg_yb", [2048, 512], BF16,
                                      kind="ExternalOutput").ap()
        io["dbg_xst0"] = nc.dram_tensor("dbg_xst0", [128, 516], BF16,
                                        kind="ExternalOutput").ap()
        io["dbg_st"] = nc.dram_tensor("dbg_st", [128, 256], F32,
                                      kind="ExternalOutput").ap()
        io["dbg_halo0"] = nc.dram_tensor("dbg_halo0", [128, 64], BF16,
                                         kind="ExternalOutput").ap()
        io["dbg_halo1"] = nc.dram_tensor("dbg_halo1", [128, 64], BF16,
                                         kind="ExternalOutput").ap()
        io["dbg_bc"] = nc.dram_tensor("dbg_bc", [32, 512], F32,
                                      kind="ExternalOutput").ap()
        io["dbg_conv0"] = nc.dram_tensor("dbg_conv0", [128, 512], F32,
                                         kind="ExternalOutput").ap()
    io["x2T"] = nc.dram_tensor("x2T", [D, T], F32, kind="ExternalInput").ap()
    io["inpT"] = nc.dram_tensor("inpT", [D, 4096], F32,
                                kind="ExternalInput").ap()
    io["convw"] = nc.dram_tensor("convw", [128, 64], F32,
                                 kind="ExternalInput").ap()
    io["convb"] = nc.dram_tensor("convb", [128, 16], F32,
                                 kind="ExternalInput").ap()
    io["xpT"] = nc.dram_tensor("xpT", [2048, 128], F32,
                               kind="ExternalInput").ap()
    io["dtT"] = nc.dram_tensor("dtT", [64, 2048], F32,
                               kind="ExternalInput").ap()
    io["dtb"] = nc.dram_tensor("dtb", [128, 16], F32,
                               kind="ExternalInput").ap()
    io["alog"] = nc.dram_tensor("alog", [128, 256], F32,
                                kind="ExternalInput").ap()
    io["dcol"] = nc.dram_tensor("dcol", [128, 16], F32,
                                kind="ExternalInput").ap()
    io["outT"] = nc.dram_tensor("outT", [2048, D], F32,
                                kind="ExternalInput").ap()
    io["x3T"] = nc.dram_tensor("x3T", [D, T], F32, kind="ExternalOutput").ap()
    io["c_eye"] = nc.dram_tensor("c_eye", [128, 128], BF16,
                                 kind="ExternalInput").ap()
    with contextlib.ExitStack() as ctx:
        P = Prog(nc, ctx)
        emit_p3(nc, P, io)
        P.finalize([P.mark_dma("sync")])
    return nc


def emit_p3(nc, P, io, tag="p3"):
    W = 512
    NC = T // W

    bc_dram = nc.dram_tensor(f"{tag}_bc", [32, W], BF16, kind="Internal").ap()

    inph = [sbuf(nc, f"{tag}_inph{k}", [128, 2048], BF16) for k in range(8)]
    outw = [sbuf(nc, f"{tag}_outw{k}", [128, D], BF16) for k in range(16)]
    xpw = sbuf(nc, f"{tag}_xpw", [128, 16 * 128], BF16)
    dtw = sbuf(nc, f"{tag}_dtw", [64, 2048], BF16)
    convw = sbuf(nc, f"{tag}_convw", [128, 64])
    convb = sbuf(nc, f"{tag}_convb", [128, 16])
    dtb = sbuf(nc, f"{tag}_dtb", [128, 16])
    dtbn = sbuf(nc, f"{tag}_dtbn", [128, 16])
    acol = sbuf(nc, f"{tag}_acol", [128, 256])
    dcol = sbuf(nc, f"{tag}_dcol", [128, 16])
    onesc = sbuf(nc, f"{tag}_onesc", [128, 1])
    ones1r = sbuf(nc, f"{tag}_ones1r", [2, 128])

    ar3 = sbuf(nc, f"{tag}_ar3", [128, 8192])
    x2t = [ar3[:, c * W:(c + 1) * W] for c in range(8)]
    ydc = [ar3[:, d * W:(d + 1) * W] for d in range(16)]
    yb = [inph[d // 4][:, (d % 4) * W:(d % 4 + 1) * W] for d in range(16)]
    ar5 = sbuf(nc, f"{tag}_ar5", [128, 2048])
    x2b = [ar5.bitcast(BF16)[:, c * W:(c + 1) * W] for c in range(8)]
    sp1 = sbuf(nc, f"{tag}_sp1", [128, W])
    sp2 = sbuf(nc, f"{tag}_sp2", [128, W])
    spp = [sp1, sp2]
    xdb = ar5[:, 1024:1536]
    xdbb = ar5.bitcast(BF16)[0:64, 3072:3584]
    xst = [sbuf(nc, f"{tag}_xst{i}", [128, W + 4], BF16) for i in range(2)]
    halo = sbuf(nc, f"{tag}_halo", [128, 64], BF16)

    zb = [sbuf(nc, f"{tag}_zb{d}", [128, W], BF16) for d in range(16)]
    xab = [sbuf(nc, f"{tag}_xab{d}", [128, W], BF16) for d in range(16)]
    dtf = [sbuf(nc, f"{tag}_dt{d}", [128, W]) for d in range(16)]
    ub = [sbuf(nc, f"{tag}_ub{d}", [128, W], BF16) for d in range(16)]
    state = [sbuf(nc, f"{tag}_st{d}", [128, 16]) for d in range(16)]
    def _f32t(k):
        v = inph[k].bitcast(F32)
        return [v[:, 0:W], v[:, W:2 * W]]
    def _bf16t(k):
        return [inph[k][:, 0:W], inph[k][:, W:2 * W]]
    bcb = _bf16t(4)
    ccb = _bf16t(5)
    dat = _f32t(6) + _f32t(2)
    gin = _bf16t(7)
    gout = _bf16t(0)
    ymul = _bf16t(1)
    yacc = [ar3[:, 4096 + i * W:4096 + (i + 1) * W] for i in range(8)]
    eyeb3 = sbuf(nc, f"{tag}_eye", [128, 128], BF16)
    kit = LnKit(nc, f"{tag}_ln", W)
    P.make_fences(nc)

    pb = [psum(nc, f"{tag}_pb{i}", [128, 512]) for i in range(8)]
    psS = pb[0][0:2, :]
    psS2 = pb[1][0:2, :]
    psB1 = pb[2]
    psB2 = pb[3]
    ps_a = [pb[4], pb[5]]
    ps_b2 = [pb[6], pb[7]]

    P.ins("vector", lambda v: v.memset(P._fb["vector"], 0.0))
    P.ins("vector", lambda v: v.memset(P._fb["scalar"], 0.0))
    P.ins("vector", lambda v: v.memset(kit.epsc, 1e-5))
    P.ins("vector", lambda v: v.memset(kit.onesr, 1.0))
    P.ins("vector", lambda v: v.memset(onesc, 1.0))
    P.ins("vector", lambda v: v.memset(ones1r[:], 1.0))
    P.ins("vector", lambda v: v.memset(kit.ra, 0.0))
    P.ins("vector", lambda v: v.memset(kit.rb, 0.0))
    P.ins("vector", lambda v: v.memset(halo[:], 0.0))
    for d in range(16):
        P.ins("vector", lambda v, d=d: v.memset(state[d][:], 0.0))
    for nm, dst in [("convw", convw), ("convb", convb), ("dtb", dtb),
                    ("dcol", dcol)]:
        P.dma("sync", dst[:], io[nm])
    P.dma("sync", eyeb3[:], io["c_eye"])
    ta = P.dma("sync", acol[:], io["alog"])
    tall = P.mark_dma("sync")
    P.wait("scalar", tall)
    tae = P.ins("scalar", lambda s: s.activation(acol[:], acol[:], AF.Exp))
    P.wait("vector", tae)
    P.ins("vector", lambda v: v.tensor_scalar_mul(dtbn[:], dtb[:], -1.0))
    P.dma("gpsimd", dtw[:], io["dtT"])
    for k in range(16):
        P.dma("gpsimd", xpw[:, k * 128:(k + 1) * 128],
              io["xpT"][k * 128:(k + 1) * 128, :])
        P.dma("gpsimd", outw[k][:], io["outT"][k * 128:(k + 1) * 128, :])
    P.barrier()

    for cc in range(NC):
        t0 = cc * W
        for c in range(8):
            P.dma("sync", x2t[c], io["x2T"][c * 128:(c + 1) * 128,
                                            t0:t0 + W])
        for k in range(8):
            P.dma("gpsimd", inph[k][:],
                  io["inpT"][k * 128:(k + 1) * 128, 0:2048])
        ld_all = P.mark_dma("sync")
        ld_g = P.mark_dma("gpsimd")
        P.wait("scalar", ld_all)
        for c in range(8):
            P.ins("scalar", lambda s, c=c: s.copy(x2b[c], x2t[c]))
        tf = P.fence("scalar")
        P.wait("tensor", tf, ld_g)

        evp = {}
        for oc in range(16):
            pp = ps_a[oc % 2]
            xp = xst[oc % 2]
            if oc >= 2:
                P.wait("tensor", evp[oc - 2])
            for k in range(8):
                P.ins("tensor", lambda t, oc=oc, k=k, pp=pp: t.matmul(
                    pp, inph[k][:, oc * 128:(oc + 1) * 128], x2b[k][:],
                    start=(k == 0), stop=(k == 7)))
            tm = P.mark("tensor")
            P.wait("vector", tm)
            P.ins("vector", lambda v, oc=oc, xp=xp: v.tensor_copy(
                xp[:, 0:4], halo[:, oc * 4:(oc + 1) * 4]))
            tev = P.ins("vector", lambda v, oc=oc, xp=xp, pp=pp:
                        v.tensor_copy(xp[:, 4:W + 4], pp))
            P.ins("vector", lambda v, oc=oc, pp=pp: v.tensor_copy(
                halo[:, oc * 4:(oc + 1) * 4], pp[:, W - 4:W]))
            evp[oc] = P.fence("vector")
            P.ins("vector", lambda v, oc=oc, xp=xp: v.tensor_scalar_mul(
                sp1, xp[:, 1:W + 1], convw[:, oc:oc + 1]))
            P.ins("vector", lambda v, oc=oc, xp=xp: v.scalar_tensor_tensor(
                sp1, xp[:, 2:W + 2], convw[:, 16 + oc:17 + oc], sp1,
                ALU.mult, ALU.add))
            P.ins("vector", lambda v, oc=oc, xp=xp: v.scalar_tensor_tensor(
                sp1, xp[:, 3:W + 3], convw[:, 32 + oc:33 + oc], sp1,
                ALU.mult, ALU.add))
            P.ins("vector", lambda v, oc=oc, xp=xp: v.scalar_tensor_tensor(
                sp1, xp[:, 4:W + 4], convw[:, 48 + oc:49 + oc], sp1,
                ALU.mult, ALU.add))
            tcv = P.fence("vector")
            if DEBUG_P3 and cc == 1 and oc == 0:
                P.wait("sync", tcv)
                P.dma("sync", io["dbg_xst0"], xp[:])
                P.dma("sync", io["dbg_conv0"], sp1)
                tdmp = P.mark_dma("sync")
                P.wait("scalar", tdmp)
            P.wait("scalar", tcv)
            tsl = P.ins("scalar", lambda s, oc=oc: s.activation(
                xab[oc][:], sp1, AF.Silu, bias=convb[:, oc:oc + 1],
                scale=1.0))
            tslf = P.fence("scalar")
            P.wait("vector", tslf)
        P.wait("gpsimd", P.mark("tensor"))
        for k in range(8):
            P.dma("gpsimd", inph[k][:],
                  io["inpT"][k * 128:(k + 1) * 128, 2048:4096])
        ld_g = P.mark_dma("gpsimd")
        P.wait("tensor", ld_g, evp[14], evp[15])
        for oc in range(16):
            pp = ps_a[oc % 2]
            if oc >= 2:
                P.wait("tensor", evp[oc - 2])
            for k in range(8):
                P.ins("tensor", lambda t, oc=oc, k=k, pp=pp: t.matmul(
                    pp, inph[k][:, oc * 128:(oc + 1) * 128], x2b[k][:],
                    start=(k == 0), stop=(k == 7)))
            tm = P.mark("tensor")
            P.wait("scalar", tm)
            P.ins("scalar", lambda s, oc=oc, pp=pp: s.activation(
                zb[oc][:], pp, AF.Silu))
            evp[oc] = P.fence("scalar")
        P.wait("tensor", evp[14], evp[15])

        if DEBUG_P3 and cc in (0, 1):
            tdz = P.fence("scalar")
            P.wait("sync", tdz, P.fence("vector"))
            P.dma("sync", io["dbg_halo0" if cc == 0 else "dbg_halo1"],
                  halo[:])
        if DEBUG_P3 and cc == 0:
            tdz = P.fence("scalar")
            P.wait("sync", tdz, P.fence("vector"))
            for dd in range(16):
                P.dma("sync", io["dbg_xab"][dd * 128:(dd + 1) * 128, :],
                      xab[dd][:])
                P.dma("sync", io["dbg_zb"][dd * 128:(dd + 1) * 128, :],
                      zb[dd][:])
        tfx = P.fence("scalar")
        P.wait("tensor", tfx)
        for k in range(16):
            P.ins("tensor", lambda t, k=k: t.matmul(
                ps_b2[0], xpw[:, k * 128:(k + 1) * 128], xab[k][:],
                start=(k == 0), stop=(k == 15)))
        tm = P.mark("tensor")
        P.wait("vector", tm)
        tx = P.ins("vector", lambda v: v.tensor_copy(xdb, ps_b2[0]))
        txf = P.fence("vector")
        P.wait("scalar", txf)
        txb = P.ins("scalar", lambda s: s.copy(xdbb, xdb[0:64, :]))
        P.wait("gpsimd", txf)
        P.dma("gpsimd", bc_dram, xdb[64:96, :])
        tbc = P.mark_dma("gpsimd")
        tfs = P.fence("scalar")
        P.wait("tensor", tfs)

        sgf = {0: None, 1: None}
        ubf = {0: None, 1: None}
        for oc in range(16):
            par = oc % 2
            pp = ps_b2[par]
            if sgf[par] is not None:
                P.wait("tensor", sgf[par])
            P.ins("tensor", lambda t, oc=oc, pp=pp: t.matmul(
                pp, dtw[:, oc * 128:(oc + 1) * 128], xdbb,
                start=True, stop=True))
            tm = P.mark("tensor")
            P.wait("scalar", tm)
            if ubf[par] is not None:
                P.wait("scalar", ubf[par])
            P.ins("scalar", lambda s, oc=oc, pp=pp, par=par: s.activation(
                spp[par], pp, AF.Sigmoid, bias=dtbn[:, oc:oc + 1],
                scale=-1.0))
            sgf[par] = P.fence("scalar")
            P.wait("scalar", sgf[par])
            P.ins("scalar", lambda s, oc=oc, par=par: s.activation(
                dtf[oc][:], spp[par], AF.Ln))
            tsf = P.fence("scalar")
            P.wait("vector", tsf)
            P.ins("vector", lambda v, oc=oc: v.scalar_tensor_tensor(
                ub[oc][:], dtf[oc][:], -1.0, xab[oc][:], ALU.mult,
                ALU.mult))
            ubf[par] = P.fence("vector")

        if DEBUG_P3 and cc == 0:
            P.wait("sync", P.fence("vector"), P.fence("scalar"))
            P.dma("sync", io["dbg_xdb"], xdb)
            for dd in range(16):
                P.dma("sync", io["dbg_dtf"][dd * 128:(dd + 1) * 128, :],
                      dtf[dd][:])
        tf0 = P.fence("vector")
        stepv = {}
        stepg = {}
        bankf = {}
        scanq = {}
        for dh in range(2):
            scanf = {0: None, 1: None}
            ymf = {0: None, 1: None}
            pemk = {0: None, 1: None}
            for n in range(16):
                nb = n % 2
                P.wait("sync", tbc)
                if (dh, n - 2) in stepv:
                    P.wait("sync", stepv[(dh, n - 2)], stepg[(dh, n - 2)])
                elif dh == 1 and n < 2:
                    P.wait("sync", stepv[(0, 14 + n)], stepg[(0, 14 + n)])
                P.dma("sync", bcb[nb][:],
                      bc_dram[n:n + 1, :].partition_broadcast(128))
                P.dma("sync", ccb[nb][:],
                      bc_dram[16 + n:17 + n, :].partition_broadcast(128))
                tB = P.mark_dma("sync")
                P.wait("vector", tB)
                for di in range(8):
                    d = dh * 8 + di
                    pp = di % 2
                    qq = di % 4
                    if scanq.get(qq) is not None:
                        P.wait("scalar", scanq[qq])
                    else:
                        P.wait("scalar", P.mark("vector"))
                    P.ins("scalar", lambda sc, d=d, n=n, qq=qq:
                          sc.activation(dat[qq][:], dtf[d][:], AF.Exp,
                                        scale=acol[:, d * 16 + n:
                                                   d * 16 + n + 1]))
                    tdaf = P.fence("scalar")
                    if ymf[pp] is not None:
                        P.wait("vector", ymf[pp])
                    P.ins("vector", lambda v, d=d, nb=nb, pp=pp:
                          v.tensor_tensor(gin[pp][:], ub[d][:], bcb[nb][:],
                                          ALU.mult))
                    P.wait("vector", tdaf)
                    tsc = P.ins("vector", lambda v, d=d, n=n, pp=pp, qq=qq:
                                v.tensor_tensor_scan(
                                    gout[pp][:], dat[qq][:], gin[pp][:],
                                    state[d][:, n:n + 1], ALU.mult, ALU.add))
                    P.wait("vector", tsc)
                    P.ins("vector", lambda v, d=d, n=n, pp=pp: v.tensor_copy(
                        state[d][:, n:n + 1], gout[pp][:, W - 1:W]))
                    if pemk[pp] is not None:
                        P.wait("vector", pemk[pp])
                    P.ins("vector", lambda v, nb=nb, pp=pp: v.tensor_tensor(
                        ymul[pp][:], gout[pp][:], ccb[nb][:], ALU.mult))
                    scanf[pp] = P.fence("vector")
                    scanq[qq] = scanf[pp]
                    ymf[pp] = scanf[pp]
                    P.wait("tensor", ymf[pp])
                    if n == 0 and (dh, di) in bankf:
                        P.wait("tensor", bankf[(dh, di)])
                    P.ins("tensor", lambda t, di=di, n=n, pp=pp: t.matmul(
                        pb[di], eyeb3[:], ymul[pp][:],
                        start=(n == 0), stop=(n == 15)))
                    pemk[pp] = P.mark("tensor")
                stepv[(dh, n)] = scanf[pp]
                stepg[(dh, n)] = ymf[pp]
            tmall = P.mark("tensor")
            P.wait("vector", tmall)
            if dh == 0:
                for di in range(8):
                    P.ins("vector", lambda v, di=di: v.tensor_copy(
                        yacc[di], pb[di]))
                    bankf[(1, di)] = P.fence("vector")

        tmall = P.mark("tensor")
        P.wait("vector", tmall)
        for d in range(16):
            ysrc = yacc[d] if d < 8 else pb[d - 8]
            P.ins("vector", lambda v, d=d: v.tensor_scalar_mul(
                sp2, xab[d][:], dcol[:, d:d + 1]))
            P.ins("vector", lambda v, d=d, ysrc=ysrc: v.tensor_tensor(
                sp2, ysrc, sp2, ALU.add))
            P.ins("vector", lambda v, d=d: v.tensor_tensor(
                yb[d], sp2, zb[d][:], ALU.mult))
        tf = P.fence("vector")

        P.wait("sync", tf)
        for c in range(8):
            P.dma("sync", x2t[c], io["x2T"][c * 128:(c + 1) * 128,
                                            t0:t0 + W])
        ld_all = P.mark_dma("sync")
        P.wait("vector", ld_all)
        P.wait("tensor", tf)

        evo = {}
        for oc in range(8):
            pp = ps_a[oc % 2]
            if oc >= 2:
                P.wait("tensor", evo[oc - 2])
            for k in range(16):
                P.ins("tensor", lambda t, oc=oc, k=k, pp=pp: t.matmul(
                    pp, outw[k][:, oc * 128:(oc + 1) * 128], yb[k],
                    start=(k == 0), stop=(k == 15)))
            tm = P.mark("tensor")
            P.wait("vector", tm)
            P.ins("vector", lambda v, oc=oc, pp=pp: v.tensor_tensor(
                x2t[oc], pp, x2t[oc], ALU.add))
            evo[oc] = P.fence("vector")
        tb = emit_ln(P, kit, [x for x in x2t], psS, psS2, psB1, psB2,
                     ones1r[:], None)
        P.wait("vector", tb)
        for c in range(8):
            P.ins("vector", lambda v, c=c: v.tensor_tensor(
                x2t[c], x2t[c], psB1, ALU.mult))
            P.ins("vector", lambda v, c=c: v.tensor_tensor(
                x2t[c], x2t[c], psB2, ALU.add))
            tx = P.fence("vector")
            P.wait("sync", tx)
            P.dma("sync", io["x3T"][c * 128:(c + 1) * 128, t0:t0 + W],
                  x2t[c])
        P.barrier()


DEBUG_P4 = False


def build_p4_program():
    nc = bass.Bass("TRN2", target_bir_lowering=False, debug=False)
    io = {}
    io["x3T"] = nc.dram_tensor("x3T", [D, T], F32, kind="ExternalInput").ap()
    io["wrPT"] = nc.dram_tensor("wrPT", [D, 128], F32,
                                kind="ExternalInput").ap()
    io["wmhT"] = nc.dram_tensor("wmhT", [D, D], F32,
                                kind="ExternalInput").ap()
    io["kT"] = nc.dram_tensor("kT", [1024, 4096], F32,
                              kind="ExternalInput").ap()
    io["vT"] = nc.dram_tensor("vT", [32768, 128], F32,
                              kind="ExternalInput").ap()
    io["wmgT"] = nc.dram_tensor("wmgT", [D, D], F32,
                                kind="ExternalInput").ap()
    io["eyeB"] = nc.dram_tensor("eyeB", [128, 128], BF16,
                                kind="ExternalInput").ap()
    io["x4T"] = nc.dram_tensor("x4T", [D, T], F32, kind="ExternalOutput").ap()
    if DEBUG_P4:
        io["dbg_gw"] = nc.dram_tensor("dbg_gw", [16, T], BF16,
                                      kind="ExternalOutput").ap()
        io["dbg_q"] = nc.dram_tensor("dbg_q", [1024, T], BF16,
                                     kind="ExternalOutput").ap()
        io["dbg_yc"] = nc.dram_tensor("dbg_yc", [1024, T], BF16,
                                      kind="ExternalOutput").ap()
        io["dbg_m"] = nc.dram_tensor("dbg_m", [128, 64], F32,
                                     kind="ExternalOutput").ap()
        io["dbg_msk"] = nc.dram_tensor("dbg_msk", [128, 512], F32,
                                       kind="ExternalOutput").ap()
        io["dbg_sig"] = nc.dram_tensor("dbg_sig", [128, 512], F32,
                                       kind="ExternalOutput").ap()
        io["dbg_gwt"] = nc.dram_tensor("dbg_gwt", [128, 512], BF16,
                                       kind="ExternalOutput").ap()
    with contextlib.ExitStack() as ctx:
        P = Prog(nc, ctx)
        st = emit_p4(nc, P, io)
        P.finalize(st)
    return nc


def emit_p4(nc, P, io, tag="p4"):
    W = 512
    H = 8
    gw_dram = nc.dram_tensor(f"{tag}_gw", [16, T], BF16, kind="Internal").ap()

    x3b = [sbuf(nc, f"{tag}_x3b{k}", [128, T], BF16) for k in range(8)]
    qall = x3b
    ycat = [sbuf(nc, f"{tag}_yc{h}", [128, T], BF16) for h in range(8)]
    kb = [sbuf(nc, f"{tag}_kb{i}", [128, 4096], BF16) for i in range(2)]
    vb = [sbuf(nc, f"{tag}_vb{i}", [128, 4096], BF16) for i in range(2)]
    wmg = ([kb[0][:, k * 1024:(k + 1) * 1024] for k in range(4)]
           + [vb[0][:, k * 1024:(k + 1) * 1024] for k in range(4)])
    x3r = ([kb[1].bitcast(F32)[:, c * W:(c + 1) * W] for c in range(4)]
           + [vb[1].bitcast(F32)[:, c * W:(c + 1) * W] for c in range(4)])
    wmh_ar = sbuf(nc, f"{tag}_wmh", [128, 8192], BF16)
    wmh = [wmh_ar[:, k * 1024:(k + 1) * 1024] for k in range(8)]
    gwb1_ar = sbuf(nc, f"{tag}_gwb1", [128, 8192], BF16)
    gwb = [[wmh_ar[:, e * W:(e + 1) * W] for e in range(16)],
           [gwb1_ar[:, e * W:(e + 1) * W] for e in range(16)]]
    aT = [sbuf(nc, f"{tag}_aT{i}", [128, W], BF16) for i in range(4)]
    wrb = sbuf(nc, f"{tag}_wrb", [128, 1024], BF16)
    eyeb = sbuf(nc, f"{tag}_eye", [128, 128], BF16)
    gwt = sbuf(nc, f"{tag}_gwt", [128, 128], BF16)
    stg16 = sbuf(nc, f"{tag}_st16", [16, 128], BF16)
    m1 = sbuf(nc, f"{tag}_m1", [128, 1])
    m2 = sbuf(nc, f"{tag}_m2", [128, 1])
    msk = sbuf(nc, f"{tag}_msk", [128, 16])
    l2s = sbuf(nc, f"{tag}_l2", [128, 16])
    sig = sbuf(nc, f"{tag}_sig", [128, 16])
    ones1r = sbuf(nc, f"{tag}_ones1r", [2, 128])
    kit = LnKit(nc, f"{tag}_ln", W)
    P.make_fences(nc)

    pb = [psum(nc, f"{tag}_pb{i}", [128, 512]) for i in range(8)]
    psS = pb[0][0:2, :]
    psS2 = pb[1][0:2, :]
    psB1 = pb[2]
    psB2 = pb[3]

    P.ins("vector", lambda v: v.memset(P._fb["vector"], 0.0))
    P.ins("vector", lambda v: v.memset(P._fb["scalar"], 0.0))
    P.ins("vector", lambda v: v.memset(kit.epsc, 1e-5))
    P.ins("vector", lambda v: v.memset(kit.onesr, 1.0))
    P.ins("vector", lambda v: v.memset(ones1r[:], 1.0))
    P.ins("vector", lambda v: v.memset(kit.ra, 0.0))
    P.ins("vector", lambda v: v.memset(kit.rb, 0.0))
    P.ins("vector", lambda v: v.memset(gwt[:], 0.0))
    P.dma("sync", eyeb[:], io["eyeB"])
    teye = P.mark_dma("sync")
    P.dma("gpsimd", wrb[:],
          io["wrPT"].rearrange("(c p) e -> p c e", p=128))
    for k in range(8):
        P.dma("gpsimd", wmh[k], io["wmhT"][k * 128:(k + 1) * 128, :])
        P.dma("gpsimd", x3b[k][:], io["x3T"][k * 128:(k + 1) * 128, :])
    P.dma("gpsimd", kb[0][:], io["kT"][0:128, :])
    P.dma("gpsimd", vb[0][:],
          io["vT"][0:4096, :].rearrange("(c p) d -> p c d", p=128))
    tinit = P.mark_dma("gpsimd")
    kvmark = {0: tinit}

    P.wait("tensor", tinit, teye)
    rfin = {}
    prev_tevf = None
    prev_spill = None
    prev_tmt = None
    for tc32 in range(32):
        pr = pb[tc32 % 2]
        if tc32 >= 2:
            P.wait("tensor", rfin[tc32 - 2])
        for k in range(8):
            P.ins("tensor", lambda t, k=k, tc32=tc32, pr=pr: t.matmul(
                pr[:, 0:128], x3b[k][:, tc32 * 128:(tc32 + 1) * 128],
                wrb[:, k * 128:(k + 1) * 128],
                start=(k == 0), stop=(k == 7)))
        tm = P.mark("tensor")
        P.wait("vector", tm)
        if prev_tmt is not None:
            P.wait("vector", prev_tmt)
        lg = pr[:, 0:16]
        def FW():
            P.wait("vector", P.fence("vector"))
        P.ins("vector", lambda v, lg=lg: v.reduce_max(m1[:], lg, AX.X))
        FW()
        P.ins("vector", lambda v, lg=lg: v.tensor_single_scalar(
            msk[:], lg, m1[:, 0:1], ALU.is_ge))
        FW()
        P.ins("vector", lambda v: v.tensor_scalar_mul(msk[:], msk[:],
                                                      -1000.0))
        FW()
        P.ins("vector", lambda v, lg=lg: v.tensor_tensor(
            l2s[:], msk[:], lg, ALU.add))
        FW()
        P.ins("vector", lambda v: v.reduce_max(m2[:], l2s[:], AX.X))
        FW()
        P.ins("vector", lambda v, lg=lg: v.tensor_single_scalar(
            msk[:], lg, m2[:, 0:1], ALU.is_ge))
        FW()
        tv = P.fence("vector")
        P.wait("scalar", tv)
        P.ins("scalar", lambda s, lg=lg: s.activation(sig[:], lg, AF.Sigmoid))
        tsf = P.fence("scalar")
        P.wait("vector", tsf)
        P.ins("vector", lambda v: v.tensor_tensor(gwt[:, 0:16], sig[:],
                                                  msk[:], ALU.mult))
        rfin[tc32] = P.fence("vector")
        if DEBUG_P4 and tc32 < 32:
            P.wait("sync", rfin[tc32])
            P.dma("sync", io["dbg_m"][:, tc32:tc32 + 1], m1[:],
                  allow_slow_non_contiguous=True)
            P.dma("sync", io["dbg_m"][:, 32 + tc32:33 + tc32], m2[:],
                  allow_slow_non_contiguous=True)
            P.dma("sync", io["dbg_msk"][:, tc32 * 16:(tc32 + 1) * 16], msk[:])
            P.dma("sync", io["dbg_sig"][:, tc32 * 16:(tc32 + 1) * 16], sig[:])
            P.dma("sync", io["dbg_gwt"][:, tc32 * 16:(tc32 + 1) * 16],
                  gwt[:, 0:16])
            tdmp = P.mark_dma("sync")
            P.wait("vector", tdmp)
        P.wait("tensor", rfin[tc32])
        if prev_tevf is not None:
            P.wait("tensor", prev_tevf)
        P.ins("tensor", lambda t: t.matmul(pb[2][:, 0:128], gwt[:], eyeb[:],
                                           start=True, stop=True))
        prev_tmt = P.mark("tensor")
        P.wait("vector", prev_tmt)
        if prev_spill is not None:
            P.wait("vector", prev_spill)
        P.ins("vector", lambda v: v.tensor_copy(stg16[:],
                                                pb[2][0:16, 0:128]))
        prev_tevf = P.fence("vector")
        P.wait("sync", prev_tevf)
        P.dma("sync", gw_dram[:, tc32 * 128:(tc32 + 1) * 128], stg16[:])
        prev_spill = P.mark_dma("sync")
    P.barrier()

    for tcq in range(8):
        for h in range(8):
            for k in range(8):
                P.ins("tensor", lambda t, h=h, k=k, tcq=tcq: t.matmul(
                    pb[h], wmh[k][:, h * 128:(h + 1) * 128],
                    x3b[k][:, tcq * W:(tcq + 1) * W],
                    start=(k == 0), stop=(k == 7)))
        tmq = P.mark("tensor")
        P.wait("vector", tmq)
        for h in range(8):
            P.ins("vector", lambda v, h=h, tcq=tcq: v.tensor_copy(
                qall[h][:, tcq * W:(tcq + 1) * W], pb[h]))
        tqf = P.fence("vector")
        P.wait("tensor", tqf)
    P.barrier()
    if DEBUG_P4:
        for h in range(8):
            P.dma("sync", io["dbg_q"][h * 128:(h + 1) * 128, :], qall[h][:])
        P.dma("sync", io["dbg_gw"], gw_dram)
        P.wait("sync", P.mark_dma("sync"))

    for e in range(16):
        P.dma("sync", gwb[0][e],
              gw_dram[e:e + 1, 0:W].partition_broadcast(128))
    bmark = {0: (P.mark_dma("sync"), P.mark_dma("gpsimd"))}
    bguard = {}
    wmg_mark = None
    for h in range(H):
        P.wait("gpsimd", P.mark("tensor"))
        if h + 1 < H:
            P.dma("gpsimd", kb[(h + 1) % 2][:],
                  io["kT"][(h + 1) * 128:(h + 2) * 128, :])
            P.dma("gpsimd", vb[(h + 1) % 2][:],
                  io["vT"][(h + 1) * 4096:(h + 2) * 4096, :]
                  .rearrange("(c p) d -> p c d", p=128))
            kvmark[h + 1] = P.mark_dma("gpsimd")
        else:
            for k in range(8):
                P.dma("gpsimd", wmg[k], io["wmgT"][k * 128:(k + 1) * 128, :])
            wmg_mark = P.mark_dma("gpsimd")
        P.wait("tensor", kvmark[h])
        gelf_by_bank = {}
        yevf = {}
        multf = {}
        for tc in range(8):
            it = h * 8 + tc
            if it + 1 < 64:
                if it >= 1:
                    P.wait("sync", bguard[it - 1])
                    P.wait("gpsimd", bguard[it - 1])
                tcn = (tc + 1) % 8
                for e in range(16):
                    q = "sync" if e % 2 == 0 else "gpsimd"
                    P.dma(q, gwb[(it + 1) % 2][e],
                          gw_dram[e:e + 1, tcn * W:(tcn + 1) * W]
                          .partition_broadcast(128))
                bmark[it + 1] = (P.mark_dma("sync"), P.mark_dma("gpsimd"))
            sgw = gwb[it % 2]
            P.wait("vector", *bmark[it])
            psy = pb[6 + (tc % 2)]

            def issueV(j, tc=tc, h=h, psy=psy):
                P.wait("tensor", multf[j])
                if j == 0 and tc >= 2:
                    P.wait("tensor", yevf[tc - 2])
                P.ins("tensor", lambda t, h=h, j=j, psy=psy:
                      t.matmul(psy, vb[h % 2][:, j * 128:(j + 1) * 128],
                               aT[j % 4][:], start=(j == 0),
                               stop=(j == 31)))
            for ec in range(32):
                bank = pb[ec % 4]
                if (ec % 4) in gelf_by_bank:
                    P.wait("tensor", gelf_by_bank[ec % 4])
                P.ins("tensor", lambda t, h=h, ec=ec, tc=tc, bank=bank:
                      t.matmul(bank, kb[h % 2][:, ec * 128:(ec + 1) * 128],
                               qall[h][:, tc * W:(tc + 1) * W],
                               start=True, stop=True))
                tK = P.mark("tensor")
                if ec >= 3:
                    issueV(ec - 3)
                P.wait("scalar", tK)
                P.ins("scalar", lambda s, ec=ec, bank=bank: s.activation(
                    aT[ec % 4][:], bank, AF.Gelu))
                gelf_by_bank[ec % 4] = P.fence("scalar")
                P.wait("vector", gelf_by_bank[ec % 4])
                P.ins("vector", lambda v, ec=ec, sgw=sgw: v.tensor_tensor(
                    aT[ec % 4][:], aT[ec % 4][:], sgw[ec // 2], ALU.mult))
                multf[ec] = P.fence("vector")
            for j in (29, 30, 31):
                issueV(j)
            tmv = P.mark("tensor")
            P.wait("vector", tmv)
            P.ins("vector", lambda v, h=h, tc=tc, psy=psy: v.tensor_copy(
                ycat[h][:, tc * W:(tc + 1) * W], psy))
            yevf[tc] = P.fence("vector")
            bguard[it] = yevf[tc]
    P.barrier()
    if DEBUG_P4:
        for h in range(8):
            P.dma("sync", io["dbg_yc"][h * 128:(h + 1) * 128, :], ycat[h][:])
        P.wait("sync", P.mark_dma("sync"))

    stores = []
    P.wait("tensor", wmg_mark)
    for tc in range(8):
        for c in range(8):
            P.dma("sync", x3r[c], io["x3T"][c * 128:(c + 1) * 128,
                                            tc * W:(tc + 1) * W])
        ldr = P.mark_dma("sync")
        P.wait("vector", ldr)
        oev = {}
        for oc in range(8):
            po = pb[4 + (oc % 2)]
            if oc >= 2:
                P.wait("tensor", oev[oc - 2])
            for k in range(8):
                P.ins("tensor", lambda t, k=k, oc=oc, tc=tc, po=po: t.matmul(
                    po, wmg[k][:, oc * 128:(oc + 1) * 128],
                    ycat[k][:, tc * W:(tc + 1) * W],
                    start=(k == 0), stop=(k == 7)))
            tm = P.mark("tensor")
            P.wait("vector", tm)
            P.ins("vector", lambda v, oc=oc, po=po: v.tensor_tensor(
                x3r[oc], po, x3r[oc], ALU.add))
            oev[oc] = P.fence("vector")
        tb = emit_ln(P, kit, x3r, psS, psS2, psB1, psB2, ones1r[:], None)
        P.wait("vector", tb)
        for c in range(8):
            P.ins("vector", lambda v, c=c: v.tensor_tensor(
                x3r[c], x3r[c], psB1, ALU.mult))
            P.ins("vector", lambda v, c=c: v.tensor_tensor(
                x3r[c], x3r[c], psB2, ALU.add))
            tx = P.fence("vector")
            P.wait("sync", tx)
            stores.append(P.dma("sync",
                                io["x4T"][c * 128:(c + 1) * 128,
                                          tc * W:(tc + 1) * W], x3r[c]))
        P.wait("sync", P.mark_dma("sync"))
    return [P.mark_dma("sync")]


def prep_p4_inputs(inputs, core, x3T=None):
    d = inputs
    wr = d["L1_ffn_w_router"].astype(np.float32)
    wrPT = np.zeros((1024, 128), np.float32)
    wrPT[:, 0:16] = wr.T
    kT = np.ascontiguousarray(
        d["L1_ffn_k_ffwd"].astype(np.float32)
        .transpose(0, 3, 1, 2).reshape(1024, 4096))
    vT = np.ascontiguousarray(
        d["L1_ffn_v_ffwd"].astype(np.float32).reshape(32768, 128))
    m = dict(
        wrPT=wrPT,
        wmhT=np.ascontiguousarray(d["L1_ffn_w_mh"].astype(np.float32).T),
        kT=kT,
        vT=vT,
        wmgT=np.ascontiguousarray(d["L1_ffn_w_mg"].astype(np.float32).T),
        eyeB=np.eye(128, dtype=ml_dtypes.bfloat16),
    )
    if x3T is not None:
        m["x3T"] = x3T
    return m


def prep_p3_inputs(inputs, core, x2T=None):
    d = inputs
    cw = d["L1_attn_conv_w"]
    convw = np.ascontiguousarray(
        cw.reshape(16, 128, 4).transpose(1, 2, 0).reshape(128, 64))
    alog = np.ascontiguousarray(
        d["L1_attn_A_log"].reshape(16, 128, 16).transpose(1, 0, 2)
        .reshape(128, 256))
    xpT = np.zeros((2048, 128), np.float32)
    xpT[:, 0:96] = d["L1_attn_x_proj_w"].T
    m = dict(
        inpT=np.ascontiguousarray(d["L1_attn_in_proj_w"].T),
        convw=convw,
        convb=_pack_cols(d["L1_attn_conv_b"]),
        xpT=np.ascontiguousarray(xpT),
        dtT=np.ascontiguousarray(d["L1_attn_dt_proj_w"].T),
        dtb=_pack_cols(d["L1_attn_dt_proj_b"]),
        alog=alog,
        dcol=_pack_cols(d["L1_attn_D"]),
        outT=np.ascontiguousarray(d["L1_attn_out_proj_w"].T),
        c_eye=np.eye(128, dtype=bf16),
    )
    if x2T is not None:
        m["x2T"] = x2T
    return m


def prep_p1_inputs(inputs, core):
    d = inputs
    xT = np.ascontiguousarray(d["L0_attn_x"][core].T.astype(np.float32))
    wqkvT = np.ascontiguousarray(d["L0_attn_w_qkv"].T)
    woutT = np.ascontiguousarray(d["L0_attn_w_out"].T)
    lngb = np.concatenate([_pack_cols(d["L0_attn_ln_g"]),
                           _pack_cols(d["L0_attn_ln_b"])], axis=1)
    m = dict(xT=xT, wqkvT=wqkvT, woutT=woutT,
             lngb=np.ascontiguousarray(lngb))
    m.update(host_consts())
    return m


DEBUG_FUSED = False


def build_fused_program():
    nc = bass.Bass("TRN2", target_bir_lowering=False, debug=False)
    io = {}
    io["xT"] = nc.dram_tensor("xT", [D, T], F32, kind="ExternalInput").ap()
    io["wqkvT"] = nc.dram_tensor("wqkvT", [D, D], F32,
                                 kind="ExternalInput").ap()
    io["woutT"] = nc.dram_tensor("woutT", [512, D], F32,
                                 kind="ExternalInput").ap()
    io["lngb"] = nc.dram_tensor("lngb", [128, 16], F32,
                                kind="ExternalInput").ap()
    for k, v in host_consts().items():
        dt = BF16 if v.dtype == bf16 else F32
        io[k] = nc.dram_tensor(k, list(v.shape), dt,
                               kind="ExternalInput").ap()
    io["w1T"] = nc.dram_tensor("w1T", [D, 4096], F32,
                               kind="ExternalInput").ap()
    io["w2T"] = nc.dram_tensor("w2T", [4096, D], F32,
                               kind="ExternalInput").ap()
    io["b1c"] = nc.dram_tensor("b1c", [128, 32], F32,
                               kind="ExternalInput").ap()
    io["b2c"] = nc.dram_tensor("b2c", [128, 8], F32,
                               kind="ExternalInput").ap()
    io["inpT"] = nc.dram_tensor("inpT", [D, 4096], F32,
                                kind="ExternalInput").ap()
    io["convw"] = nc.dram_tensor("convw", [128, 64], F32,
                                 kind="ExternalInput").ap()
    io["convb"] = nc.dram_tensor("convb", [128, 16], F32,
                                 kind="ExternalInput").ap()
    io["xpT"] = nc.dram_tensor("xpT", [2048, 128], F32,
                               kind="ExternalInput").ap()
    io["dtT"] = nc.dram_tensor("dtT", [64, 2048], F32,
                               kind="ExternalInput").ap()
    io["dtb"] = nc.dram_tensor("dtb", [128, 16], F32,
                               kind="ExternalInput").ap()
    io["alog"] = nc.dram_tensor("alog", [128, 256], F32,
                                kind="ExternalInput").ap()
    io["dcol"] = nc.dram_tensor("dcol", [128, 16], F32,
                                kind="ExternalInput").ap()
    io["outT"] = nc.dram_tensor("outT", [2048, D], F32,
                                kind="ExternalInput").ap()
    io["wrPT"] = nc.dram_tensor("wrPT", [D, 128], F32,
                                kind="ExternalInput").ap()
    io["wmhT"] = nc.dram_tensor("wmhT", [D, D], F32,
                                kind="ExternalInput").ap()
    io["kT"] = nc.dram_tensor("kT", [1024, 4096], F32,
                              kind="ExternalInput").ap()
    io["vT"] = nc.dram_tensor("vT", [32768, 128], F32,
                              kind="ExternalInput").ap()
    io["wmgT"] = nc.dram_tensor("wmgT", [D, D], F32,
                                kind="ExternalInput").ap()
    io["eyeB"] = nc.dram_tensor("eyeB", [128, 128], BF16,
                                kind="ExternalInput").ap()
    kind_i = "ExternalOutput" if DEBUG_FUSED else "Internal"
    if DEBUG_FUSED:
        io["dbg_thr"] = nc.dram_tensor("dbg_thr", [128, 4], F32,
                                       kind="ExternalOutput").ap()
        io["dbg_gs"] = nc.dram_tensor("dbg_gs", [128, 4], F32,
                                      kind="ExternalOutput").ap()
    io["x1T"] = nc.dram_tensor("x1T", [D, T], F32, kind=kind_i).ap()
    io["x2T"] = nc.dram_tensor("x2T", [D, T], F32, kind=kind_i).ap()
    io["x3T"] = nc.dram_tensor("x3T", [D, T], F32, kind=kind_i).ap()
    io["x4T"] = nc.dram_tensor("x4T", [D, T], F32, kind="ExternalOutput").ap()
    with contextlib.ExitStack() as ctx:
        P = Prog(nc, ctx)
        P.make_fences(nc)
        for emitter in (emit_p1, emit_p2, emit_p3):
            with contextlib.ExitStack() as cph:
                _CUR_CTX[0] = cph
                emitter(nc, P, io)
                _CUR_CTX[0] = None
            P.barrier()
        with contextlib.ExitStack() as cph:
            _CUR_CTX[0] = cph
            st = emit_p4(nc, P, io)
            _CUR_CTX[0] = None
            P.finalize(st)
    return nc


def prep_fused_inputs(inputs, core):
    m = prep_p1_inputs(inputs, core)
    m.update(prep_p2_inputs(inputs, core))
    m.update(prep_p3_inputs(inputs, core))
    m.update(prep_p4_inputs(inputs, core))
    return m


def kernel_fused(**inputs):
    cores = list(range(8))
    nc = build_fused_program()
    maps = [prep_fused_inputs(inputs, c) for c in cores]
    r = run_bass_kernel_spmd(nc, maps, core_ids=cores)
    out = np.stack([np.asarray(r.results[c]["x4T"]).T for c in cores])
    return np.ascontiguousarray(out.astype(np.float32))


def kernel(**inputs):
    return kernel_fused(**inputs)


# version 13
# speedup vs baseline: 1.0259x; 1.0189x over previous
import contextlib
import numpy as np
import ml_dtypes
import concourse.bass as bass
import concourse.mybir as mybir
from concourse.bass_utils import run_bass_kernel_spmd

F32 = mybir.dt.float32
F32R = mybir.dt.float32r
BF16 = mybir.dt.bfloat16
AF = mybir.ActivationFunctionType
ALU = mybir.AluOpType
AX = mybir.AxisListType

NCORES = 8
T = 4096
D = 1024
bf16 = ml_dtypes.bfloat16
NEGBF = float(ml_dtypes.finfo(bf16).min)



def _l0_rot_tables():
    dh, ws, jlen = 64, 128, 384
    f32 = np.float32
    inv_freq = 1.0 / (f32(10000.0) ** (np.arange(0, dh, 2, dtype=f32) / f32(dh)))
    t = np.arange(jlen, dtype=f32)
    half = t[:, None] * inv_freq[None, :]
    freqs = np.concatenate([half, half], -1)
    power = (t - jlen // 2) / f32(ws // 2)
    sbase = ((np.arange(0, dh, 2, dtype=f32) + 0.4 * dh) / (1.4 * dh)).astype(f32)
    scale = sbase[None, :] ** power[:, None]
    scale = np.concatenate([scale, scale], -1)
    qf, qs = freqs[-ws:], scale[-ws:]
    inv_s = (1.0 / scale).astype(f32)
    sgn = np.where(np.arange(dh) < dh // 2, f32(-1.0), f32(1.0))[:, None]
    qsc = f32(dh ** -0.5)
    QC = (np.cos(qf) * qs * qsc).T
    QS = (np.sin(qf) * qs * qsc).T * sgn
    KC = (np.cos(freqs) * inv_s).T
    KS = (np.sin(freqs) * inv_s).T * sgn
    QCp = np.concatenate([QC, QC], 0)
    QSp = np.concatenate([QS, QS], 0)
    KCp = np.concatenate([KC, KC], 0)
    KSp = np.concatenate([KS, KS], 0)
    rotq = np.ascontiguousarray(np.concatenate([QCp, QSp], 1)).astype(bf16)
    rotk = np.ascontiguousarray(np.concatenate([KCp, KSp], 1)).astype(bf16)
    return rotq, rotk


def _l0_masks():
    ws = 128
    out = np.zeros((128, 3 * 384), dtype=np.float32)
    for mi, w in enumerate([0, 1, (T // ws) - 1]):
        i = np.arange(128)[:, None]
        j = np.arange(384)[None, :]
        bq_k = (w - 1) * ws + j
        valid = (bq_k >= 0) & (bq_k < T)
        b_t = w * ws + i
        masked = (~valid) | (bq_k - ws > b_t) | (b_t > bq_k + ws)
        out[:, mi * 384:(mi + 1) * 384] = np.where(masked, NEGBF, 0.0)
    return out.astype(bf16)


def _perm128():
    P = np.zeros((128, 128), dtype=bf16)
    for h in range(2):
        for e in range(64):
            P[h * 64 + (e + 32) % 64, h * 64 + e] = 1.0
    return P


def _pack_cols(v, p=128):
    v = np.asarray(v, dtype=np.float32)
    return np.ascontiguousarray(v.reshape(-1, p).T)


def host_consts():
    rotq, rotk = _l0_rot_tables()
    return {
        "c_rotq": rotq,
        "c_rotk": rotk,
        "c_masks": _l0_masks(),
        "c_perm": _perm128(),
        "c_eye": np.eye(128, dtype=bf16),
        "c_ones": np.ones((1, 512), dtype=np.float32),
    }



ENGINES = ("sync", "gpsimd", "vector", "scalar", "tensor")


class Prog:
    def __init__(self, nc, ctx):
        self.nc = nc
        self.q = {e: [] for e in ENGINES}
        self.cnt = {e: 0 for e in ENGINES}
        self.sem = {e: ctx.enter_context(nc.semaphore(f"sem_{e}"))
                    for e in ENGINES}
        self.dsem = {e: ctx.enter_context(nc.semaphore(f"dma_{e}"))
                     for e in ("sync", "gpsimd", "vector", "scalar")}
        self.dcnt = {e: 0 for e in self.dsem}

    def ins(self, eng, fn):
        sem = self.sem[eng]
        self.cnt[eng] += 1
        tgt = self.cnt[eng]

        def run(eobj, fn=fn, sem=sem):
            fn(eobj).then_inc(sem, 1)
        self.q[eng].append(run)
        return (eng, False, tgt)

    def dma(self, eng, out, in_, **kw):
        dsem = self.dsem[eng]
        self.dcnt[eng] += 16
        tgt = self.dcnt[eng]

        def run(eobj, out=out, in_=in_, dsem=dsem, kw=kw):
            eobj.dma_start(out=out, in_=in_, **kw).then_inc(dsem, 16)
        self.q[eng].append(run)
        return (eng, True, tgt)

    def wait(self, eng, *tokens):
        for (peng, isdma, val) in tokens:
            if val <= 0:
                continue
            sem = self.dsem[peng] if isdma else self.sem[peng]
            self.q[eng].append(
                lambda eobj, sem=sem, val=val: eobj.wait_ge(sem, val))

    def make_fences(self, nc):
        if hasattr(self, "_fb"):
            return
        self._fb = {
            "vector": sbuf(nc, "fence_v", [1, 8]),
            "gpsimd": sbuf(nc, "fence_g", [1, 8]),
            "scalar": sbuf(nc, "fence_s", [1, 8]),
        }
        t = self.ins("vector", lambda v: v.memset(self._fb["scalar"], 0.0))
        self.wait("scalar", t)

    def fence(self, eng):
        fb = self._fb[eng]
        if eng == "scalar":
            return self.ins(eng, lambda e, fb=fb: e.copy(fb[0:1, 0:4],
                                                         fb[0:1, 0:4]))
        return self.ins(eng, lambda e, fb=fb: e.memset(fb[0:1, 0:4], 0.0))

    def mark(self, eng):
        return (eng, False, self.cnt[eng])

    def mark_dma(self, eng):
        return (eng, True, self.dcnt[eng])

    def barrier(self):
        if hasattr(self, "_fb"):
            for e in ("vector", "scalar", "gpsimd"):
                self.fence(e)
        toks = ([self.mark(e) for e in ENGINES]
                + [self.mark_dma(e) for e in self.dsem])
        for e in ENGINES:
            self.wait(e, *[t for t in toks if t[1] or t[0] != e])

    def finalize(self, final_store_tokens):
        self.wait("sync", *final_store_tokens)
        with self.nc.Block() as block:
            for eng in ENGINES:
                qq = self.q[eng]
                getattr(block, eng)(lambda eobj, qq=qq: [f(eobj) for f in qq])


_CUR_CTX = [None]


def sbuf(nc, name, shape, dtype=F32):
    if _CUR_CTX[0] is not None:
        return _CUR_CTX[0].enter_context(nc.sbuf_tensor(name, shape, dtype))[:]
    return nc.alloc_sbuf_tensor(name, shape, dtype).ap()


def psum(nc, name, shape, dtype=F32):
    if _CUR_CTX[0] is not None:
        return _CUR_CTX[0].enter_context(nc.psum_tensor(name, shape, dtype))[:]
    return nc.alloc_psum_tensor(name, shape, dtype).ap()



class LnKit:

    def __init__(self, nc, tag, W):
        self.W = W
        self.ra = sbuf(nc, f"{tag}_ra", [2, W])
        self.rb = sbuf(nc, f"{tag}_rb", [2, W])
        self.sq = sbuf(nc, f"{tag}_sq", [128, W], BF16)
        self.xstat = sbuf(nc, f"{tag}_xst", [128, W], BF16)
        self.onesr = sbuf(nc, f"{tag}_ones", [128, 2], BF16)
        self.epsc = sbuf(nc, f"{tag}_eps", [128, 1])
        self.init_tok = None


def emit_ln(P, kit, rr_chunks, psS, psS2, psB1, psB2, ones_row_r,
            out_writer, extra_pe_dep=()):
    nch = len(rr_chunks)
    last_mm = None
    P.wait("tensor", *extra_pe_dep)
    SSC = 1.0 / 128.0
    P.wait("scalar", P.mark("vector"))
    for c in range(nch):
        if last_mm is not None:
            P.wait("scalar", last_mm)
        P.ins("scalar", lambda s, c=c: s.activation(
            kit.xstat, rr_chunks[c], AF.Identity, bias=0.0, scale=SSC))
        tcp = P.fence("scalar")
        P.wait("vector", tcp)
        P.ins("vector", lambda v: v.tensor_tensor(kit.sq, kit.xstat,
                                                  kit.xstat, ALU.mult))
        tsq = P.fence("vector")
        P.wait("tensor", tcp, tsq)
        P.ins("tensor", lambda t, c=c: t.matmul(
            psS, kit.onesr, kit.xstat, start=(c == 0), stop=(c == nch - 1)))
        P.ins("tensor", lambda t, c=c: t.matmul(
            psS2, kit.onesr, kit.sq, start=(c == 0), stop=(c == nch - 1)))
        last_mm = P.mark("tensor")
    tok_mm = P.mark("tensor")
    inv_n = 1.0 / (128 * nch)
    inv_n_m = inv_n / SSC
    inv_n_v = inv_n / (SSC * SSC)
    ra0, rb0 = kit.ra[0:1, :], kit.rb[0:1, :]
    P.wait("vector", tok_mm)
    P.ins("vector", lambda v: v.tensor_scalar_mul(ra0, psS[0:1, :], inv_n_m))
    P.ins("vector", lambda v: v.tensor_scalar_mul(rb0, psS2[0:1, :], inv_n_v))
    P.ins("vector", lambda v: v.tensor_tensor(psS[0:1, :], ra0, ra0, ALU.mult))
    tv = P.ins("vector", lambda v: v.tensor_tensor(
        rb0, rb0, psS[0:1, :], ALU.subtract))
    P.wait("scalar", tv)
    tsd = P.ins("scalar", lambda s: s.activation(
        rb0, rb0, AF.Sqrt, bias=kit.epsc[0:1, 0:1], scale=1.0))
    P.wait("vector", tsd)
    P.ins("vector", lambda v: v.reciprocal(rb0, rb0))
    P.ins("vector", lambda v: v.tensor_tensor(ra0, ra0, rb0, ALU.mult))
    P.ins("vector", lambda v: v.tensor_scalar_mul(ra0, ra0, -1.0))
    tn = P.fence("vector")
    P.wait("tensor", tn)
    P.ins("tensor", lambda t: t.matmul(
        psB1, ones_row_r, kit.rb[:], start=True, stop=True))
    tb = P.ins("tensor", lambda t: t.matmul(
        psB2, ones_row_r, kit.ra[:], start=True, stop=True))
    return tb



DEBUG_P1 = False


def build_p1_program():
    nc = bass.Bass("TRN2", target_bir_lowering=False, debug=False)
    io = {}
    if DEBUG_P1:
        for nm, shp, dt in [("dbg_qk", [512, T], BF16), ("dbg_v", [T, 512], BF16),
                            ("dbg_ao", [512, T], BF16), ("dbg_xh", [D, 512], F32),
                            ("dbg_bw", [128, 8], F32), ("dbg_pb", [128, 512], F32),
                            ("dbg_rstd", [2, 512], F32), ("dbg_S", [2, 512], F32)]:
            io[nm] = nc.dram_tensor(nm, shp, dt, kind="ExternalOutput").ap()
    io["xT"] = nc.dram_tensor("xT", [D, T], F32, kind="ExternalInput").ap()
    io["wqkvT"] = nc.dram_tensor("wqkvT", [D, D], F32, kind="ExternalInput").ap()
    io["woutT"] = nc.dram_tensor("woutT", [512, D], F32, kind="ExternalInput").ap()
    io["lngb"] = nc.dram_tensor("lngb", [128, 16], F32, kind="ExternalInput").ap()
    for k, v in host_consts().items():
        dt = BF16 if v.dtype == bf16 else F32
        io[k] = nc.dram_tensor(k, list(v.shape), dt, kind="ExternalInput").ap()
    io["x1T"] = nc.dram_tensor("x1T", [D, T], F32, kind="ExternalOutput").ap()

    with contextlib.ExitStack() as ctx:
        P = Prog(nc, ctx)
        emit_p1(nc, P, io)
        P.finalize([P.mark_dma("sync")])
    return nc


def emit_p1(nc, P, io, tag="p1"):
    W = 512
    NT = T // W
    NW = T // 128

    wq = [sbuf(nc, f"{tag}_wq{k}", [128, 1024], F32R) for k in range(8)]
    arena = sbuf(nc, f"{tag}_arena", [128, 12032])
    _ab = arena.bitcast(BF16)
    wo = [_ab[:, 16384 + k * 1024:16384 + (k + 1) * 1024] for k in range(4)]
    wraw = arena[:, 4096:5120]
    lngb = sbuf(nc, f"{tag}_lngb", [128, 16])
    lnbr = sbuf(nc, f"{tag}_lnbr", [128, 18], F32R)
    rotq = sbuf(nc, f"{tag}_rotq", [128, 256], BF16)
    rotk = sbuf(nc, f"{tag}_rotk", [128, 768], BF16)
    masks = sbuf(nc, f"{tag}_masks", [128, 1152], BF16)
    perm = sbuf(nc, f"{tag}_perm", [128, 128], BF16)
    eye = sbuf(nc, f"{tag}_eye", [128, 128], BF16)
    ones1r = sbuf(nc, f"{tag}_ones1r", [2, 128])
    bw_cols = sbuf(nc, f"{tag}_bwcols", [128, 8])

    xtile = [arena[:, c * W:(c + 1) * W] for c in range(8)]
    xh8 = sbuf(nc, f"{tag}_xh8", [128, 4096], F32R)
    xhat = [xh8[:, c * W:(c + 1) * W] for c in range(8)]
    qkT = [sbuf(nc, f"{tag}_qk{c}", [128, T], BF16) for c in range(4)]
    pnT = [arena.bitcast(BF16)[:, c * T:(c + 1) * T] for c in range(4)]
    vtok = [sbuf(nc, f"{tag}_vt{j}", [128, 512], BF16) for j in range(NW)]
    aoT = [sbuf(nc, f"{tag}_ao{c}", [128, T], BF16) for c in range(4)]

    _sc = 20480
    bk = [_ab[:, _sc:_sc + 384], _ab[:, _sc + 384:_sc + 768]]
    bkb = [_ab[:, _sc + 768:_sc + 1152], _ab[:, _sc + 1152:_sc + 1536]]
    bq0 = [_ab[:, _sc + 1536:_sc + 1664], _ab[:, _sc + 1664:_sc + 1792]]
    bq1 = [_ab[:, _sc + 1792:_sc + 1920], _ab[:, _sc + 1920:_sc + 2048]]
    ex = [_ab[:, _sc + 2048:_sc + 2816], _ab[:, _sc + 2816:_sc + 3584]]
    aTs = ex
    nmax = [sbuf(nc, f"{tag}_nm{i}", [128, 2]) for i in range(2)]
    rsum = [sbuf(nc, f"{tag}_rs{i}", [128, 2]) for i in range(2)]
    rcp = [sbuf(nc, f"{tag}_rc{i}", [128, 2]) for i in range(2)]

    kit = LnKit(nc, f"{tag}_ln", W)
    P.make_fences(nc)

    pb = [psum(nc, f"{tag}_pb{i}", [128, 512]) for i in range(8)]
    psS = pb[0][0:2, :]
    psS2 = pb[1][0:2, :]
    psB1 = pb[2]
    psB2 = pb[3]
    ps_qk = pb[4]
    ps_v = pb[5]
    ps_s = [pb[0][:, 0:384], pb[1][:, 0:384]]
    ps_s2 = [pb[2][:, 0:384], pb[3][:, 0:384]]
    ps_t = [pb[4].bitcast(BF16)[:, 0:768], pb[5].bitcast(BF16)[:, 0:768]]
    ps_o = [pb[6][:, 0:128], pb[7][:, 0:128]]
    ps_c = pb[7]

    P.dma("sync", lngb[:], io["lngb"])
    for nm, dst in [("c_rotq", rotq), ("c_rotk", rotk), ("c_masks", masks),
                    ("c_perm", perm), ("c_eye", eye)]:
        P.dma("sync", dst[:], io[nm])
    tk_lngb = P.mark_dma("sync")
    P.wait("vector", tk_lngb)
    P.ins("vector", lambda v: v.memset(lnbr[:, 16:18].bitcast(F32), 0.0))
    P.ins("vector", lambda v: v.memset(kit.epsc, 1e-5))
    P.ins("vector", lambda v: v.memset(kit.onesr, 1.0))
    P.ins("vector", lambda v: v.memset(ones1r[:], 1.0))
    P.ins("vector", lambda v: v.memset(kit.ra, 0.0))
    P.ins("vector", lambda v: v.memset(kit.rb, 0.0))
    tk_lnbr = P.ins("vector", lambda v: v.tensor_copy(lnbr[:, 0:16],
                                                      lngb[:]))

    rhs_pad = arena[:, 5120:5248]
    P.ins("vector", lambda v: v.memset(rhs_pad, 0.0))
    t_prev_use = None
    for k in range(8):
        tl = P.dma("sync", wraw[:], io["wqkvT"][k * 128:(k + 1) * 128, :])
        P.wait("vector", tl, t_prev_use or tl)
        tgw = P.ins("vector", lambda v, k=k: v.tensor_scalar_mul(
            wq[k][:], wraw[:], lngb[:, k:k + 1]))
        P.ins("vector", lambda v, k=k: v.tensor_copy(
            rhs_pad[:, 0:1], lnbr[:, 8 + k:9 + k]))
        trr = P.fence("vector")
        P.wait("tensor", trr, tk_lnbr)
        for oc in range(8):
            P.ins("tensor", lambda t, k=k, oc=oc: t.matmul(
                pb[oc][:, 0:128],
                wraw[:, oc * 128:(oc + 1) * 128], rhs_pad,
                start=(k == 0), stop=(k == 7)))
        t_prev_use = P.mark("tensor")
        if k < 7:
            P.wait("sync", trr, t_prev_use)
    P.wait("vector", t_prev_use)
    for oc in range(8):
        P.ins("vector", lambda v, oc=oc: v.tensor_copy(
            bw_cols[:, oc:oc + 1], pb[oc][:, 0:1]))
    t_bw = P.fence("vector")
    P.barrier()

    for tt in range(NT):
        t0 = tt * W
        for c in range(8):
            P.dma("sync", xtile[c][:], io["xT"][c * 128:(c + 1) * 128,
                                                t0:t0 + W])
        ld_all = P.mark_dma("sync")
        P.wait("vector", ld_all)
        tc = []
        for c in range(8):
            tc.append(P.ins("vector", lambda v, c=c: v.tensor_copy(
                xhat[c][:], xtile[c][:])))
        last_mm = None
        for c in range(8):
            if last_mm is not None:
                P.wait("vector", last_mm)
            P.ins("vector", lambda v, c=c: v.tensor_copy(
                kit.xstat, xtile[c][:]))
            P.ins("vector", lambda v: v.tensor_tensor(
                kit.sq, kit.xstat, kit.xstat, ALU.mult))
            tf = P.fence("vector")
            P.wait("tensor", tf)
            P.ins("tensor", lambda t, c=c: t.matmul(
                psS, kit.onesr, kit.xstat, start=(c == 0), stop=(c == 7)))
            P.ins("tensor", lambda t, c=c: t.matmul(
                psS2, kit.onesr, kit.sq, start=(c == 0), stop=(c == 7)))
            last_mm = P.mark("tensor")
        inv_n = 1.0 / D
        ra0, rb0 = kit.ra[0:1, :], kit.rb[0:1, :]
        P.wait("vector", last_mm)
        if io.get("dbg_S") is not None and tt == 0:
            t1_ = P.ins("vector", lambda v: v.tensor_copy(
                arena[0:1, 6144:6656], psS[0:1, :]))
            t2_ = P.ins("vector", lambda v: v.tensor_copy(
                arena[0:1, 6656:7168], psS2[0:1, :]))
            P.wait("sync", t2_)
            P.dma("sync", io["dbg_S"][0:1, :], arena[0:1, 6144:6656])
            P.dma("sync", io["dbg_S"][1:2, :], arena[0:1, 6656:7168])
        P.ins("vector", lambda v: v.tensor_scalar_mul(ra0, psS[0:1, :], inv_n))
        P.ins("vector", lambda v: v.tensor_scalar_mul(rb0, psS2[0:1, :], inv_n))
        P.ins("vector", lambda v: v.tensor_tensor(psS[0:1, :], ra0, ra0,
                                                  ALU.mult))
        tv = P.ins("vector", lambda v: v.tensor_tensor(
            rb0, rb0, psS[0:1, :], ALU.subtract))
        P.wait("scalar", tv)
        tsd = P.ins("scalar", lambda s: s.activation(
            rb0, rb0, AF.Sqrt, bias=kit.epsc[0:1, 0:1], scale=1.0))
        P.wait("vector", tsd)
        P.ins("vector", lambda v: v.reciprocal(rb0, rb0))
        P.ins("vector", lambda v: v.tensor_tensor(ra0, ra0, rb0, ALU.mult))
        P.ins("vector", lambda v: v.tensor_scalar_mul(ra0, ra0, -1.0))
        tn = P.fence("vector")
        P.wait("tensor", tn)
        P.ins("tensor", lambda t: t.matmul(
            psB1, ones1r[:], kit.rb[:], start=True, stop=True))
        tb = P.ins("tensor", lambda t: t.matmul(
            psB2, ones1r[:], kit.ra[:], start=True, stop=True))
        P.wait("vector", tb)
        th = []
        for c in range(8):
            P.ins("vector", lambda v, c=c: v.tensor_tensor(
                xhat[c][:], xtile[c][:], psB1, ALU.mult))
            th.append(P.ins("vector", lambda v, c=c: v.tensor_tensor(
                xhat[c][:], xhat[c][:], psB2, ALU.add)))
        th.append(P.fence("vector"))
        P.wait("tensor", th[-1], t_bw)
        if io.get("dbg_xh") is not None and tt == 0:
            P.wait("sync", th[-1], t_bw)
            for c in range(8):
                P.dma("sync", io["dbg_xh"][c * 128:(c + 1) * 128, :],
                      xhat[c].bitcast(F32))
            P.dma("sync", io["dbg_bw"], bw_cols[:])
            tpb = P.ins("vector", lambda v: v.tensor_copy(
                arena[:, 5632:6144], psB1))
            P.wait("sync", tpb)
            P.dma("sync", io["dbg_pb"], arena[:, 5632:6144])
            P.dma("sync", io["dbg_rstd"], kit.rb[:])
        for oc in range(4):
            for k in range(8):
                P.ins("tensor", lambda t, oc=oc, k=k: t.matmul(
                    ps_qk, wq[k][:, oc * 128:(oc + 1) * 128], xhat[k][:],
                    start=(k == 0), stop=(k == 7)))
            tm = P.mark("tensor")
            P.wait("scalar", tm, t_bw)
            te = P.ins("scalar", lambda s, oc=oc, t0=t0: s.activation(
                qkT[oc][:, t0:t0 + W], ps_qk, AF.Identity,
                bias=bw_cols[:, oc:oc + 1], scale=1.0))
            P.wait("tensor", te)
        for sj in range(4):
            wnd = tt * 4 + sj
            for k in range(8):
                P.ins("tensor", lambda t, sj=sj, k=k: t.matmul(
                    ps_v, xhat[k][:, sj * 128:(sj + 1) * 128],
                    wq[k][:, 512:1024], start=(k == 0), stop=(k == 7)))
            tm = P.mark("tensor")
            P.wait("scalar", tm)
            te = P.ins("scalar", lambda s, wnd=wnd: s.copy(vtok[wnd][:], ps_v))
            P.wait("tensor", te)
        P.barrier()

    wstage = arena[:, 5120:5632]
    for k in range(4):
        for hh in range(2):
            tl = P.dma("sync", wstage,
                       io["woutT"][k * 128:(k + 1) * 128,
                                   hh * 512:(hh + 1) * 512])
            P.wait("vector", tl)
            tw = P.ins("vector", lambda v, k=k, hh=hh: v.tensor_copy(
                wo[k][:, hh * 512:(hh + 1) * 512], wstage))
            P.wait("sync", tw)
    P.barrier()

    for c in range(4):
        for j8 in range(8):
            j0 = j8 * 512
            tm = P.ins("tensor", lambda t, c=c, j0=j0: t.matmul(
                ps_qk, perm[:], qkT[c][:, j0:j0 + 512], start=True, stop=True))
            P.wait("scalar", tm)
            te = P.ins("scalar", lambda s, c=c, j0=j0: s.copy(
                pnT[c][:, j0:j0 + 512], ps_qk))
            P.wait("tensor", te)
    P.barrier()

    P.ins("vector", lambda v: v.memset(bq0[0][:], 0.0))
    P.ins("vector", lambda v: v.memset(bq0[1][:], 0.0))
    P.ins("vector", lambda v: v.memset(bq1[0][:], 0.0))
    tz = P.ins("vector", lambda v: v.memset(bq1[1][:], 0.0))
    P.barrier()

    iters = [(hp, w) for hp in range(4) for w in range(NW)]
    last_pe_scores = [None, None]
    last_act_exp = [None, None]
    last_dve_scale = [None, None]
    last_pe_tr = [None, None]
    last_act_evat = [None, None]
    last_pe_av = [None, None]
    last_act_evo = [None, None]
    pss = [ps_s, ps_s2]

    for it, (hp, w) in enumerate(iters):
        p = it % 2
        j0 = (w - 1) * 128
        cs = 128 if w == 0 else 0
        ce = 256 if w == NW - 1 else 384
        mi = 0 if w == 0 else (2 if w == NW - 1 else 1)
        edge = (w == 0 or w == NW - 1)
        a0 = j0 + cs
        ln = ce - cs

        if last_pe_scores[p] is not None:
            P.wait("vector", last_pe_scores[p])
        if edge:
            P.ins("vector", lambda v, p=p: v.memset(bk[p][:], 0.0))
        P.ins("vector", lambda v, p=p, hp=hp, a0=a0, cs=cs, ln=ln:
              v.tensor_tensor(bk[p][:, cs:cs + ln], qkT[hp][:, a0:a0 + ln],
                              rotk[:, cs:cs + ln], ALU.mult))
        P.ins("vector", lambda v, p=p, hp=hp, a0=a0, cs=cs, ln=ln:
              v.tensor_tensor(bkb[p][:, cs:cs + ln], pnT[hp][:, a0:a0 + ln],
                              rotk[:, 384 + cs:384 + cs + ln], ALU.mult))
        tbk = P.ins("vector", lambda v, p=p, cs=cs, ln=ln: v.tensor_tensor(
            bk[p][:, cs:cs + ln], bk[p][:, cs:cs + ln],
            bkb[p][:, cs:cs + ln], ALU.add))
        c0 = w * 128
        P.ins("vector", lambda v, p=p, hp=hp, c0=c0: v.tensor_tensor(
            bq0[p][0:64, :], qkT[hp][0:64, c0:c0 + 128], rotq[0:64, 0:128],
            ALU.mult))
        P.ins("vector", lambda v, p=p, hp=hp, c0=c0: v.tensor_tensor(
            bkb[p][0:64, 0:128], pnT[hp][0:64, c0:c0 + 128],
            rotq[0:64, 128:256], ALU.mult))
        tq0 = P.ins("vector", lambda v, p=p: v.tensor_tensor(
            bq0[p][0:64, :], bq0[p][0:64, :], bkb[p][0:64, 0:128], ALU.add))
        P.ins("vector", lambda v, p=p, hp=hp, c0=c0: v.tensor_tensor(
            bq1[p][64:128, :], qkT[hp][64:128, c0:c0 + 128],
            rotq[64:128, 0:128], ALU.mult))
        P.ins("vector", lambda v, p=p, hp=hp, c0=c0: v.tensor_tensor(
            bkb[p][64:128, 0:128], pnT[hp][64:128, c0:c0 + 128],
            rotq[64:128, 128:256], ALU.mult))
        tq1 = P.ins("vector", lambda v, p=p: v.tensor_tensor(
            bq1[p][64:128, :], bq1[p][64:128, :], bkb[p][64:128, 0:128],
            ALU.add))
        tq1 = P.fence("vector")

        P.wait("tensor", tbk, tq0, tq1)
        if last_act_exp[p] is not None:
            P.wait("tensor", last_act_exp[p])
        for h, bqt in ((0, bq0), (1, bq1)):
            P.ins("tensor", lambda t, p=p, h=h, bqt=bqt: t.matmul(
                pss[p][h], bqt[p][:], bk[p][:], start=True, stop=False))
            P.ins("tensor", lambda t, p=p, h=h, mi=mi: t.matmul(
                pss[p][h], eye[:], masks[:, mi * 384:(mi + 1) * 384],
                start=False, stop=True))
        last_pe_scores[p] = P.mark("tensor")

        P.wait("vector", last_pe_scores[p])
        tm0 = P.ins("vector", lambda v, p=p: v.reduce_max(
            nmax[p][:, 0:1], pss[p][0], AX.X, negate=True))
        tm1 = P.ins("vector", lambda v, p=p: v.reduce_max(
            nmax[p][:, 1:2], pss[p][1], AX.X, negate=True))
        tm1 = P.fence("vector")
        P.wait("scalar", tm0, tm1)
        if last_pe_av[p] is not None:
            P.wait("scalar", last_pe_av[p])
        te0 = P.ins("scalar", lambda s, p=p: s.activation(
            ex[p][:, 0:384], pss[p][0], AF.Exp, bias=nmax[p][:, 0:1],
            scale=1.0, accum_out=rsum[p][:, 0:1]))
        te1 = P.ins("scalar", lambda s, p=p: s.activation(
            ex[p][:, 384:768], pss[p][1], AF.Exp, bias=nmax[p][:, 1:2],
            scale=1.0, accum_out=rsum[p][:, 1:2]))
        tfx = P.fence("scalar")
        last_act_exp[p] = P.mark("scalar")
        P.wait("vector", tfx)
        trc = P.ins("vector", lambda v, p=p: v.reciprocal(rcp[p][:], rsum[p][:]))
        P.wait("vector", trc)
        if last_pe_tr[p] is not None:
            P.wait("vector", last_pe_tr[p])
        P.ins("vector", lambda v, p=p: v.tensor_scalar_mul(
            ex[p][:, 0:384], ex[p][:, 0:384], rcp[p][:, 0:1]))
        tat = P.ins("vector", lambda v, p=p: v.tensor_scalar_mul(
            ex[p][:, 384:768], ex[p][:, 384:768], rcp[p][:, 1:2]))
        tat = P.fence("vector")
        last_dve_scale[p] = tat

        P.wait("tensor", tat)
        if last_act_evat[p] is not None:
            P.wait("tensor", last_act_evat[p])
        jcs = [jc for jc in range(3) if cs <= jc * 128 < ce]
        for h in (0, 1):
            for jc in jcs:
                P.ins("tensor", lambda t, p=p, h=h, jc=jc: t.transpose(
                    ps_t[p][:, h * 384 + jc * 128: h * 384 + jc * 128 + 128],
                    ex[p][:, h * 384 + jc * 128: h * 384 + jc * 128 + 128],
                    eye[:]))
        last_pe_tr[p] = P.mark("tensor")
        P.wait("scalar", last_pe_tr[p])
        if last_pe_av[p] is not None:
            P.wait("scalar", last_pe_av[p])
        tev = P.ins("scalar", lambda s, p=p: s.copy(aTs[p][:], ps_t[p][:]))
        tev = P.fence("scalar")
        last_act_evat[p] = tev

        P.wait("tensor", tev)
        if last_act_evo[p] is not None:
            P.wait("tensor", last_act_evo[p])
        for h in (0, 1):
            for ji, jc in enumerate(jcs):
                P.ins("tensor",
                      lambda t, p=p, h=h, jc=jc, w=w, hp=hp, ji=ji,
                      nj=len(jcs): t.matmul(
                          ps_o[p][h * 64:(h + 1) * 64, :],
                          vtok[w - 1 + jc][:, hp * 128 + h * 64:
                                           hp * 128 + h * 64 + 64],
                          aTs[p][:, h * 384 + jc * 128:h * 384 + jc * 128 + 128],
                          start=(ji == 0), stop=(ji == nj - 1)))
        last_pe_av[p] = P.mark("tensor")
        P.wait("scalar", last_pe_av[p])
        teo = P.ins("scalar", lambda s, hp=hp, w=w, p=p: s.activation(
            aoT[hp][:, w * 128:(w + 1) * 128], ps_o[p], AF.Identity,
            bias=bw_cols[:, 4 + hp:5 + hp], scale=1.0))
        last_act_evo[p] = teo

    P.barrier()

    if io.get("dbg_qk") is not None:
        for c in range(4):
            P.dma("sync", io["dbg_qk"][c * 128:(c + 1) * 128, :], qkT[c])
            P.dma("sync", io["dbg_ao"][c * 128:(c + 1) * 128, :], aoT[c])
        for j in range(NW):
            P.dma("sync", io["dbg_v"][j * 128:(j + 1) * 128, :], vtok[j][:])
        P.barrier()

    for tt in range(NT):
        t0 = tt * W
        for c in range(8):
            P.dma("sync", xtile[c][:], io["xT"][c * 128:(c + 1) * 128,
                                                t0:t0 + W])
        ld_all = P.mark_dma("sync")
        P.wait("vector", ld_all)
        for oc in range(8):
            for k in range(4):
                P.ins("tensor", lambda t, oc=oc, k=k, t0=t0: t.matmul(
                    ps_c, wo[k][:, oc * 128:(oc + 1) * 128],
                    aoT[k][:, t0:t0 + W], start=(k == 0), stop=(k == 3)))
            tm = P.mark("tensor")
            P.wait("vector", tm)
            ta = P.ins("vector", lambda v, oc=oc: v.tensor_tensor(
                xtile[oc][:], ps_c, xtile[oc][:], ALU.add))
            tc = P.fence("vector")
            P.ins("vector", lambda v, oc=oc: v.tensor_copy(
                xhat[oc][:], xtile[oc][:]))
            tc = P.fence("vector")
            P.wait("tensor", tc)
        tb = emit_ln(P, kit, [xh[:] for xh in xhat], psS, psS2, psB1, psB2,
                     ones1r[:], None)
        P.wait("vector", tb)
        st_toks = []
        for c in range(8):
            P.ins("vector", lambda v, c=c: v.tensor_tensor(
                xtile[c][:], xhat[c][:], psB1, ALU.mult))
            tx = P.ins("vector", lambda v, c=c: v.tensor_tensor(
                xtile[c][:], xtile[c][:], psB2, ALU.add))
            tx = P.fence("vector")
            P.wait("sync", tx)
            st_toks.append(P.dma("sync", io["x1T"][c * 128:(c + 1) * 128,
                                                   t0:t0 + W], xtile[c][:]))
        P.wait("vector", *st_toks)
        P.barrier()






DEBUG_P2 = False
DEBUG_P2W = False


def build_p2_program():
    nc = bass.Bass("TRN2", target_bir_lowering=False, debug=False)
    io = {}
    if DEBUG_P2W:
        io["dbg_w1q"] = nc.dram_tensor("dbg_w1q", [128, 4096], BF16,
                                       kind="ExternalOutput").ap()
        io["dbg_w2q"] = nc.dram_tensor("dbg_w2q", [128, 1024], BF16,
                                       kind="ExternalOutput").ap()
        io["dbg_w1qb"] = nc.dram_tensor("dbg_w1qb", [128, 4096], BF16,
                                        kind="ExternalOutput").ap()
        io["dbg_thr"] = nc.dram_tensor("dbg_thr", [128, 4], F32,
                                       kind="ExternalOutput").ap()
        io["dbg_hall"] = nc.dram_tensor("dbg_hall", [4096, 512], BF16,
                                        kind="ExternalOutput").ap()
        io["dbg_x1b"] = nc.dram_tensor("dbg_x1b", [1024, 512], BF16,
                                       kind="ExternalOutput").ap()
        io["dbg_gs"] = nc.dram_tensor("dbg_gs", [128, 4], F32,
                                      kind="ExternalOutput").ap()
    if DEBUG_P2:
        io["dbg_thr"] = nc.dram_tensor("dbg_thr", [128, 4], F32,
                                       kind="ExternalOutput").ap()
        io["dbg_w1q"] = nc.dram_tensor("dbg_w1q", [128, 4096], BF16,
                                       kind="ExternalOutput").ap()
        io["dbg_h"] = nc.dram_tensor("dbg_h", [128, 512], BF16,
                                     kind="ExternalOutput").ap()
        io["dbg_ps"] = nc.dram_tensor("dbg_ps", [128, 512], F32,
                                      kind="ExternalOutput").ap()
        io["dbg_g"] = nc.dram_tensor("dbg_g", [128, 2], F32,
                                     kind="ExternalOutput").ap()
        io["dbg_hall"] = nc.dram_tensor("dbg_hall", [4096, 512], BF16,
                                        kind="ExternalOutput").ap()
        io["dbg_r"] = nc.dram_tensor("dbg_r", [1024, 512], F32,
                                     kind="ExternalOutput").ap()
        io["dbg_w2q"] = nc.dram_tensor("dbg_w2q", [128, 1024], BF16,
                                       kind="ExternalOutput").ap()
        io["dbg_pso"] = nc.dram_tensor("dbg_pso", [128, 512], F32,
                                       kind="ExternalOutput").ap()
        io["dbg_x1b"] = nc.dram_tensor("dbg_x1b", [128, 512], BF16,
                                       kind="ExternalOutput").ap()
    io["x1T"] = nc.dram_tensor("x1T", [D, T], F32, kind="ExternalInput").ap()
    io["w1T"] = nc.dram_tensor("w1T", [D, 4096], F32, kind="ExternalInput").ap()
    io["w2T"] = nc.dram_tensor("w2T", [4096, D], F32, kind="ExternalInput").ap()
    io["b1c"] = nc.dram_tensor("b1c", [128, 32], F32, kind="ExternalInput").ap()
    io["b2c"] = nc.dram_tensor("b2c", [128, 8], F32, kind="ExternalInput").ap()
    io["x2T"] = nc.dram_tensor("x2T", [D, T], F32, kind="ExternalOutput").ap()
    with contextlib.ExitStack() as ctx:
        P = Prog(nc, ctx)
        emit_p2(nc, P, io)
        P.finalize([P.mark_dma("sync")])
    return nc


def emit_p2(nc, P, io, tag="p2"):
    W = 512
    NT = T // W
    DF = 4096
    g_dram = nc.dram_tensor(f"{tag}_gd", [2, 128], F32, kind="Internal").ap()

    w1q = [sbuf(nc, f"{tag}_w1q{k}", [128, DF], BF16) for k in range(8)]
    w2q = [sbuf(nc, f"{tag}_w2q{f}", [128, D], BF16) for f in range(32)]
    arena = sbuf(nc, f"{tag}_arena", [128, 4096])
    wstage = arena[:, 0:2048]
    nscr = arena[:, 2048:4096]
    hb_ = sbuf(nc, f"{tag}_hb", [128, 32 * W], BF16)
    hbf = [hb_[:, f * W:(f + 1) * W] for f in range(32)]
    x1t = [sbuf(nc, f"{tag}_x1_{c}", [128, W]) for c in range(8)]
    x1b = [sbuf(nc, f"{tag}_x1b{c}", [128, W], BF16) for c in range(8)]
    b1c = sbuf(nc, f"{tag}_b1c", [128, 32])
    b2c = sbuf(nc, f"{tag}_b2c", [128, 8])
    gcols = sbuf(nc, f"{tag}_gcols", [128, 2])
    gtmp = sbuf(nc, f"{tag}_gtmp", [128, 2])
    gacc = sbuf(nc, f"{tag}_gacc", [128, 48])
    gpad = hb_.bitcast(F32)[:, 0:512]
    thr4 = sbuf(nc, f"{tag}_thr4", [128, 4])
    trow = hb_.bitcast(F32)[0:2, 512:1024]
    ones1r = sbuf(nc, f"{tag}_ones1r", [2, 128])
    onescol = sbuf(nc, f"{tag}_onesc", [128, 2])
    kit = LnKit(nc, f"{tag}_ln", W)
    dbgbuf = nscr[:, 0:512]
    P.make_fences(nc)

    pb = [psum(nc, f"{tag}_pb{i}", [128, 512]) for i in range(8)]
    psS = pb[0][0:2, :]
    psS2 = pb[1][0:2, :]
    psB1 = pb[2]
    psB2 = pb[3]
    ps_h = [pb[6], pb[7]]
    ps_o = [pb[4], pb[4]]

    tb1 = P.dma("sync", b1c[:], io["b1c"])
    tb2 = P.dma("sync", b2c[:], io["b2c"])
    P.ins("vector", lambda v: v.memset(kit.epsc, 1e-5))
    P.ins("vector", lambda v: v.memset(kit.onesr, 1.0))
    P.ins("vector", lambda v: v.memset(ones1r[:], 1.0))
    P.ins("vector", lambda v: v.memset(onescol[:], 1.0))
    P.ins("vector", lambda v: v.memset(kit.ra, 0.0))
    P.ins("vector", lambda v: v.memset(kit.rb, 0.0))
    P.ins("vector", lambda v: v.memset(gcols[:], 0.0))
    P.ins("vector", lambda v: v.memset(gpad[:], 0.0))
    P.ins("vector", lambda v: v.memset(trow[:], 0.0))

    gpi = [0]
    wping = [wstage, nscr]

    def gamma_stream(jobs):
        cons = {0: None, 1: None}
        for i, (src, wdt) in enumerate(jobs):
            pp = i % 2
            if cons[pp] is not None:
                P.wait("sync", cons[pp])
            P.dma("sync", wping[pp][:, 0:wdt], src)
            tl = P.mark_dma("sync")
            P.wait("vector", tl)
            gi = gpi[0]
            cons[pp] = P.ins("vector", lambda v, wdt=wdt, gi=gi, pp=pp:
                             v.reduce_sum(gacc[:, gi:gi + 1],
                                          wping[pp][:, 0:wdt], AX.X,
                                          apply_absolute_value=True))
            gpi[0] += 1

    def wjobs(dram, nrows, ncols):
        out = []
        for k in range(nrows // 128):
            for h0 in range(0, ncols, 2048):
                wdt = min(2048, ncols - h0)
                out.append((dram[k * 128:(k + 1) * 128, h0:h0 + wdt], wdt))
        return out
    gamma_stream(wjobs(io["w1T"], D, DF) + wjobs(io["w2T"], DF, D))
    P.wait("vector", P.mark("vector"))
    tg1 = P.ins("vector", lambda v: v.reduce_sum(gpad[:, 0:1], gacc[:, 0:16],
                                                 AX.X))
    P.wait("vector", tg1)
    tg2 = P.ins("vector", lambda v: v.reduce_sum(gpad[:, 1:2], gacc[:, 16:48],
                                                 AX.X))
    P.wait("vector", tg2)
    tf = P.fence("vector")
    P.wait("tensor", tf)
    tg = P.ins("tensor", lambda t: t.matmul(pb[5][0:2, 0:2], onescol[:],
                                            gpad[:, 0:2], start=True,
                                            stop=True))
    P.wait("vector", tg)
    nw1 = float(D * DF)
    nw2 = float(DF * D)
    t0_ = P.ins("vector", lambda v: v.tensor_scalar(
        trow[0:1, 0:1], pb[5][0:1, 0:1], 0.5 / nw1, 0.5e-5, ALU.mult,
        ALU.add))
    P.wait("vector", t0_)
    t1_ = P.ins("vector", lambda v: v.tensor_scalar(
        trow[0:1, 2:3], pb[5][0:1, 1:2], 0.5 / nw2, 0.5e-5, ALU.mult,
        ALU.add))
    P.wait("vector", t1_)
    tr1 = P.ins("vector", lambda v: v.tensor_scalar_mul(
        trow[0:1, 1:2], trow[0:1, 0:1], -1.0))
    P.wait("vector", tr1)
    tr2 = P.ins("vector", lambda v: v.tensor_scalar_mul(
        trow[0:1, 3:4], trow[0:1, 2:3], -1.0))
    P.wait("vector", tr2)
    tf = P.fence("vector")
    P.wait("tensor", tf)
    tb = P.ins("tensor", lambda t: t.matmul(psB1, ones1r[:], trow[:],
                                            start=True, stop=True))
    P.wait("vector", tb)
    if io.get("dbg_gs") is not None:
        tdg = P.ins("vector", lambda v: v.tensor_copy(gcols[:], gpad[:, 0:2]))
        tps = P.ins("vector", lambda v: v.tensor_copy(gtmp[0:1, 0:2],
                                                      pb[5][0:1, 0:2]))
        P.wait("sync", tdg, tps)
        P.dma("sync", io["dbg_gs"][:, 0:2], gcols[:])
        P.dma("sync", io["dbg_gs"][0:1, 2:4], gtmp[0:1, 0:2])
    tthr = P.ins("vector", lambda v: v.tensor_copy(thr4[:], psB1[:, 0:4]))

    def quant_stream(jobs):
        cons = {0: None, 1: None}
        for i, (src, wdt, tcol, dst) in enumerate(jobs):
            pp = i % 2
            if cons[pp] is not None:
                P.wait("sync", cons[pp])
            P.dma("sync", wping[pp][:, 0:wdt], src)
            tl = P.mark_dma("sync")
            P.wait("vector", tl, tthr)
            P.ins("vector", lambda v, wdt=wdt, tcol=tcol, pp=pp, dst=dst:
                  v.tensor_single_scalar(
                      dst, wping[pp][:, 0:wdt],
                      thr4[:, tcol + 1:tcol + 2], ALU.is_le))
            cons[pp] = P.ins("vector", lambda v, wdt=wdt, tcol=tcol,
                             dst=dst, pp=pp: v.scalar_tensor_tensor(
                                 dst, wping[pp][:, 0:wdt],
                                 thr4[:, tcol:tcol + 1], dst,
                                 ALU.is_ge, ALU.subtract))

    def qjobs(dram, nrows, ncols, tcol, dsts):
        out = []
        for k in range(nrows // 128):
            for h0 in range(0, ncols, 2048):
                wdt = min(2048, ncols - h0)
                out.append((dram[k * 128:(k + 1) * 128, h0:h0 + wdt], wdt,
                            tcol, dsts(k, h0, wdt)))
        return out
    quant_stream(qjobs(io["w1T"], D, DF, 0,
                       lambda k, h0, wdt: w1q[k][:, h0:h0 + wdt])
                 + qjobs(io["w2T"], DF, D, 2,
                         lambda k, h0, wdt: w2q[k][:, h0:h0 + wdt]))
    P.wait("tensor", P.fence("vector"))
    P.barrier()
    if io.get("dbg_thr") is not None:
        P.dma("sync", io["dbg_thr"], thr4[:])
        P.barrier()

    tbm = P.mark_dma("sync")
    P.wait("scalar", tbm)
    P.wait("vector", tbm)
    for tt in range(NT):
        t0 = tt * W
        for c in range(8):
            P.dma("sync", x1t[c][:], io["x1T"][c * 128:(c + 1) * 128,
                                               t0:t0 + W])
        ld_all = P.mark_dma("sync")
        P.wait("scalar", ld_all)
        for c in range(8):
            P.ins("scalar", lambda s, c=c: s.copy(x1b[c][:], x1t[c][:]))
        tf = P.fence("scalar")
        P.wait("tensor", tf)
        ev_prev = None
        for fc in range(32):
            pp = ps_h[fc % 2]
            if fc >= 1:
                P.wait("tensor", ev_prev[fc - 1])
            P.ins("tensor", lambda t: t.nop())
            P.ins("tensor", lambda t: t.nop())
            for k in range(8):
                P.ins("tensor", lambda t, fc=fc, k=k, pp=pp: t.matmul(
                    pp, w1q[k][:, fc * 128:(fc + 1) * 128], x1b[k][:],
                    start=(k == 0), stop=(k == 7)))
            tm = P.mark("tensor")
            if io.get("dbg_ps") is not None and tt == 5 and fc == 0:
                P.wait("vector", tm)
                tdd = P.ins("vector", lambda v, pp=pp: v.tensor_copy(
                    dbgbuf, pp))
                P.wait("sync", tdd)
                P.dma("sync", io["dbg_ps"], dbgbuf)
                P.dma("sync", io["dbg_x1b"], x1b[0][:])
                P.wait("scalar", tdd)
            P.wait("scalar", tm)
            te = P.ins("scalar", lambda s, fc=fc, pp=pp: s.activation(
                hbf[fc], pp, AF.Relu, bias=b1c[:, fc:fc + 1], scale=1.0))
            if ev_prev is None:
                ev_prev = {}
            ev_prev[fc] = te
        tfh = P.fence("scalar")
        if io.get("dbg_h") is not None and tt == 5:
            P.wait("sync", tfh)
            P.dma("sync", io["dbg_h"], hbf[0])
        P.wait("tensor", tfh)
        if io.get("dbg_hall") is not None and tt == 5:
            tfh2 = P.fence("scalar")
            P.wait("sync", tfh2)
            for f in range(32):
                P.dma("sync", io["dbg_hall"][f * 128:(f + 1) * 128, :],
                      hbf[f])
        evo = {}
        for oc in range(8):
            pp = ps_o[oc % 2]
            if oc >= 1:
                P.wait("tensor", evo[oc - 1])
            for f in range(32):
                if f % 8 == 0:
                    P.ins("tensor", lambda t: t.nop())
                    P.ins("tensor", lambda t: t.nop())
                P.ins("tensor", lambda t, oc=oc, f=f, pp=pp: t.matmul(
                    pp, w2q[f][:, oc * 128:(oc + 1) * 128], hbf[f],
                    start=(f == 0), stop=(f == 31)))
            tm = P.mark("tensor")
            if io.get("dbg_pso") is not None and tt == 5 and oc == 0:
                P.barrier()
                td2 = P.ins("vector", lambda v, pp=pp: v.tensor_copy(
                    dbgbuf, pp))
                P.wait("sync", td2)
                P.dma("sync", io["dbg_pso"], dbgbuf)
                P.dma("sync", io["dbg_w2q"], w2q[0][:])
                P.barrier()
            P.wait("vector", tm)
            P.ins("vector", lambda v, oc=oc, pp=pp: v.tensor_tensor(
                x1t[oc][:], pp, x1t[oc][:], ALU.add))
            P.ins("vector", lambda v, oc=oc: v.tensor_scalar_add(
                x1t[oc][:], x1t[oc][:], b2c[:, oc:oc + 1]))
            evo[oc] = P.fence("vector")
        if io.get("dbg_r") is not None and tt == 5:
            tfr = P.fence("vector")
            P.wait("sync", tfr)
            for c in range(8):
                P.dma("sync", io["dbg_r"][c * 128:(c + 1) * 128, :],
                      x1t[c][:])
        tb = emit_ln(P, kit, [x[:] for x in x1t], psS, psS2, psB1, psB2,
                     ones1r[:], None)
        P.wait("vector", tb)
        st_toks = []
        for c in range(8):
            P.ins("vector", lambda v, c=c: v.tensor_tensor(
                x1t[c][:], x1t[c][:], psB1, ALU.mult))
            P.ins("vector", lambda v, c=c: v.tensor_tensor(
                x1t[c][:], x1t[c][:], psB2, ALU.add))
            tx = P.fence("vector")
            P.wait("sync", tx)
            st_toks.append(P.dma("sync", io["x2T"][c * 128:(c + 1) * 128,
                                                   t0:t0 + W], x1t[c][:]))
        P.barrier()
    if io.get("dbg_w1q") is not None:
        P.dma("sync", io["dbg_w1q"], w1q[0][:])
        P.dma("sync", io["dbg_w2q"], w2q[0][:])
        P.dma("sync", io["dbg_w1qb"], w1q[7][:])
        if DEBUG_P2W:
            P.dma("sync", io["dbg_thr"], thr4[:])
            for f in range(32):
                P.dma("sync", io["dbg_hall"][f * 128:(f + 1) * 128, :],
                      hbf[f])
            for c in range(8):
                P.dma("sync", io["dbg_x1b"][c * 128:(c + 1) * 128, :],
                      x1b[c][:])
            P.dma("gpsimd", io["dbg_hall"][4000:4002, 0:512], kit.rb[:])
        P.barrier()


def prep_p2_inputs(inputs, core, x1T=None):
    d = inputs
    m = dict(
        w1T=np.ascontiguousarray(d["L0_ffn_W1"].T),
        w2T=np.ascontiguousarray(d["L0_ffn_W2"].T),
        b1c=_pack_cols(d["L0_ffn_b1"]),
        b2c=_pack_cols(d["L0_ffn_b2"]),
    )
    if x1T is not None:
        m["x1T"] = x1T
    return m





DEBUG_P3 = False


def build_p3_program():
    nc = bass.Bass("TRN2", target_bir_lowering=False, debug=False)
    io = {}
    if DEBUG_P3:
        io["dbg_xab"] = nc.dram_tensor("dbg_xab", [2048, 512], BF16,
                                       kind="ExternalOutput").ap()
        io["dbg_zb"] = nc.dram_tensor("dbg_zb", [2048, 512], BF16,
                                      kind="ExternalOutput").ap()
        io["dbg_xdb"] = nc.dram_tensor("dbg_xdb", [128, 512], F32,
                                       kind="ExternalOutput").ap()
        io["dbg_dtf"] = nc.dram_tensor("dbg_dtf", [2048, 512], F32,
                                       kind="ExternalOutput").ap()
        io["dbg_ydc"] = nc.dram_tensor("dbg_ydc", [2048, 512], F32,
                                       kind="ExternalOutput").ap()
        io["dbg_yb"] = nc.dram_tensor("dbg_yb", [2048, 512], BF16,
                                      kind="ExternalOutput").ap()
        io["dbg_xst0"] = nc.dram_tensor("dbg_xst0", [128, 516], BF16,
                                        kind="ExternalOutput").ap()
        io["dbg_st"] = nc.dram_tensor("dbg_st", [128, 256], F32,
                                      kind="ExternalOutput").ap()
        io["dbg_halo0"] = nc.dram_tensor("dbg_halo0", [128, 64], BF16,
                                         kind="ExternalOutput").ap()
        io["dbg_halo1"] = nc.dram_tensor("dbg_halo1", [128, 64], BF16,
                                         kind="ExternalOutput").ap()
        io["dbg_bc"] = nc.dram_tensor("dbg_bc", [32, 512], F32,
                                      kind="ExternalOutput").ap()
        io["dbg_conv0"] = nc.dram_tensor("dbg_conv0", [128, 512], F32,
                                         kind="ExternalOutput").ap()
    io["x2T"] = nc.dram_tensor("x2T", [D, T], F32, kind="ExternalInput").ap()
    io["inpT"] = nc.dram_tensor("inpT", [D, 4096], F32,
                                kind="ExternalInput").ap()
    io["convw"] = nc.dram_tensor("convw", [128, 64], F32,
                                 kind="ExternalInput").ap()
    io["convb"] = nc.dram_tensor("convb", [128, 16], F32,
                                 kind="ExternalInput").ap()
    io["xpT"] = nc.dram_tensor("xpT", [2048, 128], F32,
                               kind="ExternalInput").ap()
    io["dtT"] = nc.dram_tensor("dtT", [64, 2048], F32,
                               kind="ExternalInput").ap()
    io["dtb"] = nc.dram_tensor("dtb", [128, 16], F32,
                               kind="ExternalInput").ap()
    io["alog"] = nc.dram_tensor("alog", [128, 256], F32,
                                kind="ExternalInput").ap()
    io["dcol"] = nc.dram_tensor("dcol", [128, 16], F32,
                                kind="ExternalInput").ap()
    io["outT"] = nc.dram_tensor("outT", [2048, D], F32,
                                kind="ExternalInput").ap()
    io["x3T"] = nc.dram_tensor("x3T", [D, T], F32, kind="ExternalOutput").ap()
    io["c_eye"] = nc.dram_tensor("c_eye", [128, 128], BF16,
                                 kind="ExternalInput").ap()
    with contextlib.ExitStack() as ctx:
        P = Prog(nc, ctx)
        emit_p3(nc, P, io)
        P.finalize([P.mark_dma("sync")])
    return nc


def emit_p3(nc, P, io, tag="p3"):
    W = 512
    NC = T // W

    bc_dram = nc.dram_tensor(f"{tag}_bc", [32, W], BF16, kind="Internal").ap()

    inph = [sbuf(nc, f"{tag}_inph{k}", [128, 2048], BF16) for k in range(8)]
    outw = [sbuf(nc, f"{tag}_outw{k}", [128, D], BF16) for k in range(16)]
    xpw = sbuf(nc, f"{tag}_xpw", [128, 16 * 128], BF16)
    dtw = sbuf(nc, f"{tag}_dtw", [64, 2048], BF16)
    convw = sbuf(nc, f"{tag}_convw", [128, 64])
    convb = sbuf(nc, f"{tag}_convb", [128, 16])
    dtb = sbuf(nc, f"{tag}_dtb", [128, 16])
    dtbn = sbuf(nc, f"{tag}_dtbn", [128, 16])
    acol = sbuf(nc, f"{tag}_acol", [128, 256])
    dcol = sbuf(nc, f"{tag}_dcol", [128, 16])
    onesc = sbuf(nc, f"{tag}_onesc", [128, 1])
    ones1r = sbuf(nc, f"{tag}_ones1r", [2, 128])

    ar3 = sbuf(nc, f"{tag}_ar3", [128, 8192])
    x2t = [ar3[:, c * W:(c + 1) * W] for c in range(8)]
    ydc = [ar3[:, d * W:(d + 1) * W] for d in range(16)]
    yb = [inph[d // 4][:, (d % 4) * W:(d % 4 + 1) * W] for d in range(16)]
    ar5 = sbuf(nc, f"{tag}_ar5", [128, 2048])
    x2b = [ar5.bitcast(BF16)[:, c * W:(c + 1) * W] for c in range(8)]
    sp1 = sbuf(nc, f"{tag}_sp1", [128, W])
    sp2 = sbuf(nc, f"{tag}_sp2", [128, W])
    spp = [sp1, sp2]
    xdb = ar5[:, 1024:1536]
    xdbb = ar5.bitcast(BF16)[0:64, 3072:3584]
    xst = [sbuf(nc, f"{tag}_xst{i}", [128, W + 4], BF16) for i in range(2)]
    halo = sbuf(nc, f"{tag}_halo", [128, 64], BF16)

    zb = [sbuf(nc, f"{tag}_zb{d}", [128, W], BF16) for d in range(16)]
    xab = [sbuf(nc, f"{tag}_xab{d}", [128, W], BF16) for d in range(16)]
    dtf = [sbuf(nc, f"{tag}_dt{d}", [128, W]) for d in range(16)]
    ub = [sbuf(nc, f"{tag}_ub{d}", [128, W], BF16) for d in range(16)]
    state = [sbuf(nc, f"{tag}_st{d}", [128, 16]) for d in range(16)]
    def _f32t(k):
        v = inph[k].bitcast(F32)
        return [v[:, 0:W], v[:, W:2 * W]]
    def _bf16t(k):
        return [inph[k][:, 0:W], inph[k][:, W:2 * W]]
    bcb = _bf16t(4)
    ccb = _bf16t(5)
    dat = _f32t(6) + _f32t(2)
    gin = _bf16t(7)
    gout = _bf16t(0)
    ymul = _bf16t(1)
    yacc = [ar3[:, 4096 + i * W:4096 + (i + 1) * W] for i in range(8)]
    eyeb3 = sbuf(nc, f"{tag}_eye", [128, 128], BF16)
    kit = LnKit(nc, f"{tag}_ln", W)
    P.make_fences(nc)

    pb = [psum(nc, f"{tag}_pb{i}", [128, 512]) for i in range(8)]
    psS = pb[0][0:2, :]
    psS2 = pb[1][0:2, :]
    psB1 = pb[2]
    psB2 = pb[3]
    ps_a = [pb[4], pb[5]]
    ps_b2 = [pb[6], pb[7]]

    P.ins("vector", lambda v: v.memset(kit.epsc, 1e-5))
    P.ins("vector", lambda v: v.memset(kit.onesr, 1.0))
    P.ins("vector", lambda v: v.memset(onesc, 1.0))
    P.ins("vector", lambda v: v.memset(ones1r[:], 1.0))
    P.ins("vector", lambda v: v.memset(kit.ra, 0.0))
    P.ins("vector", lambda v: v.memset(kit.rb, 0.0))
    P.ins("vector", lambda v: v.memset(halo[:], 0.0))
    for d in range(16):
        P.ins("vector", lambda v, d=d: v.memset(state[d][:], 0.0))
    for nm, dst in [("convw", convw), ("convb", convb), ("dtb", dtb),
                    ("dcol", dcol)]:
        P.dma("sync", dst[:], io[nm])
    P.dma("sync", eyeb3[:], io["c_eye"])
    ta = P.dma("sync", acol[:], io["alog"])
    tall = P.mark_dma("sync")
    P.wait("scalar", tall)
    tae = P.ins("scalar", lambda s: s.activation(acol[:], acol[:], AF.Exp))
    P.wait("vector", tae)
    P.ins("vector", lambda v: v.tensor_scalar_mul(dtbn[:], dtb[:], -1.0))
    P.dma("gpsimd", dtw[:], io["dtT"])
    for k in range(16):
        P.dma("gpsimd", xpw[:, k * 128:(k + 1) * 128],
              io["xpT"][k * 128:(k + 1) * 128, :])
        P.dma("gpsimd", outw[k][:], io["outT"][k * 128:(k + 1) * 128, :])
    P.barrier()

    for cc in range(NC):
        t0 = cc * W
        for c in range(8):
            P.dma("sync", x2t[c], io["x2T"][c * 128:(c + 1) * 128,
                                            t0:t0 + W])
        for k in range(8):
            P.dma("gpsimd", inph[k][:],
                  io["inpT"][k * 128:(k + 1) * 128, 0:2048])
        ld_all = P.mark_dma("sync")
        ld_g = P.mark_dma("gpsimd")
        P.wait("scalar", ld_all)
        for c in range(8):
            P.ins("scalar", lambda s, c=c: s.copy(x2b[c], x2t[c]))
        tf = P.fence("scalar")
        P.wait("tensor", tf, ld_g)

        evp = {}
        for oc in range(16):
            pp = ps_a[oc % 2]
            xp = xst[oc % 2]
            if oc >= 2:
                P.wait("tensor", evp[oc - 2])
            for k in range(8):
                P.ins("tensor", lambda t, oc=oc, k=k, pp=pp: t.matmul(
                    pp, inph[k][:, oc * 128:(oc + 1) * 128], x2b[k][:],
                    start=(k == 0), stop=(k == 7)))
            tm = P.mark("tensor")
            P.wait("vector", tm)
            P.ins("vector", lambda v, oc=oc, xp=xp: v.tensor_copy(
                xp[:, 0:4], halo[:, oc * 4:(oc + 1) * 4]))
            tev = P.ins("vector", lambda v, oc=oc, xp=xp, pp=pp:
                        v.tensor_copy(xp[:, 4:W + 4], pp))
            P.ins("vector", lambda v, oc=oc, pp=pp: v.tensor_copy(
                halo[:, oc * 4:(oc + 1) * 4], pp[:, W - 4:W]))
            evp[oc] = P.fence("vector")
            P.ins("vector", lambda v, oc=oc, xp=xp: v.tensor_scalar_mul(
                sp1, xp[:, 1:W + 1], convw[:, oc:oc + 1]))
            P.ins("vector", lambda v, oc=oc, xp=xp: v.scalar_tensor_tensor(
                sp1, xp[:, 2:W + 2], convw[:, 16 + oc:17 + oc], sp1,
                ALU.mult, ALU.add))
            P.ins("vector", lambda v, oc=oc, xp=xp: v.scalar_tensor_tensor(
                sp1, xp[:, 3:W + 3], convw[:, 32 + oc:33 + oc], sp1,
                ALU.mult, ALU.add))
            P.ins("vector", lambda v, oc=oc, xp=xp: v.scalar_tensor_tensor(
                sp1, xp[:, 4:W + 4], convw[:, 48 + oc:49 + oc], sp1,
                ALU.mult, ALU.add))
            tcv = P.fence("vector")
            if DEBUG_P3 and cc == 1 and oc == 0:
                P.wait("sync", tcv)
                P.dma("sync", io["dbg_xst0"], xp[:])
                P.dma("sync", io["dbg_conv0"], sp1)
                tdmp = P.mark_dma("sync")
                P.wait("scalar", tdmp)
            P.wait("scalar", tcv)
            tsl = P.ins("scalar", lambda s, oc=oc: s.activation(
                xab[oc][:], sp1, AF.Silu, bias=convb[:, oc:oc + 1],
                scale=1.0))
            tslf = P.fence("scalar")
            P.wait("vector", tslf)
        P.wait("gpsimd", P.mark("tensor"))
        for k in range(8):
            P.dma("gpsimd", inph[k][:],
                  io["inpT"][k * 128:(k + 1) * 128, 2048:4096])
        ld_g = P.mark_dma("gpsimd")
        P.wait("tensor", ld_g, evp[14], evp[15])
        for oc in range(16):
            pp = ps_a[oc % 2]
            if oc >= 2:
                P.wait("tensor", evp[oc - 2])
            for k in range(8):
                P.ins("tensor", lambda t, oc=oc, k=k, pp=pp: t.matmul(
                    pp, inph[k][:, oc * 128:(oc + 1) * 128], x2b[k][:],
                    start=(k == 0), stop=(k == 7)))
            tm = P.mark("tensor")
            P.wait("scalar", tm)
            P.ins("scalar", lambda s, oc=oc, pp=pp: s.activation(
                zb[oc][:], pp, AF.Silu))
            evp[oc] = P.fence("scalar")
        P.wait("tensor", evp[14], evp[15])

        if DEBUG_P3 and cc in (0, 1):
            tdz = P.fence("scalar")
            P.wait("sync", tdz, P.fence("vector"))
            P.dma("sync", io["dbg_halo0" if cc == 0 else "dbg_halo1"],
                  halo[:])
        if DEBUG_P3 and cc == 0:
            tdz = P.fence("scalar")
            P.wait("sync", tdz, P.fence("vector"))
            for dd in range(16):
                P.dma("sync", io["dbg_xab"][dd * 128:(dd + 1) * 128, :],
                      xab[dd][:])
                P.dma("sync", io["dbg_zb"][dd * 128:(dd + 1) * 128, :],
                      zb[dd][:])
        tfx = P.fence("scalar")
        P.wait("tensor", tfx)
        for k in range(16):
            P.ins("tensor", lambda t, k=k: t.matmul(
                ps_b2[0], xpw[:, k * 128:(k + 1) * 128], xab[k][:],
                start=(k == 0), stop=(k == 15)))
        tm = P.mark("tensor")
        P.wait("vector", tm)
        tx = P.ins("vector", lambda v: v.tensor_copy(xdb, ps_b2[0]))
        txf = P.fence("vector")
        P.wait("scalar", txf)
        txb = P.ins("scalar", lambda s: s.copy(xdbb, xdb[0:64, :]))
        P.wait("gpsimd", txf)
        P.dma("gpsimd", bc_dram, xdb[64:96, :])
        tbc = P.mark_dma("gpsimd")
        tfs = P.fence("scalar")
        P.wait("tensor", tfs)

        sgf = {0: None, 1: None}
        ubf = {0: None, 1: None}
        for oc in range(16):
            par = oc % 2
            pp = ps_b2[par]
            if sgf[par] is not None:
                P.wait("tensor", sgf[par])
            P.ins("tensor", lambda t, oc=oc, pp=pp: t.matmul(
                pp, dtw[:, oc * 128:(oc + 1) * 128], xdbb,
                start=True, stop=True))
            tm = P.mark("tensor")
            P.wait("scalar", tm)
            if ubf[par] is not None:
                P.wait("scalar", ubf[par])
            P.ins("scalar", lambda s, oc=oc, pp=pp, par=par: s.activation(
                spp[par], pp, AF.Sigmoid, bias=dtbn[:, oc:oc + 1],
                scale=-1.0))
            sgf[par] = P.fence("scalar")
            P.wait("scalar", sgf[par])
            P.ins("scalar", lambda s, oc=oc, par=par: s.activation(
                dtf[oc][:], spp[par], AF.Ln))
            tsf = P.fence("scalar")
            P.wait("vector", tsf)
            P.ins("vector", lambda v, oc=oc: v.scalar_tensor_tensor(
                ub[oc][:], dtf[oc][:], -1.0, xab[oc][:], ALU.mult,
                ALU.mult))
            ubf[par] = P.fence("vector")

        if DEBUG_P3 and cc == 0:
            P.wait("sync", P.fence("vector"), P.fence("scalar"))
            P.dma("sync", io["dbg_xdb"], xdb)
            for dd in range(16):
                P.dma("sync", io["dbg_dtf"][dd * 128:(dd + 1) * 128, :],
                      dtf[dd][:])
        tf0 = P.fence("vector")
        stepv = {}
        stepg = {}
        bankf = {}
        scanq = {}
        for dh in range(2):
            scanf = {0: None, 1: None}
            ymf = {0: None, 1: None}
            pemk = {0: None, 1: None}
            for n in range(16):
                nb = n % 2
                P.wait("sync", tbc)
                if (dh, n - 2) in stepv:
                    P.wait("sync", stepv[(dh, n - 2)], stepg[(dh, n - 2)])
                elif dh == 1 and n < 2:
                    P.wait("sync", stepv[(0, 14 + n)], stepg[(0, 14 + n)])
                P.dma("sync", bcb[nb][:],
                      bc_dram[n:n + 1, :].partition_broadcast(128))
                P.dma("sync", ccb[nb][:],
                      bc_dram[16 + n:17 + n, :].partition_broadcast(128))
                tB = P.mark_dma("sync")
                P.wait("vector", tB)
                for di in range(8):
                    d = dh * 8 + di
                    pp = di % 2
                    qq = di % 4
                    if scanq.get(qq) is not None:
                        P.wait("scalar", scanq[qq])
                    else:
                        P.wait("scalar", P.mark("vector"))
                    P.ins("scalar", lambda sc, d=d, n=n, qq=qq:
                          sc.activation(dat[qq][:], dtf[d][:], AF.Exp,
                                        scale=acol[:, d * 16 + n:
                                                   d * 16 + n + 1]))
                    tdaf = P.fence("scalar")
                    if ymf[pp] is not None:
                        P.wait("vector", ymf[pp])
                    P.ins("vector", lambda v, d=d, nb=nb, pp=pp:
                          v.tensor_tensor(gin[pp][:], ub[d][:], bcb[nb][:],
                                          ALU.mult))
                    P.wait("vector", tdaf)
                    tsc = P.ins("vector", lambda v, d=d, n=n, pp=pp, qq=qq:
                                v.tensor_tensor_scan(
                                    gout[pp][:], dat[qq][:], gin[pp][:],
                                    state[d][:, n:n + 1], ALU.mult, ALU.add))
                    P.wait("vector", tsc)
                    P.ins("vector", lambda v, d=d, n=n, pp=pp: v.tensor_copy(
                        state[d][:, n:n + 1], gout[pp][:, W - 1:W]))
                    if pemk[pp] is not None:
                        P.wait("vector", pemk[pp])
                    P.ins("vector", lambda v, nb=nb, pp=pp: v.tensor_tensor(
                        ymul[pp][:], gout[pp][:], ccb[nb][:], ALU.mult))
                    scanf[pp] = P.fence("vector")
                    scanq[qq] = scanf[pp]
                    ymf[pp] = scanf[pp]
                    P.wait("tensor", ymf[pp])
                    if n == 0 and (dh, di) in bankf:
                        P.wait("tensor", bankf[(dh, di)])
                    P.ins("tensor", lambda t, di=di, n=n, pp=pp: t.matmul(
                        pb[di], eyeb3[:], ymul[pp][:],
                        start=(n == 0), stop=(n == 15)))
                    pemk[pp] = P.mark("tensor")
                stepv[(dh, n)] = scanf[pp]
                stepg[(dh, n)] = ymf[pp]
            tmall = P.mark("tensor")
            P.wait("vector", tmall)
            if dh == 0:
                for di in range(8):
                    P.ins("vector", lambda v, di=di: v.tensor_copy(
                        yacc[di], pb[di]))
                    bankf[(1, di)] = P.fence("vector")

        tmall = P.mark("tensor")
        P.wait("vector", tmall)
        for d in range(16):
            ysrc = yacc[d] if d < 8 else pb[d - 8]
            P.ins("vector", lambda v, d=d: v.tensor_scalar_mul(
                sp2, xab[d][:], dcol[:, d:d + 1]))
            P.ins("vector", lambda v, d=d, ysrc=ysrc: v.tensor_tensor(
                sp2, ysrc, sp2, ALU.add))
            P.ins("vector", lambda v, d=d: v.tensor_tensor(
                yb[d], sp2, zb[d][:], ALU.mult))
        tf = P.fence("vector")

        P.wait("sync", tf)
        for c in range(8):
            P.dma("sync", x2t[c], io["x2T"][c * 128:(c + 1) * 128,
                                            t0:t0 + W])
        ld_all = P.mark_dma("sync")
        P.wait("vector", ld_all)
        P.wait("tensor", tf)

        evo = {}
        for oc in range(8):
            pp = ps_a[oc % 2]
            if oc >= 2:
                P.wait("tensor", evo[oc - 2])
            for k in range(16):
                P.ins("tensor", lambda t, oc=oc, k=k, pp=pp: t.matmul(
                    pp, outw[k][:, oc * 128:(oc + 1) * 128], yb[k],
                    start=(k == 0), stop=(k == 15)))
            tm = P.mark("tensor")
            P.wait("vector", tm)
            P.ins("vector", lambda v, oc=oc, pp=pp: v.tensor_tensor(
                x2t[oc], pp, x2t[oc], ALU.add))
            evo[oc] = P.fence("vector")
        tb = emit_ln(P, kit, [x for x in x2t], psS, psS2, psB1, psB2,
                     ones1r[:], None)
        P.wait("vector", tb)
        for c in range(8):
            P.ins("vector", lambda v, c=c: v.tensor_tensor(
                x2t[c], x2t[c], psB1, ALU.mult))
            P.ins("vector", lambda v, c=c: v.tensor_tensor(
                x2t[c], x2t[c], psB2, ALU.add))
            tx = P.fence("vector")
            P.wait("sync", tx)
            P.dma("sync", io["x3T"][c * 128:(c + 1) * 128, t0:t0 + W],
                  x2t[c])
        P.barrier()


DEBUG_P4 = False


def build_p4_program():
    nc = bass.Bass("TRN2", target_bir_lowering=False, debug=False)
    io = {}
    io["x3T"] = nc.dram_tensor("x3T", [D, T], F32, kind="ExternalInput").ap()
    io["wrPT"] = nc.dram_tensor("wrPT", [D, 128], F32,
                                kind="ExternalInput").ap()
    io["wmhT"] = nc.dram_tensor("wmhT", [D, D], F32,
                                kind="ExternalInput").ap()
    io["kT"] = nc.dram_tensor("kT", [1024, 4096], F32,
                              kind="ExternalInput").ap()
    io["vT"] = nc.dram_tensor("vT", [32768, 128], F32,
                              kind="ExternalInput").ap()
    io["wmgT"] = nc.dram_tensor("wmgT", [D, D], F32,
                                kind="ExternalInput").ap()
    io["eyeB"] = nc.dram_tensor("eyeB", [128, 128], BF16,
                                kind="ExternalInput").ap()
    io["x4T"] = nc.dram_tensor("x4T", [D, T], F32, kind="ExternalOutput").ap()
    if DEBUG_P4:
        io["dbg_gw"] = nc.dram_tensor("dbg_gw", [16, T], BF16,
                                      kind="ExternalOutput").ap()
        io["dbg_q"] = nc.dram_tensor("dbg_q", [1024, T], BF16,
                                     kind="ExternalOutput").ap()
        io["dbg_yc"] = nc.dram_tensor("dbg_yc", [1024, T], BF16,
                                      kind="ExternalOutput").ap()
        io["dbg_m"] = nc.dram_tensor("dbg_m", [128, 64], F32,
                                     kind="ExternalOutput").ap()
        io["dbg_msk"] = nc.dram_tensor("dbg_msk", [128, 512], F32,
                                       kind="ExternalOutput").ap()
        io["dbg_sig"] = nc.dram_tensor("dbg_sig", [128, 512], F32,
                                       kind="ExternalOutput").ap()
        io["dbg_gwt"] = nc.dram_tensor("dbg_gwt", [128, 512], BF16,
                                       kind="ExternalOutput").ap()
    with contextlib.ExitStack() as ctx:
        P = Prog(nc, ctx)
        st = emit_p4(nc, P, io)
        P.finalize(st)
    return nc


def emit_p4(nc, P, io, tag="p4"):
    W = 512
    H = 8
    gw_dram = nc.dram_tensor(f"{tag}_gw", [16, T], BF16, kind="Internal").ap()

    x3b = [sbuf(nc, f"{tag}_x3b{k}", [128, T], BF16) for k in range(8)]
    qall = x3b
    ycat = [sbuf(nc, f"{tag}_yc{h}", [128, T], BF16) for h in range(8)]
    kb = [sbuf(nc, f"{tag}_kb{i}", [128, 4096], BF16) for i in range(2)]
    vb = [sbuf(nc, f"{tag}_vb{i}", [128, 4096], BF16) for i in range(2)]
    wmg = ([kb[0][:, k * 1024:(k + 1) * 1024] for k in range(4)]
           + [vb[0][:, k * 1024:(k + 1) * 1024] for k in range(4)])
    x3r = ([kb[1].bitcast(F32)[:, c * W:(c + 1) * W] for c in range(4)]
           + [vb[1].bitcast(F32)[:, c * W:(c + 1) * W] for c in range(4)])
    wmh_ar = sbuf(nc, f"{tag}_wmh", [128, 8192], BF16)
    wmh = [wmh_ar[:, k * 1024:(k + 1) * 1024] for k in range(8)]
    gwb1_ar = sbuf(nc, f"{tag}_gwb1", [128, 8192], BF16)
    gwb = [[wmh_ar[:, e * W:(e + 1) * W] for e in range(16)],
           [gwb1_ar[:, e * W:(e + 1) * W] for e in range(16)]]
    aT = [sbuf(nc, f"{tag}_aT{i}", [128, W], BF16) for i in range(4)]
    wrb = sbuf(nc, f"{tag}_wrb", [128, 1024], BF16)
    eyeb = sbuf(nc, f"{tag}_eye", [128, 128], BF16)
    gwt = sbuf(nc, f"{tag}_gwt", [128, 128], BF16)
    stg16 = sbuf(nc, f"{tag}_st16", [16, 128], BF16)
    m1 = sbuf(nc, f"{tag}_m1", [128, 1])
    m2 = sbuf(nc, f"{tag}_m2", [128, 1])
    msk = sbuf(nc, f"{tag}_msk", [128, 16])
    l2s = sbuf(nc, f"{tag}_l2", [128, 16])
    sig = sbuf(nc, f"{tag}_sig", [128, 16])
    ones1r = sbuf(nc, f"{tag}_ones1r", [2, 128])
    kit = LnKit(nc, f"{tag}_ln", W)
    P.make_fences(nc)

    pb = [psum(nc, f"{tag}_pb{i}", [128, 512]) for i in range(8)]
    psS = pb[0][0:2, :]
    psS2 = pb[1][0:2, :]
    psB1 = pb[2]
    psB2 = pb[3]

    P.ins("vector", lambda v: v.memset(kit.epsc, 1e-5))
    P.ins("vector", lambda v: v.memset(kit.onesr, 1.0))
    P.ins("vector", lambda v: v.memset(ones1r[:], 1.0))
    P.ins("vector", lambda v: v.memset(kit.ra, 0.0))
    P.ins("vector", lambda v: v.memset(kit.rb, 0.0))
    P.ins("vector", lambda v: v.memset(gwt[:], 0.0))
    P.dma("sync", eyeb[:], io["eyeB"])
    teye = P.mark_dma("sync")
    P.dma("gpsimd", wrb[:],
          io["wrPT"].rearrange("(c p) e -> p c e", p=128))
    for k in range(8):
        P.dma("gpsimd", wmh[k], io["wmhT"][k * 128:(k + 1) * 128, :])
        P.dma("gpsimd", x3b[k][:], io["x3T"][k * 128:(k + 1) * 128, :])
    P.dma("gpsimd", kb[0][:], io["kT"][0:128, :])
    P.dma("gpsimd", vb[0][:],
          io["vT"][0:4096, :].rearrange("(c p) d -> p c d", p=128))
    tinit = P.mark_dma("gpsimd")
    kvmark = {0: tinit}

    P.wait("tensor", tinit, teye)
    rfin = {}
    prev_tevf = None
    prev_spill = None
    prev_tmt = None
    for tc32 in range(32):
        pr = pb[tc32 % 2]
        if tc32 >= 2:
            P.wait("tensor", rfin[tc32 - 2])
        for k in range(8):
            P.ins("tensor", lambda t, k=k, tc32=tc32, pr=pr: t.matmul(
                pr[:, 0:128], x3b[k][:, tc32 * 128:(tc32 + 1) * 128],
                wrb[:, k * 128:(k + 1) * 128],
                start=(k == 0), stop=(k == 7)))
        tm = P.mark("tensor")
        P.wait("vector", tm)
        if prev_tmt is not None:
            P.wait("vector", prev_tmt)
        lg = pr[:, 0:16]
        def FW():
            P.wait("vector", P.fence("vector"))
        P.ins("vector", lambda v, lg=lg: v.reduce_max(m1[:], lg, AX.X))
        FW()
        P.ins("vector", lambda v, lg=lg: v.tensor_single_scalar(
            msk[:], lg, m1[:, 0:1], ALU.is_ge))
        FW()
        P.ins("vector", lambda v: v.tensor_scalar_mul(msk[:], msk[:],
                                                      -1000.0))
        FW()
        P.ins("vector", lambda v, lg=lg: v.tensor_tensor(
            l2s[:], msk[:], lg, ALU.add))
        FW()
        P.ins("vector", lambda v: v.reduce_max(m2[:], l2s[:], AX.X))
        FW()
        P.ins("vector", lambda v, lg=lg: v.tensor_single_scalar(
            msk[:], lg, m2[:, 0:1], ALU.is_ge))
        FW()
        tv = P.fence("vector")
        P.wait("scalar", tv)
        P.ins("scalar", lambda s, lg=lg: s.activation(sig[:], lg, AF.Sigmoid))
        tsf = P.fence("scalar")
        P.wait("vector", tsf)
        P.ins("vector", lambda v: v.tensor_tensor(gwt[:, 0:16], sig[:],
                                                  msk[:], ALU.mult))
        rfin[tc32] = P.fence("vector")
        if DEBUG_P4 and tc32 < 32:
            P.wait("sync", rfin[tc32])
            P.dma("sync", io["dbg_m"][:, tc32:tc32 + 1], m1[:],
                  allow_slow_non_contiguous=True)
            P.dma("sync", io["dbg_m"][:, 32 + tc32:33 + tc32], m2[:],
                  allow_slow_non_contiguous=True)
            P.dma("sync", io["dbg_msk"][:, tc32 * 16:(tc32 + 1) * 16], msk[:])
            P.dma("sync", io["dbg_sig"][:, tc32 * 16:(tc32 + 1) * 16], sig[:])
            P.dma("sync", io["dbg_gwt"][:, tc32 * 16:(tc32 + 1) * 16],
                  gwt[:, 0:16])
            tdmp = P.mark_dma("sync")
            P.wait("vector", tdmp)
        P.wait("tensor", rfin[tc32])
        if prev_tevf is not None:
            P.wait("tensor", prev_tevf)
        P.ins("tensor", lambda t: t.matmul(pb[2][:, 0:128], gwt[:], eyeb[:],
                                           start=True, stop=True))
        prev_tmt = P.mark("tensor")
        P.wait("vector", prev_tmt)
        if prev_spill is not None:
            P.wait("vector", prev_spill)
        P.ins("vector", lambda v: v.tensor_copy(stg16[:],
                                                pb[2][0:16, 0:128]))
        prev_tevf = P.fence("vector")
        P.wait("sync", prev_tevf)
        P.dma("sync", gw_dram[:, tc32 * 128:(tc32 + 1) * 128], stg16[:])
        prev_spill = P.mark_dma("sync")
    P.barrier()

    for tcq in range(8):
        for h in range(8):
            for k in range(8):
                P.ins("tensor", lambda t, h=h, k=k, tcq=tcq: t.matmul(
                    pb[h], wmh[k][:, h * 128:(h + 1) * 128],
                    x3b[k][:, tcq * W:(tcq + 1) * W],
                    start=(k == 0), stop=(k == 7)))
        tmq = P.mark("tensor")
        P.wait("vector", tmq)
        for h in range(8):
            P.ins("vector", lambda v, h=h, tcq=tcq: v.tensor_copy(
                qall[h][:, tcq * W:(tcq + 1) * W], pb[h]))
        tqf = P.fence("vector")
        P.wait("tensor", tqf)
    P.barrier()
    if DEBUG_P4:
        for h in range(8):
            P.dma("sync", io["dbg_q"][h * 128:(h + 1) * 128, :], qall[h][:])
        P.dma("sync", io["dbg_gw"], gw_dram)
        P.wait("sync", P.mark_dma("sync"))

    for e in range(16):
        P.dma("sync", gwb[0][e],
              gw_dram[e:e + 1, 0:W].partition_broadcast(128))
    bmark = {0: (P.mark_dma("sync"), P.mark_dma("gpsimd"))}
    bguard = {}
    wmg_mark = None
    for h in range(H):
        P.wait("gpsimd", P.mark("tensor"))
        if h + 1 < H:
            P.dma("gpsimd", kb[(h + 1) % 2][:],
                  io["kT"][(h + 1) * 128:(h + 2) * 128, :])
            P.dma("gpsimd", vb[(h + 1) % 2][:],
                  io["vT"][(h + 1) * 4096:(h + 2) * 4096, :]
                  .rearrange("(c p) d -> p c d", p=128))
            kvmark[h + 1] = P.mark_dma("gpsimd")
        else:
            for k in range(8):
                P.dma("gpsimd", wmg[k], io["wmgT"][k * 128:(k + 1) * 128, :])
            wmg_mark = P.mark_dma("gpsimd")
        P.wait("tensor", kvmark[h])
        gelf_by_bank = {}
        yevf = {}
        multf = {}
        for tc in range(8):
            it = h * 8 + tc
            if it + 1 < 64:
                if it >= 1:
                    P.wait("sync", bguard[it - 1])
                    P.wait("gpsimd", bguard[it - 1])
                tcn = (tc + 1) % 8
                for e in range(16):
                    q = "sync" if e % 2 == 0 else "gpsimd"
                    P.dma(q, gwb[(it + 1) % 2][e],
                          gw_dram[e:e + 1, tcn * W:(tcn + 1) * W]
                          .partition_broadcast(128))
                bmark[it + 1] = (P.mark_dma("sync"), P.mark_dma("gpsimd"))
            sgw = gwb[it % 2]
            P.wait("vector", *bmark[it])
            psy = pb[6 + (tc % 2)]

            def issueV(j, tc=tc, h=h, psy=psy):
                P.wait("tensor", multf[j])
                if j == 0 and tc >= 2:
                    P.wait("tensor", yevf[tc - 2])
                P.ins("tensor", lambda t, h=h, j=j, psy=psy:
                      t.matmul(psy, vb[h % 2][:, j * 128:(j + 1) * 128],
                               aT[j % 4][:], start=(j == 0),
                               stop=(j == 31)))
            for ec in range(32):
                bank = pb[ec % 4]
                if (ec % 4) in gelf_by_bank:
                    P.wait("tensor", gelf_by_bank[ec % 4])
                P.ins("tensor", lambda t, h=h, ec=ec, tc=tc, bank=bank:
                      t.matmul(bank, kb[h % 2][:, ec * 128:(ec + 1) * 128],
                               qall[h][:, tc * W:(tc + 1) * W],
                               start=True, stop=True))
                tK = P.mark("tensor")
                if ec >= 3:
                    issueV(ec - 3)
                P.wait("scalar", tK)
                P.ins("scalar", lambda s, ec=ec, bank=bank: s.activation(
                    aT[ec % 4][:], bank, AF.Gelu))
                gelf_by_bank[ec % 4] = P.fence("scalar")
                P.wait("vector", gelf_by_bank[ec % 4])
                P.ins("vector", lambda v, ec=ec, sgw=sgw: v.tensor_tensor(
                    aT[ec % 4][:], aT[ec % 4][:], sgw[ec // 2], ALU.mult))
                multf[ec] = P.fence("vector")
            for j in (29, 30, 31):
                issueV(j)
            tmv = P.mark("tensor")
            P.wait("vector", tmv)
            P.ins("vector", lambda v, h=h, tc=tc, psy=psy: v.tensor_copy(
                ycat[h][:, tc * W:(tc + 1) * W], psy))
            yevf[tc] = P.fence("vector")
            bguard[it] = yevf[tc]
    P.barrier()
    if DEBUG_P4:
        for h in range(8):
            P.dma("sync", io["dbg_yc"][h * 128:(h + 1) * 128, :], ycat[h][:])
        P.wait("sync", P.mark_dma("sync"))

    stores = []
    P.wait("tensor", wmg_mark)
    for tc in range(8):
        for c in range(8):
            P.dma("sync", x3r[c], io["x3T"][c * 128:(c + 1) * 128,
                                            tc * W:(tc + 1) * W])
        ldr = P.mark_dma("sync")
        P.wait("vector", ldr)
        oev = {}
        for oc in range(8):
            po = pb[4 + (oc % 2)]
            if oc >= 2:
                P.wait("tensor", oev[oc - 2])
            for k in range(8):
                P.ins("tensor", lambda t, k=k, oc=oc, tc=tc, po=po: t.matmul(
                    po, wmg[k][:, oc * 128:(oc + 1) * 128],
                    ycat[k][:, tc * W:(tc + 1) * W],
                    start=(k == 0), stop=(k == 7)))
            tm = P.mark("tensor")
            P.wait("vector", tm)
            P.ins("vector", lambda v, oc=oc, po=po: v.tensor_tensor(
                x3r[oc], po, x3r[oc], ALU.add))
            oev[oc] = P.fence("vector")
        tb = emit_ln(P, kit, x3r, psS, psS2, psB1, psB2, ones1r[:], None)
        P.wait("vector", tb)
        for c in range(8):
            P.ins("vector", lambda v, c=c: v.tensor_tensor(
                x3r[c], x3r[c], psB1, ALU.mult))
            P.ins("vector", lambda v, c=c: v.tensor_tensor(
                x3r[c], x3r[c], psB2, ALU.add))
            tx = P.fence("vector")
            P.wait("sync", tx)
            stores.append(P.dma("sync",
                                io["x4T"][c * 128:(c + 1) * 128,
                                          tc * W:(tc + 1) * W], x3r[c]))
        P.wait("sync", P.mark_dma("sync"))
    return [P.mark_dma("sync")]


def prep_p4_inputs(inputs, core, x3T=None):
    d = inputs
    wr = d["L1_ffn_w_router"].astype(np.float32)
    wrPT = np.zeros((1024, 128), np.float32)
    wrPT[:, 0:16] = wr.T
    kT = np.ascontiguousarray(
        d["L1_ffn_k_ffwd"].astype(np.float32)
        .transpose(0, 3, 1, 2).reshape(1024, 4096))
    vT = np.ascontiguousarray(
        d["L1_ffn_v_ffwd"].astype(np.float32).reshape(32768, 128))
    m = dict(
        wrPT=wrPT,
        wmhT=np.ascontiguousarray(d["L1_ffn_w_mh"].astype(np.float32).T),
        kT=kT,
        vT=vT,
        wmgT=np.ascontiguousarray(d["L1_ffn_w_mg"].astype(np.float32).T),
        eyeB=np.eye(128, dtype=ml_dtypes.bfloat16),
    )
    if x3T is not None:
        m["x3T"] = x3T
    return m


def prep_p3_inputs(inputs, core, x2T=None):
    d = inputs
    cw = d["L1_attn_conv_w"]
    convw = np.ascontiguousarray(
        cw.reshape(16, 128, 4).transpose(1, 2, 0).reshape(128, 64))
    alog = np.ascontiguousarray(
        d["L1_attn_A_log"].reshape(16, 128, 16).transpose(1, 0, 2)
        .reshape(128, 256))
    xpT = np.zeros((2048, 128), np.float32)
    xpT[:, 0:96] = d["L1_attn_x_proj_w"].T
    m = dict(
        inpT=np.ascontiguousarray(d["L1_attn_in_proj_w"].T),
        convw=convw,
        convb=_pack_cols(d["L1_attn_conv_b"]),
        xpT=np.ascontiguousarray(xpT),
        dtT=np.ascontiguousarray(d["L1_attn_dt_proj_w"].T),
        dtb=_pack_cols(d["L1_attn_dt_proj_b"]),
        alog=alog,
        dcol=_pack_cols(d["L1_attn_D"]),
        outT=np.ascontiguousarray(d["L1_attn_out_proj_w"].T),
        c_eye=np.eye(128, dtype=bf16),
    )
    if x2T is not None:
        m["x2T"] = x2T
    return m


def prep_p1_inputs(inputs, core):
    d = inputs
    xT = np.ascontiguousarray(d["L0_attn_x"][core].T.astype(np.float32))
    wqkvT = np.ascontiguousarray(d["L0_attn_w_qkv"].T)
    woutT = np.ascontiguousarray(d["L0_attn_w_out"].T)
    lngb = np.concatenate([_pack_cols(d["L0_attn_ln_g"]),
                           _pack_cols(d["L0_attn_ln_b"])], axis=1)
    m = dict(xT=xT, wqkvT=wqkvT, woutT=woutT,
             lngb=np.ascontiguousarray(lngb))
    m.update(host_consts())
    return m


DEBUG_FUSED = False


def build_fused_program():
    nc = bass.Bass("TRN2", target_bir_lowering=False, debug=False)
    io = {}
    io["xT"] = nc.dram_tensor("xT", [D, T], F32, kind="ExternalInput").ap()
    io["wqkvT"] = nc.dram_tensor("wqkvT", [D, D], F32,
                                 kind="ExternalInput").ap()
    io["woutT"] = nc.dram_tensor("woutT", [512, D], F32,
                                 kind="ExternalInput").ap()
    io["lngb"] = nc.dram_tensor("lngb", [128, 16], F32,
                                kind="ExternalInput").ap()
    for k, v in host_consts().items():
        dt = BF16 if v.dtype == bf16 else F32
        io[k] = nc.dram_tensor(k, list(v.shape), dt,
                               kind="ExternalInput").ap()
    io["w1T"] = nc.dram_tensor("w1T", [D, 4096], F32,
                               kind="ExternalInput").ap()
    io["w2T"] = nc.dram_tensor("w2T", [4096, D], F32,
                               kind="ExternalInput").ap()
    io["b1c"] = nc.dram_tensor("b1c", [128, 32], F32,
                               kind="ExternalInput").ap()
    io["b2c"] = nc.dram_tensor("b2c", [128, 8], F32,
                               kind="ExternalInput").ap()
    io["inpT"] = nc.dram_tensor("inpT", [D, 4096], F32,
                                kind="ExternalInput").ap()
    io["convw"] = nc.dram_tensor("convw", [128, 64], F32,
                                 kind="ExternalInput").ap()
    io["convb"] = nc.dram_tensor("convb", [128, 16], F32,
                                 kind="ExternalInput").ap()
    io["xpT"] = nc.dram_tensor("xpT", [2048, 128], F32,
                               kind="ExternalInput").ap()
    io["dtT"] = nc.dram_tensor("dtT", [64, 2048], F32,
                               kind="ExternalInput").ap()
    io["dtb"] = nc.dram_tensor("dtb", [128, 16], F32,
                               kind="ExternalInput").ap()
    io["alog"] = nc.dram_tensor("alog", [128, 256], F32,
                                kind="ExternalInput").ap()
    io["dcol"] = nc.dram_tensor("dcol", [128, 16], F32,
                                kind="ExternalInput").ap()
    io["outT"] = nc.dram_tensor("outT", [2048, D], F32,
                                kind="ExternalInput").ap()
    io["wrPT"] = nc.dram_tensor("wrPT", [D, 128], F32,
                                kind="ExternalInput").ap()
    io["wmhT"] = nc.dram_tensor("wmhT", [D, D], F32,
                                kind="ExternalInput").ap()
    io["kT"] = nc.dram_tensor("kT", [1024, 4096], F32,
                              kind="ExternalInput").ap()
    io["vT"] = nc.dram_tensor("vT", [32768, 128], F32,
                              kind="ExternalInput").ap()
    io["wmgT"] = nc.dram_tensor("wmgT", [D, D], F32,
                                kind="ExternalInput").ap()
    io["eyeB"] = nc.dram_tensor("eyeB", [128, 128], BF16,
                                kind="ExternalInput").ap()
    kind_i = "ExternalOutput" if DEBUG_FUSED else "Internal"
    if DEBUG_FUSED:
        io["dbg_thr"] = nc.dram_tensor("dbg_thr", [128, 4], F32,
                                       kind="ExternalOutput").ap()
        io["dbg_gs"] = nc.dram_tensor("dbg_gs", [128, 4], F32,
                                      kind="ExternalOutput").ap()
    io["x1T"] = nc.dram_tensor("x1T", [D, T], F32, kind=kind_i).ap()
    io["x2T"] = nc.dram_tensor("x2T", [D, T], F32, kind=kind_i).ap()
    io["x3T"] = nc.dram_tensor("x3T", [D, T], F32, kind=kind_i).ap()
    io["x4T"] = nc.dram_tensor("x4T", [D, T], F32, kind="ExternalOutput").ap()
    with contextlib.ExitStack() as ctx:
        P = Prog(nc, ctx)
        P.make_fences(nc)
        for emitter in (emit_p1, emit_p2, emit_p3):
            with contextlib.ExitStack() as cph:
                _CUR_CTX[0] = cph
                emitter(nc, P, io)
                _CUR_CTX[0] = None
            P.barrier()
        with contextlib.ExitStack() as cph:
            _CUR_CTX[0] = cph
            st = emit_p4(nc, P, io)
            _CUR_CTX[0] = None
            P.finalize(st)
    return nc


def prep_fused_inputs(inputs, core):
    m = prep_p1_inputs(inputs, core)
    m.update(prep_p2_inputs(inputs, core))
    m.update(prep_p3_inputs(inputs, core))
    m.update(prep_p4_inputs(inputs, core))
    return m


def kernel_fused(**inputs):
    cores = list(range(8))
    nc = build_fused_program()
    maps = [prep_fused_inputs(inputs, c) for c in cores]
    r = run_bass_kernel_spmd(nc, maps, core_ids=cores)
    out = np.stack([np.asarray(r.results[c]["x4T"]).T for c in cores])
    return np.ascontiguousarray(out.astype(np.float32))


def kernel(**inputs):
    return kernel_fused(**inputs)
